# version 18
# baseline (speedup 1.0000x reference)
import numpy as np
from contextlib import ExitStack
import concourse.bass as bass
import concourse.mybir as mybir

F32 = mybir.dt.float32
BF16 = mybir.dt.bfloat16
I32 = mybir.dt.int32
AF = mybir.ActivationFunctionType
ALU = mybir.AluOpType
AX = mybir.AxisListType

TC = 1024
NT = 2 * TC
H, M, DK = 4, 64, 256
D = 1024
EPS = 1e-5
SCALE = 1.0 / 16.0

ENGINES = ("tensor", "vector", "scalar", "gpsimd", "sync")


class Buf:
    __slots__ = ("t", "writers", "readers", "name")

    def __init__(self, t, name):
        self.t = t
        self.name = name
        self.writers = []
        self.readers = []

    def __getitem__(self, key):
        return self.t.ap()[key] if not isinstance(self.t, bass.AP) else self.t[key]

    @property
    def ap(self):
        return self.t.ap() if not isinstance(self.t, bass.AP) else self.t


class Prog:
    def __init__(self, nc):
        self.nc = nc
        self.ops = []
        self.NDMASEM = 8
        self.counts = {f"eng_{e}": 0 for e in ENGINES}
        for q in ("gpsimd", "sync", "scalar"):
            for s in range(self.NDMASEM):
                self.counts[f"dma_{q}_{s}"] = 0
        self.counts["cc"] = 0
        self.dma_rr = {"gpsimd": 0, "sync": 0, "scalar": 0}
        self.floor = {}
        self.scopes = [[]]
        g = nc.sbuf_tensor("constpool", [128, 16], F32)
        self.constpool = Buf(g.__enter__(), "constpool")
        self.scopes[0].append(g)
        self.constvals = {}

    def push_scope(self):
        self.scopes.append([])

    def pop_scope(self):
        for g in reversed(self.scopes.pop()):
            g.__exit__(None, None, None)
        self.barrier()

    def op(self, engine, fn, r=(), w=(), dma=False, cc=False, tag=""):
        deps = dict(self.floor)

        def add(ev):
            s, v = ev
            if deps.get(s, 0) < v:
                deps[s] = v
        for b in r:
            for ev in b.writers:
                add(ev)
        for b in w:
            for ev in b.writers:
                add(ev)
            for ev in b.readers:
                add(ev)
        if cc:
            self.counts["cc"] += 1
            ev = ("cc", self.counts["cc"])
            amt = 1
        elif dma:
            s = self.dma_rr[engine]
            self.dma_rr[engine] = (s + 1) % self.NDMASEM
            k = f"dma_{engine}_{s}"
            if deps.get(k, 0) < self.counts[k]:
                deps[k] = self.counts[k]
            self.counts[k] += 16
            ev = (k, self.counts[k])
            amt = 16
        else:
            k = f"eng_{engine}"
            self.counts[k] += 1
            ev = (k, self.counts[k])
            amt = 1
        self.ops.append(dict(engine=engine, fn=fn, waits=deps, inc=(ev[0], ev[1], amt), tag=tag))
        for b in r:
            b.readers.append(ev)
        for b in w:
            b.writers = [ev]
            b.readers = []
        return ev

    def barrier(self):
        self.floor = {k: v for k, v in self.counts.items() if v > 0 and k != "cc"}

    def sb(self, name, shape, dtype):
        g = self.nc.sbuf_tensor(name, shape, dtype)
        t = g.__enter__()
        self.scopes[-1].append(g)
        return Buf(t, name)

    def ps(self, name, shape, dtype=F32):
        t = self.nc.alloc_psum_tensor(name, shape, dtype)
        return Buf(t, name)

    def dma(self, engine, out_ap, in_ap, r=(), w=(), tag="", slow=False):
        if slow:
            return self.op(engine,
                           lambda e: e.dma_start(out=out_ap, in_=in_ap,
                                                 allow_slow_non_contiguous=True),
                           r=r, w=w, dma=True, tag=tag)
        return self.op(engine, lambda e: e.dma_start(out=out_ap, in_=in_ap),
                       r=r, w=w, dma=True, tag=tag)

    def mm(self, out_ap, lhsT_ap, rhs_ap, start, stop, r=(), w=(), tag=""):
        return self.op("tensor",
                       lambda e: e.matmul(out=out_ap, lhsT=lhsT_ap, rhs=rhs_ap,
                                          start=start, stop=stop),
                       r=r, w=w, tag=tag)

    def tr(self, out_ap, in_ap, ident_ap, r=(), w=(), tag=""):
        return self.op("tensor",
                       lambda e: e.transpose(out=out_ap, in_=in_ap, identity=ident_ap),
                       r=r, w=w, tag=tag)

    def trp(self, out_ap, in_ap, ident_ap, start, stop, r=(), w=(), tag=""):
        return self.op("tensor",
                       lambda e: e.matmul(out=out_ap, lhsT=in_ap, rhs=ident_ap,
                                          start=start, stop=stop, is_transpose=True),
                       r=r, w=w, tag=tag)

    def cp(self, eng, out_ap, in_ap, r=(), w=(), tag=""):
        if eng == "scalar":
            return self.op(eng, lambda e: e.copy(out=out_ap, in_=in_ap), r=r, w=w, tag=tag)
        return self.op(eng, lambda e: e.tensor_copy(out=out_ap, in_=in_ap), r=r, w=w, tag=tag)

    def tt(self, eng, out_ap, a_ap, b_ap, alu, r=(), w=(), tag=""):
        return self.op(eng, lambda e: e.tensor_tensor(out=out_ap, in0=a_ap, in1=b_ap, op=alu),
                       r=r, w=w, tag=tag)

    def tss(self, eng, out_ap, in_ap, scalar, alu, r=(), w=(), tag=""):
        return self.op(eng, lambda e: e.tensor_single_scalar(out=out_ap, in_=in_ap,
                                                             scalar=scalar, op=alu),
                       r=r, w=w, tag=tag)

    def ts2(self, eng, out_ap, in_ap, s1, s2, op0, op1, r=(), w=(), tag=""):
        return self.op(eng, lambda e: e.tensor_scalar(out=out_ap, in0=in_ap, scalar1=s1,
                                                      scalar2=s2, op0=op0, op1=op1),
                       r=r, w=w, tag=tag)

    def stt(self, eng, out_ap, in0_ap, scalar, in1_ap, op0, op1, r=(), w=(), tag=""):
        return self.op(eng, lambda e: e.scalar_tensor_tensor(out=out_ap, in0=in0_ap,
                                                             scalar=scalar, in1=in1_ap,
                                                             op0=op0, op1=op1),
                       r=r, w=w, tag=tag)

    def register_const(self, val):
        if val in self.constvals:
            return
        i = len(self.constvals)
        col = self.constpool[:, i:i + 1]
        self.memset("vector", col, val, w=[self.constpool])
        self.nc.const_aps.register(F32, val, col)
        self.constvals[val] = col

    def act(self, out_ap, in_ap, func, bias=0.0, scale=1.0, accum_ap=None, r=(), w=(), tag=""):
        if isinstance(bias, float) and bias != 0.0:
            self.register_const(bias)
            r = list(r) + [self.constpool]
        return self.op("scalar",
                       lambda e: e.activation(out=out_ap, in_=in_ap, func=func, bias=bias,
                                              scale=scale, accum_out=accum_ap),
                       r=r, w=w, tag=tag)

    def red(self, eng, out_ap, in_ap, alu, negate=False, r=(), w=(), tag=""):
        return self.op(eng, lambda e: e.tensor_reduce(out=out_ap, in_=in_ap, axis=AX.X,
                                                      op=alu, negate=negate),
                       r=r, w=w, tag=tag)

    def scan_cumsum(self, out_ap, data_ap, init, r=(), w=(), tag=""):
        return self.op("vector",
                       lambda e: e.tensor_tensor_scan(out=out_ap, data0=data_ap, data1=data_ap,
                                                      initial=init, op0=ALU.add, op1=ALU.bypass),
                       r=r, w=w, tag=tag)

    def memset(self, eng, ap, val, w=(), tag=""):
        return self.op(eng, lambda e: e.memset(ap, val), w=w, tag=tag)

    def recip(self, out_ap, in_ap, r=(), w=(), tag=""):
        return self.op("vector", lambda e: e.reciprocal(out=out_ap, in_=in_ap), r=r, w=w, tag=tag)

    def collective(self, kind, alu, cc_in_ap, cc_out_ap, r=(), w=(), tag=""):
        return self.op("gpsimd",
                       lambda e: e.collective_compute(kind, alu,
                                                      replica_groups=[list(range(8))],
                                                      ins=[cc_in_ap], outs=[cc_out_ap]),
                       r=r, w=w, cc=True, tag=tag)

    def emit(self):
        nc = self.nc
        with ExitStack() as ctx:
            sems = {}
            for k, v in self.counts.items():
                if v > 0:
                    sems[k] = ctx.enter_context(nc.semaphore(k))
            block = ctx.enter_context(nc.Block())
            by_engine = {e: [o for o in self.ops if o["engine"] == e] for e in ENGINES}
            finals = {k: v for k, v in self.counts.items() if v > 0}

            def section(eng_obj, engine):
                guaranteed = {}
                own = f"eng_{engine}"
                own_count = 0
                for o in by_engine[engine]:
                    for s, v in sorted(o["waits"].items()):
                        if s == own and engine == "tensor":
                            continue
                        if s == own and v <= own_count and engine != "tensor":
                            pass
                        if guaranteed.get(s, 0) < v:
                            eng_obj.wait_ge(sems[s], v)
                            guaranteed[s] = v
                    inst = o["fn"](eng_obj)
                    s, v, amt = o["inc"]
                    inst.then_inc(sems[s], amt)
                    if s == own:
                        own_count = v
                if engine == "sync":
                    for s, v in finals.items():
                        if guaranteed.get(s, 0) < v:
                            eng_obj.wait_ge(sems[s], v)

            for e in ENGINES:
                if by_engine[e] or e == "sync":
                    getattr(block, e)(lambda obj, e=e: section(obj, e))
        return nc




WNAMES_F32 = [
    "L0_attn_Wq", "L0_attn_Wk", "L0_attn_Wv", "L0_attn_Wg", "L0_attn_Wo",
    "L0_attn_sk_w1", "L0_attn_sk_w2", "L0_attn_sk_b2",
    "L0_attn_sv_w1", "L0_attn_sv_w2", "L0_attn_sv_b2", "L0_attn_gnorm_w",
    "L1_attn_q_w", "L1_attn_q_b", "L1_attn_k_w", "L1_attn_k_b",
    "L1_attn_v_w", "L1_attn_v_b", "L1_attn_out_w", "L1_attn_out_b",
    "L1_attn_rel_emb", "L1_attn_grep_w", "L1_attn_grep_b", "L1_attn_grep_a",
    "L1_attn_bias_k", "L1_attn_bias_v",
    "L1_ffn_gate_w", "L1_ffn_w1", "L1_ffn_b1", "L1_ffn_w2", "L1_ffn_b2",
    "L1_ffn_w3", "L1_ffn_b3", "L1_ffn_sw1", "L1_ffn_sw2", "L1_ffn_sw3",
]
WNAMES_I32 = ["L0_ffn_w_gate", "L0_ffn_w_up", "L0_ffn_w_down"]

WSHAPES = {
    "L0_attn_Wq": (1024, 1024), "L0_attn_Wk": (1024, 1024), "L0_attn_Wv": (1024, 1024),
    "L0_attn_Wg": (1024, 1024), "L0_attn_Wo": (1024, 1024),
    "L0_attn_sk_w1": (1024, 16), "L0_attn_sk_w2": (16, 256), "L0_attn_sk_b2": (256,),
    "L0_attn_sv_w1": (1024, 16), "L0_attn_sv_w2": (16, 256), "L0_attn_sv_b2": (256,),
    "L0_attn_gnorm_w": (256,),
    "L0_ffn_w_gate": (3072, 1024), "L0_ffn_w_up": (3072, 1024), "L0_ffn_w_down": (1024, 3072),
    "L1_attn_q_w": (1024, 1024), "L1_attn_q_b": (1024,),
    "L1_attn_k_w": (1024, 1024), "L1_attn_k_b": (1024,),
    "L1_attn_v_w": (1024, 1024), "L1_attn_v_b": (1024,),
    "L1_attn_out_w": (1024, 1024), "L1_attn_out_b": (1024,),
    "L1_attn_rel_emb": (32, 16), "L1_attn_grep_w": (64, 8), "L1_attn_grep_b": (8,),
    "L1_attn_grep_a": (1, 16, 1, 1), "L1_attn_bias_k": (1, 1, 1024), "L1_attn_bias_v": (1, 1, 1024),
    "L1_ffn_gate_w": (8, 1024), "L1_ffn_w1": (8, 1024, 1024), "L1_ffn_b1": (8, 1024),
    "L1_ffn_w2": (8, 1024, 1024), "L1_ffn_b2": (8, 1024),
    "L1_ffn_w3": (8, 1024, 1024), "L1_ffn_b3": (8, 1024),
    "L1_ffn_sw1": (2048, 1024), "L1_ffn_sw2": (2048, 1024), "L1_ffn_sw3": (1024, 2048),
}

BLOBK_W = 8 * 128
BLOBVZ_W = 8 * 257
BLOBZK_W = 8


def build(cfg):
    upto = cfg.get("upto", 99)
    prefix = cfg.get("prefix", "cc")
    seg_only = cfg.get("seg_only", False)
    scanstop = cfg.get("scanstop", 99)
    a1stop = cfg.get("a1stop", 99)
    debug = cfg.get("debug", set())

    nc = bass.Bass("TRN2", target_bir_lowering=False)
    p = Prog(nc)

    SPECS = dict(WSHAPES)
    SPECS.update({"xs": (NT, D), "identf": (128, 128), "identb": (128, 128),
                  "utmask": (128, 128), "cos": (TC, 128), "sin": (TC, 128),
                  "pmask": (128, 8), "blobsK": (8 * 128, BLOBK_W),
                  "blobsVz": (8 * 64, BLOBVZ_W), "blobszk": (8 * 64, BLOBZK_W),
                  "dupm": (128, 256), "selm": (128, 512), "segm": (128, 128),
                  "gmaskm": (128, 128), "bcastP": (16, 2112)})
    BF16_IN = {"identb", "dupm", "selm", "segm", "bcastP"}

    class LazyInputs:
        def __init__(self):
            self.cache = {}

        def __getitem__(self, nm):
            if nm not in self.cache:
                dt = I32 if nm in WNAMES_I32 else (BF16 if nm in BF16_IN else F32)
                self.cache[nm] = nc.dram_tensor(nm, list(SPECS[nm]), dt,
                                                kind="ExternalInput").ap()
            return self.cache[nm]

    inp = LazyInputs()

    outs = []

    def out_t(name, shape, dtype=F32):
        outs.append(name)
        return nc.dram_tensor(name, shape, dtype, kind="ExternalOutput").ap()

    if seg_only:
        o_blobK = out_t("blobK", [128, BLOBK_W])
        o_blobVz = out_t("blobVz", [64, BLOBVZ_W])
        o_blobzk = out_t("blobzk", [64, BLOBZK_W])
    else:
        o_ys = out_t("ys", [NT, D])

    dbg = {}

    def dbg_t(name, shape, dtype=F32):
        if name in debug:
            dbg[name] = out_t("dbg_" + name, shape, dtype)

    dbg_t("xn", [8 * 128, NT])
    dbg_t("t1", [16, NT])
    dbg_t("segK", [128, BLOBK_W])
    dbg_t("segVz", [64, BLOBVZ_W])
    dbg_t("segzk", [64, BLOBZK_W])
    dbg_t("preK", [128, BLOBK_W])
    dbg_t("preVz", [64, BLOBVZ_W])
    dbg_t("prezk", [64, BLOBZK_W])
    dbg_t("ffn0", [8 * 128, NT])
    dbg_t("og", [16 * 128, TC])
    dbg_t("attn0", [8 * 128, NT])
    dbg_t("x1", [8 * 128, NT])
    dbg_t("xn1", [8 * 128, NT])
    dbg_t("attn1", [8 * 128, NT])
    dbg_t("moemsc", [16, NT])
    dbg_t("moe", [8 * 128, NT])
    dbg_t("zsh", [8 * 128, NT])

    identf = p.sb("identf_s", [128, 128], F32)
    identb = p.sb("identb_s", [128, 128], BF16)
    utmask = p.sb("utmask_s", [128, 128], F32)
    cos_s = p.sb("cos_s", [128, 8 * 128], BF16)
    sin_s = p.sb("sin_s", [128, 8 * 128], BF16)
    pmask_s = p.sb("pmask_s", [128, 8], F32)
    onesf = p.sb("onesf", [128, 128], F32)
    onesb = p.sb("onesb", [128, 128], BF16)

    p.dma("sync", identf[:, :], inp["identf"], w=[identf])
    p.dma("sync", identb[:, :], inp["identb"], w=[identb])
    p.dma("sync", utmask[:, :], inp["utmask"], w=[utmask])
    p.dma("sync", pmask_s[:, :], inp["pmask"], w=[pmask_s])
    for tt in range(8):
        p.dma("gpsimd", cos_s[:, tt * 128:(tt + 1) * 128], inp["cos"][tt * 128:(tt + 1) * 128, :], w=[cos_s])
        p.dma("gpsimd", sin_s[:, tt * 128:(tt + 1) * 128], inp["sin"][tt * 128:(tt + 1) * 128, :], w=[sin_s])
    p.memset("vector", onesf[:, :], 1.0, w=[onesf])
    p.memset("vector", onesb[:, :], 1.0, w=[onesb])

    psf = [p.ps(f"psf{i}", [128, 512]) for i in range(6)]
    PBA = p.ps("pbA", [128, 1024], BF16)
    PBB = p.ps("pbB", [128, 1024], BF16)

    acc = [p.sb(f"acc{k}", [128, NT], F32) for k in range(8)]
    p.push_scope()
    xnat = [p.sb(f"xnat{j}", [128, D], F32) for j in range(3)]
    for tt in range(16):
        xb = xnat[tt % 3]
        p.dma("sync", xb[:, :], inp["xs"][tt * 128:(tt + 1) * 128, :], w=[xb])
        for k in range(8):
            pb = psf[(tt * 8 + k) % 4]
            p.tr(pb[:, 0:128], xb[:, k * 128:(k + 1) * 128], identf[:, :], r=[xb, identf], w=[pb])
            p.cp("vector" if k % 2 == 0 else "scalar",
                 acc[k][:, tt * 128:(tt + 1) * 128], pb[:, 0:128], r=[pb], w=[acc[k]])
    p.pop_scope()

    def layernorm_fm(src_tiles, xn_tiles, tag):
        p.push_scope()
        stmp = [p.sb(f"stmp_{tag}{j}", [128, NT], F32) for j in range(2)]
        rowA = p.sb(f"rowA_{tag}", [1, NT], F32)
        rowB = p.sb(f"rowB_{tag}", [1, NT], F32)
        rowC = p.sb(f"rowC_{tag}", [1, NT], F32)
        xsq = [p.sb(f"xsq_{tag}{j}", [128, 512], BF16) for j in range(2)]
        for k in range(8):
            p.cp("vector" if k % 2 == 0 else "gpsimd", xn_tiles[k][:, :], src_tiles[k][:, :],
                 r=[src_tiles[k]], w=[xn_tiles[k]])
        for mch in range(4):
            ms = slice(mch * 512, (mch + 1) * 512)
            pbx = psf[4]
            for k in range(8):
                p.mm(pbx[0:1, 0:512], onesb[:, 0:1], xn_tiles[k][:, ms],
                     start=(k == 0), stop=(k == 7), r=[onesb, xn_tiles[k]], w=[pbx])
            p.cp("vector", rowA[0:1, ms], pbx[0:1, 0:512], r=[pbx], w=[rowA])
            pbq = psf[5]
            for k in range(8):
                xq = xsq[k % 2]
                p.act(xq[:, 0:512], src_tiles[k][:, ms], AF.Square, r=[src_tiles[k]], w=[xq])
                p.mm(pbq[0:1, 0:512], onesb[:, 0:1], xq[:, 0:512],
                     start=(k == 0), stop=(k == 7), r=[onesb, xq], w=[pbq])
            p.cp("vector", rowB[0:1, ms], pbq[0:1, 0:512], r=[pbq], w=[rowB])
        p.tss("vector", rowA[0:1, :], rowA[0:1, :], 1.0 / D, ALU.mult, r=[rowA], w=[rowA])
        p.tt("vector", rowC[0:1, :], rowA[0:1, :], rowA[0:1, :], ALU.mult, r=[rowA], w=[rowC])
        p.stt("vector", rowB[0:1, :], rowB[0:1, :], 1.0 / D, rowC[0:1, :], ALU.mult, ALU.subtract,
              r=[rowB, rowC], w=[rowB])
        p.act(rowB[0:1, :], rowB[0:1, :], AF.Sqrt, bias=EPS, scale=1.0, r=[rowB], w=[rowB])
        p.recip(rowB[0:1, :], rowB[0:1, :], r=[rowB], w=[rowB])
        p.tt("vector", rowC[0:1, :], rowA[0:1, :], rowB[0:1, :], ALU.mult, r=[rowA, rowB], w=[rowC])
        rstd_bc = p.sb(f"rstdbc_{tag}", [128, NT], F32)
        mr_bc = p.sb(f"mrbc_{tag}", [128, NT], F32)
        for mch in range(4):
            ms = slice(mch * 512, (mch + 1) * 512)
            pbx = psf[4]
            p.mm(pbx[:, 0:512], onesf[0:1, :], rowB[0:1, ms], start=True, stop=True,
                 r=[onesf, rowB], w=[pbx])
            p.cp("vector", rstd_bc[:, ms], pbx[:, 0:512], r=[pbx], w=[rstd_bc])
            pbq = psf[5]
            p.mm(pbq[:, 0:512], onesf[0:1, :], rowC[0:1, ms], start=True, stop=True,
                 r=[onesf, rowC], w=[pbq])
            p.cp("scalar", mr_bc[:, ms], pbq[:, 0:512], r=[pbq], w=[mr_bc])
        for k in range(8):
            e_ = "vector" if k % 2 == 0 else "gpsimd"
            p.tt(e_, stmp[k % 2][:, :], src_tiles[k][:, :], rstd_bc[:, :], ALU.mult,
                 r=[src_tiles[k], rstd_bc], w=[stmp[k % 2]])
            p.tt(e_, xn_tiles[k][:, :], stmp[k % 2][:, :], mr_bc[:, :], ALU.subtract,
                 r=[stmp[k % 2], mr_bc], w=[xn_tiles[k]])
        p.pop_scope()

    xn = [p.sb(f"xn{k}", [128, NT], BF16) for k in range(8)]
    layernorm_fm(acc, xn, "ln0")

    def dump_fm(name, tiles, conv=True):
        if name not in dbg:
            return
        for k in range(len(tiles)):
            if tiles[k].t.dtype == BF16:
                p.dma("gpsimd", dbg[name][k * 128:(k + 1) * 128, :], tiles[k][:, :], r=[tiles[k]])
            else:
                p.dma("sync", dbg[name][k * 128:(k + 1) * 128, :], tiles[k][:, :], r=[tiles[k]])

    dump_fm("xn", xn)
    if upto <= 1:
        return finish(p, nc, outs, inp)

    p.push_scope()
    skw1 = p.sb("skw1", [128, 8 * 16], BF16)
    svw1 = p.sb("svw1", [128, 8 * 16], BF16)
    for k in range(8):
        p.dma("gpsimd", skw1[:, k * 16:(k + 1) * 16], inp["L0_attn_sk_w1"][k * 128:(k + 1) * 128, :], w=[skw1])
        p.dma("gpsimd", svw1[:, k * 16:(k + 1) * 16], inp["L0_attn_sv_w1"][k * 128:(k + 1) * 128, :], w=[svw1])
    t1k = p.sb("t1k", [16, NT], BF16)
    t1v = p.sb("t1v", [16, NT], BF16)
    for mch in range(4):
        ms = slice(mch * 512, (mch + 1) * 512)
        pbx = psf[0]
        for k in range(8):
            p.mm(pbx[0:16, 0:512], skw1[:, k * 16:(k + 1) * 16], xn[k][:, ms],
                 start=(k == 0), stop=(k == 7), r=[skw1, xn[k]], w=[pbx])
        p.cp("vector", t1k[0:16, ms], pbx[0:16, 0:512], r=[pbx], w=[t1k])
        pby = psf[1]
        for k in range(8):
            p.mm(pby[0:16, 0:512], svw1[:, k * 16:(k + 1) * 16], xn[k][:, ms],
                 start=(k == 0), stop=(k == 7), r=[svw1, xn[k]], w=[pby])
        p.cp("vector", t1v[0:16, ms], pby[0:16, 0:512], r=[pby], w=[t1v])
    if "t1" in dbg:
        tf = p.sb("t1f", [16, NT], F32)
        p.cp("vector", tf[:, :], t1k[:, :], r=[t1k], w=[tf])
        p.dma("sync", dbg["t1"], tf[:, :], r=[tf])

    skw2 = p.sb("skw2", [16, 256], BF16)
    svw2 = p.sb("svw2", [16, 256], BF16)
    p.dma("gpsimd", skw2[:, :], inp["L0_attn_sk_w2"], w=[skw2])
    p.dma("gpsimd", svw2[:, :], inp["L0_attn_sv_w2"], w=[svw2])
    skb2r = p.sb("skb2r", [4, 64], F32)
    svb2r = p.sb("svb2r", [4, 64], F32)
    p.dma("gpsimd", skb2r[:, :], inp["L0_attn_sk_b2"].rearrange("(a b) -> a b", b=64), w=[skb2r])
    p.dma("gpsimd", svb2r[:, :], inp["L0_attn_sv_b2"].rearrange("(a b) -> a b", b=64), w=[svb2r])
    skb2c = p.sb("skb2c", [64, 4], F32)
    svb2c = p.sb("svb2c", [64, 4], F32)
    pbx = psf[0]
    p.tr(pbx[0:64, 0:4], skb2r[:, :], identf[0:4, 0:4], r=[skb2r, identf], w=[pbx])
    p.cp("vector", skb2c[:, :], pbx[0:64, 0:4], r=[pbx], w=[skb2c])
    pby = psf[1]
    p.tr(pby[0:64, 0:4], svb2r[:, :], identf[0:4, 0:4], r=[svb2r, identf], w=[pby])
    p.cp("vector", svb2c[:, :], pby[0:64, 0:4], r=[pby], w=[svb2c])

    def load_wslice(buf, wname, h, k):
        p.dma("gpsimd", buf[:, k * 256:(k + 1) * 256],
              inp[wname][k * 128:(k + 1) * 128, h * 256:(h + 1) * 256], w=[buf])

    ktm_b = [p.sb(f"ktm{j}", [128, 256], BF16) for j in range(2)]
    vtm_b = [p.sb(f"vtm{j}", [128, 256], BF16) for j in range(2)]
    wkf_b = [p.sb(f"wkf{j}", [64, 128], BF16) for j in range(2)]
    wvf_b = [p.sb(f"wvf{j}", [64, 128], BF16) for j in range(2)]
    wktm_b = [p.sb(f"wktm{j}", [128, 64], BF16) for j in range(2)]
    wvtm_b = [p.sb(f"wvtm{j}", [128, 64], BF16) for j in range(2)]
    rtmp = [p.sb(f"rtmp{j}", [128, 256], F32) for j in range(2)]
    rtmp += [p.sb(f"rtmp{j+2}", [64, 128], F32) for j in range(2)]

    def proj_chunk(b, h, tc, j, Wkh=None, Wvh=None, need=("k", "v", "wk", "wv")):
        col = b * TC + tc * 128
        cs = slice(col, col + 128)
        ts = slice(tc * 128, (tc + 1) * 128)
        res = {}
        if "k" in need:
            pk = psf[0]
            for k in range(8):
                p.mm(pk[:, 0:256], xn[k][:, cs], Wkh[:, k * 256:(k + 1) * 256],
                     start=(k == 0), stop=(k == 7), r=[xn[k], Wkh], w=[pk])
            t0 = rtmp[0]
            t1_ = rtmp[1]
            p.tt("vector", t0[:, 0:128], pk[:, 0:128], cos_s[:, ts], ALU.mult, r=[pk, cos_s], w=[t0])
            p.tt("vector", t0[:, 128:256], pk[:, 128:256], sin_s[:, ts], ALU.mult, r=[pk, sin_s], w=[t0])
            p.tt("vector", t1_[:, 0:128], pk[:, 128:256], cos_s[:, ts], ALU.mult, r=[pk, cos_s], w=[t1_])
            p.tt("vector", t1_[:, 128:256], pk[:, 0:128], sin_s[:, ts], ALU.mult, r=[pk, sin_s], w=[t1_])
            p.tt("vector", ktm_b[j][:, 0:128], t0[:, 0:128], t0[:, 128:256], ALU.subtract,
                 r=[t0], w=[ktm_b[j]])
            p.tt("vector", ktm_b[j][:, 128:256], t1_[:, 0:128], t1_[:, 128:256], ALU.add,
                 r=[t1_], w=[ktm_b[j]])
            res["ktm"] = ktm_b[j]
        if "v" in need:
            pv = psf[1]
            for k in range(8):
                p.mm(pv[:, 0:256], xn[k][:, cs], Wvh[:, k * 256:(k + 1) * 256],
                     start=(k == 0), stop=(k == 7), r=[xn[k], Wvh], w=[pv])
            p.cp("vector", vtm_b[j][:, :], pv[:, 0:256], r=[pv], w=[vtm_b[j]])
            res["vtm"] = vtm_b[j]
        if "wk" in need:
            pw = psf[2]
            p.mm(pw[0:64, 0:128], skw2[:, h * 64:(h + 1) * 64], t1k[:, cs],
                 start=True, stop=True, r=[skw2, t1k], w=[pw])
            hc = skb2c[0:64, h:h + 1]
            wtmp = rtmp[2]
            p.ts2("vector", wtmp[0:64, 0:128], pw[0:64, 0:128], hc, 32.0, ALU.add, ALU.min,
                  r=[pw, skb2c], w=[wtmp])
            p.tss("vector", wtmp[0:64, 0:128], wtmp[0:64, 0:128], -32.0, ALU.max,
                  r=[wtmp], w=[wtmp])
            p.act(wkf_b[j][0:64, 0:128], wtmp[0:64, 0:128], AF.Exp, r=[wtmp], w=[wkf_b[j]])
            p.tr(PBA[0:128, 0:64], wkf_b[j][0:64, 0:128], identb[0:64, 0:64],
                 r=[wkf_b[j], identb], w=[PBA])
            p.cp("vector", wktm_b[j][:, :], PBA[0:128, 0:64], r=[PBA], w=[wktm_b[j]])
            res["wkf"] = wkf_b[j]
            res["wktm"] = wktm_b[j]
        if "wv" in need:
            pw = psf[2]
            p.mm(pw[0:64, 0:128], svw2[:, h * 64:(h + 1) * 64], t1v[:, cs],
                 start=True, stop=True, r=[svw2, t1v], w=[pw])
            hc = svb2c[0:64, h:h + 1]
            wtmp = rtmp[3]
            p.ts2("vector", wtmp[0:64, 0:128], pw[0:64, 0:128], hc, 32.0, ALU.add, ALU.min,
                  r=[pw, svb2c], w=[wtmp])
            p.tss("vector", wtmp[0:64, 0:128], wtmp[0:64, 0:128], -32.0, ALU.max,
                  r=[wtmp], w=[wtmp])
            p.act(wvf_b[j][0:64, 0:128], wtmp[0:64, 0:128], AF.Exp, r=[wtmp], w=[wvf_b[j]])
            p.tr(PBB[0:128, 0:64], wvf_b[j][0:64, 0:128], identb[0:64, 0:64],
                 r=[wvf_b[j], identb], w=[PBB])
            p.cp("vector", wvtm_b[j][:, :], PBB[0:128, 0:64], r=[PBB], w=[wvtm_b[j]])
            res["wvf"] = wvf_b[j]
            res["wvtm"] = wvtm_b[j]
        return res

    if prefix != "cc":
        preK = p.sb("preK", [128, BLOBK_W], F32)
        preVz = p.sb("preVz", [64, BLOBVZ_W], F32)
        prezk = p.sb("prezk", [64, BLOBZK_W], F32)
    else:
        preK = preVz = prezk = None

    p.push_scope()
    blobK = p.sb("blobK_s", [128, BLOBK_W], F32)
    blobVz = p.sb("blobVz_s", [64, BLOBVZ_W], F32)
    blobzk = p.sb("blobzk_s", [64, BLOBZK_W], F32)
    onecol = p.sb("onecol", [128, 1], BF16)
    p.memset("vector", onecol[:, :], 1.0, w=[onecol])

    p.push_scope()
    Wkh_bufs = [p.sb(f"Wkh{j}", [128, 8 * 256], BF16) for j in range(2)]
    Wvh_bufs = [p.sb(f"Wvh{j}", [128, 8 * 256], BF16) for j in range(2)]
    for h in range(4):
        Wkh, Wvh = Wkh_bufs[h % 2], Wvh_bufs[h % 2]
        for k in range(8):
            load_wslice(Wkh, "L0_attn_Wk", h, k)
            load_wslice(Wvh, "L0_attn_Wv", h, k)
        for b in range(2):
            bh = b * 4 + h
            nvz = 0
            for tc in range(8):
                j = tc % 2
                r = proj_chunk(b, h, tc, j, Wkh=Wkh, Wvh=Wvh)
                st, sp = (tc == 0), (tc == 7)
                p.mm(psf[4][:, 0:64], r["ktm"][:, 0:128], r["wktm"][:, :],
                     start=st, stop=sp, r=[r["ktm"], r["wktm"]], w=[psf[4]])
                p.mm(psf[5][:, 0:64], r["ktm"][:, 128:256], r["wktm"][:, :],
                     start=st, stop=sp, r=[r["ktm"], r["wktm"]], w=[psf[5]])
                p.mm(psf[3][0:64, 0:256], r["wvtm"][:, :], r["vtm"][:, :],
                     start=(nvz == 0), stop=False, r=[r["wvtm"], r["vtm"]], w=[psf[3]])
                nvz += 1
                p.mm(psf[3][0:64, 256:257], r["wvtm"][:, :], onecol[:, :],
                     start=False, stop=False, r=[r["wvtm"], onecol], w=[psf[3]])
                p.mm(psf[3][0:64, 260:261], r["wktm"][:, :], onecol[:, :],
                     start=False, stop=(tc == 7), r=[r["wktm"], onecol], w=[psf[3]])
            p.cp("vector", blobK[:, bh * 128:bh * 128 + 64], psf[4][:, 0:64], r=[psf[4]], w=[blobK])
            p.cp("vector", blobK[:, bh * 128 + 64:(bh + 1) * 128], psf[5][:, 0:64], r=[psf[5]], w=[blobK])
            p.cp("vector", blobVz[0:64, bh * 257:(bh + 1) * 257], psf[3][0:64, 0:257],
                 r=[psf[3]], w=[blobVz])
            p.cp("vector", blobzk[0:64, bh:bh + 1], psf[3][0:64, 260:261], r=[psf[3]], w=[blobzk])
    p.pop_scope()

    if "segK" in dbg:
        p.dma("sync", dbg["segK"], blobK[:, :], r=[blobK])
    if "segVz" in dbg:
        p.dma("sync", dbg["segVz"], blobVz[:, :], r=[blobVz])
    if "segzk" in dbg:
        p.dma("sync", dbg["segzk"], blobzk[:, :], r=[blobzk])
    if seg_only:
        p.dma("sync", o_blobK, blobK[:, :], r=[blobK])
        p.dma("sync", o_blobVz, blobVz[:, :], r=[blobVz])
        p.dma("sync", o_blobzk, blobzk[:, :], r=[blobzk])
        return finish(p, nc, outs, inp)
    if upto <= 2:
        return finish(p, nc, outs, inp)

    if prefix == "zero":
        p.memset("vector", preK[:, :], 0.0, w=[preK])
        p.memset("vector", preVz[:, :], 0.0, w=[preVz])
        p.memset("vector", prezk[:, :], 0.0, w=[prezk])
    elif prefix == "input":
        p.memset("vector", preK[:, :], 0.0, w=[preK])
        p.memset("vector", preVz[:, :], 0.0, w=[preVz])
        p.memset("vector", prezk[:, :], 0.0, w=[prezk])
        bK = p.sb("binK", [128, BLOBK_W], F32)
        bVz = p.sb("binVz", [64, BLOBVZ_W], F32)
        bzk = p.sb("binzk", [64, BLOBZK_W], F32)
        for cc_ in range(8):
            p.dma("sync", bK[:, :], inp["blobsK"][cc_ * 128:(cc_ + 1) * 128, :], w=[bK])
            p.dma("sync", bVz[:, :], inp["blobsVz"][cc_ * 64:(cc_ + 1) * 64, :], w=[bVz])
            p.dma("sync", bzk[:, :], inp["blobszk"][cc_ * 64:(cc_ + 1) * 64, :], w=[bzk])
            p.stt("vector", preK[:, :], bK[:, :], pmask_s[0:128, cc_:cc_ + 1], preK[:, :],
                  ALU.mult, ALU.add, r=[bK, pmask_s, preK], w=[preK])
            p.stt("vector", preVz[:, :], bVz[:, :], pmask_s[0:64, cc_:cc_ + 1], preVz[:, :],
                  ALU.mult, ALU.add, r=[bVz, pmask_s, preVz], w=[preVz])
            p.stt("vector", prezk[:, :], bzk[:, :], pmask_s[0:64, cc_:cc_ + 1], prezk[:, :],
                  ALU.mult, ALU.add, r=[bzk, pmask_s, prezk], w=[prezk])
    else:
        ccK_in = nc.dram_tensor("ccK_in", [8 * 128, BLOBK_W], F32).ap()
        ccK_out = nc.dram_tensor("ccK_out", [128, BLOBK_W], F32).ap()
        ccVz_in = nc.dram_tensor("ccVz_in", [8 * 64, BLOBVZ_W], F32).ap()
        ccVz_out = nc.dram_tensor("ccVz_out", [64, BLOBVZ_W], F32).ap()
        cczk_in = nc.dram_tensor("cczk_in", [8 * 64, BLOBZK_W], F32).ap()
        cczk_out = nc.dram_tensor("cczk_out", [64, BLOBZK_W], F32).ap()
        bK_d = Buf(None, "ccKin_d")
        bVz_d = Buf(None, "ccVzin_d")
        bzk_d = Buf(None, "cczkin_d")
        oK_d = Buf(None, "ccKout_d")
        oVz_d = Buf(None, "ccVzout_d")
        ozk_d = Buf(None, "cczkout_d")
        msK = p.sb("msK", [128, BLOBK_W], F32)
        msVz = p.sb("msVz", [64, BLOBVZ_W], F32)
        mszk = p.sb("mszk", [64, BLOBZK_W], F32)
        for jj in range(8):
            p.tss("vector", msK[:, :], blobK[:, :], pmask_s[0:128, jj:jj + 1], ALU.mult,
                  r=[blobK, pmask_s], w=[msK])
            p.dma("gpsimd", ccK_in[jj * 128:(jj + 1) * 128, :], msK[:, :], r=[msK], w=[bK_d])
            p.tss("vector", msVz[:, :], blobVz[:, :], pmask_s[0:64, jj:jj + 1], ALU.mult,
                  r=[blobVz, pmask_s], w=[msVz])
            p.dma("gpsimd", ccVz_in[jj * 64:(jj + 1) * 64, :], msVz[:, :], r=[msVz], w=[bVz_d])
            p.tss("vector", mszk[:, :], blobzk[:, :], pmask_s[0:64, jj:jj + 1], ALU.mult,
                  r=[blobzk, pmask_s], w=[mszk])
            p.dma("gpsimd", cczk_in[jj * 64:(jj + 1) * 64, :], mszk[:, :], r=[mszk], w=[bzk_d])
        p.collective("ReduceScatter", ALU.add, ccK_in.opt(), ccK_out.opt(),
                     r=[bK_d], w=[oK_d], tag="rsK")
        p.collective("ReduceScatter", ALU.add, ccVz_in.opt(), ccVz_out.opt(),
                     r=[bVz_d], w=[oVz_d], tag="rsVz")
        p.collective("ReduceScatter", ALU.add, cczk_in.opt(), cczk_out.opt(),
                     r=[bzk_d], w=[ozk_d], tag="rszk")
        readback = (ccK_out, ccVz_out, cczk_out, [oK_d, oVz_d, ozk_d])
    p.pop_scope()

    if "preK" in dbg and prefix != "cc":
        p.dma("sync", dbg["preK"], preK[:, :], r=[preK])
        p.dma("sync", dbg["preVz"], preVz[:, :], r=[preVz])
        p.dma("sync", dbg["prezk"], prezk[:, :], r=[prezk])
    if upto <= 3:
        return finish(p, nc, outs, inp)

    S127 = 1.0 / 127.0
    p.push_scope()
    hbuf = [p.sb(f"hb{i}", [128, 1024], BF16) for i in range(24)]
    wnat = [p.sb(f"wnat{j}", [128, 1024], F32) for j in range(2)]
    wnb = [p.sb(f"wnb{j}", [128, 1024], BF16) for j in range(1)]
    wT = [p.sb(f"wTf{j}", [128, 8 * 128], BF16) for j in range(4)]
    wTd8 = [p.sb(f"wTd8_{j}", [128, 8 * 128], BF16) for j in range(2)]
    gsil0s = [p.sb(f"gsil0_{i}", [128, 512], BF16) for i in range(2)]

    def load_wT(dst, wname, rows, kk_n, j, col0=0):
        for blk in range(0, kk_n, 8):
            nb = min(8, kk_n - blk)
            wn = wnat[j]
            wb = wnb[0]
            p.dma("gpsimd", wn[:, 0:nb * 128],
                  inp[wname][rows:rows + 128, col0 + blk * 128:col0 + (blk + nb) * 128], w=[wn])
            p.cp("vector" if j % 2 == 0 else "gpsimd", wb[:, 0:nb * 128], wn[:, 0:nb * 128],
                 r=[wn], w=[wb])
            pb = PBA if (j + blk // 8) % 2 == 0 else PBB
            for k in range(nb):
                p.trp(pb[:, k * 128:(k + 1) * 128], wb[:, k * 128:(k + 1) * 128], identb[:, :],
                      start=(k == 0), stop=(k == nb - 1), r=[wb, identb], w=[pb])
            p.cp("vector" if j % 2 == 0 else "scalar",
                 dst[:, blk * 128:(blk + nb) * 128], pb[:, 0:nb * 128], r=[pb], w=[dst])

    for c2 in range(2):
        tbase = c2 * 1024
        load_wT(wT[0], "L0_ffn_w_gate", 0, 8, 0)
        load_wT(wT[1], "L0_ffn_w_up", 0, 8, 1)
        for ht in range(24):
            jg, ju = 2 * (ht % 2), 2 * (ht % 2) + 1
            if False:
                pass
            for m2 in range(2):
                ms = slice(tbase + m2 * 512, tbase + (m2 + 1) * 512)
                pg = psf[0 + 2 * m2]
                for k in range(8):
                    p.mm(pg[:, 0:512], wT[jg][:, k * 128:(k + 1) * 128], xn[k][:, ms],
                         start=(k == 0), stop=(k == 7), r=[wT[jg], xn[k]], w=[pg])
                p.act(gsil0s[m2][:, 0:512], pg[:, 0:512], AF.Silu, scale=S127, r=[pg], w=[gsil0s[m2]])
                pu = psf[1 + 2 * m2]
                for k in range(8):
                    p.mm(pu[:, 0:512], wT[ju][:, k * 128:(k + 1) * 128], xn[k][:, ms],
                         start=(k == 0), stop=(k == 7), r=[wT[ju], xn[k]], w=[pu])
                p.stt("vector", hbuf[ht][:, m2 * 512:(m2 + 1) * 512], pu[:, 0:512], S127,
                      gsil0s[m2][:, 0:512], ALU.mult, ALU.mult, r=[pu, gsil0s[m2]], w=[hbuf[ht]])
            if ht + 1 < 24:
                load_wT(wT[2 * ((ht + 1) % 2)], "L0_ffn_w_gate", (ht + 1) * 128, 8, 0)
                load_wT(wT[2 * ((ht + 1) % 2) + 1], "L0_ffn_w_up", (ht + 1) * 128, 8, 1)
        load_wT(wTd8[0], "L0_ffn_w_down", 0, 8, 0, col0=0)
        for dt_ in range(8):
            pd0, pd1 = psf[2], psf[3]
            for blk in range(3):
                gidx = dt_ * 3 + blk
                wsl = wTd8[gidx % 2]
                if blk < 2:
                    load_wT(wTd8[(gidx + 1) % 2], "L0_ffn_w_down", dt_ * 128, 8,
                            (gidx + 1) % 2, col0=(blk + 1) * 1024)
                elif dt_ + 1 < 8:
                    load_wT(wTd8[(gidx + 1) % 2], "L0_ffn_w_down", (dt_ + 1) * 128, 8,
                            (gidx + 1) % 2, col0=0)
                for m2 in range(2):
                    pd = pd0 if m2 == 0 else pd1
                    ms0 = tbase + m2 * 512
                    for k in range(8):
                        p.mm(pd[:, 0:512], wsl[:, k * 128:(k + 1) * 128],
                             hbuf[blk * 8 + k][:, ms0 - tbase + tbase:ms0 + 512]
                             if False else hbuf[blk * 8 + k][:, m2 * 512:(m2 + 1) * 512],
                             start=(blk == 0 and k == 0), stop=(blk == 2 and k == 7),
                             r=[wsl, hbuf[blk * 8 + k]], w=[pd])
            for m2 in range(2):
                ms = slice(tbase + m2 * 512, tbase + (m2 + 1) * 512)
                pd = pd0 if m2 == 0 else pd1
                p.stt("vector", acc[dt_][:, ms], pd[:, 0:512], S127, acc[dt_][:, ms],
                      ALU.mult, ALU.add, r=[pd, acc[dt_]], w=[acc[dt_]])
    p.pop_scope()
    dump_fm("ffn0", acc)
    if upto <= 4:
        return finish(p, nc, outs, inp)

    if prefix == "cc":
        ccKo, ccVzo, cczko, ccdeps = readback

    p.push_scope()
    p.push_scope()
    cosT_s = p.sb("cosT_s", [128, 8 * 128], BF16)
    sinT_s = p.sb("sinT_s", [128, 8 * 128], BF16)
    for tt in range(8):
        pb = PBA if tt % 2 == 0 else PBB
        p.trp(pb[:, 0:128], cos_s[:, tt * 128:(tt + 1) * 128], identb[:, :], start=True,
              stop=False, r=[cos_s, identb], w=[pb])
        p.trp(pb[:, 128:256], sin_s[:, tt * 128:(tt + 1) * 128], identb[:, :], start=False,
              stop=True, r=[sin_s, identb], w=[pb])
        p.cp("vector", cosT_s[:, tt * 128:(tt + 1) * 128], pb[:, 0:128], r=[pb], w=[cosT_s])
        p.cp("vector", sinT_s[:, tt * 128:(tt + 1) * 128], pb[:, 128:256], r=[pb], w=[sinT_s])

    gnr = p.sb("gnr", [2, 128], F32)
    p.dma("gpsimd", gnr[:, :], inp["L0_attn_gnorm_w"].rearrange("(a b) -> a b", b=128), w=[gnr])
    gnc = p.sb("gnc", [128, 2], F32)
    p.tr(psf[0][:, 0:2], gnr[:, :], identf[0:2, 0:2], r=[gnr, identf], w=[psf[0]])
    p.cp("vector", gnc[:, :], psf[0][:, 0:2], r=[psf[0]], w=[gnc])
    og4 = [p.sb(f"og4_{i}", [128, 256], BF16) for i in range(8)]
    wo_t = [p.sb(f"wo{kk}", [128, D], BF16) for kk in range(4)]
    wbig = [p.sb(f"wbig{j}", [128, 8 * 512], BF16) for j in range(4)]

    def load_wpair(dst, wname, hp):
        for k in range(8):
            p.dma("gpsimd", dst[:, k * 512:(k + 1) * 512],
                  inp[wname][k * 128:(k + 1) * 128, hp * 512:(hp + 1) * 512], w=[dst])
        return dst
    ktm2 = [[p.sb(f"ktm2_{j}{b}", [128, 512], BF16) for b in range(2)] for j in range(1)]
    vtm2 = [[p.sb(f"vtm2_{j}{b}", [128, 512], BF16) for b in range(2)] for j in range(1)]
    gsil2 = [[p.sb(f"gsil2_{j}{b}", [128, 512], BF16) for b in range(2)] for j in range(1)]
    qfm4 = [[p.sb(f"qfm4_{j}{hh}", [128, 512], BF16) for hh in range(2)] for j in range(2)]
    kfm4 = [[p.sb(f"kfm4_{j}{b}", [128, 512], BF16) for b in range(2)] for j in range(2)]
    wkf4 = [[p.sb(f"wkf4_{j}{s}", [64, 128], BF16) for s in range(4)] for j in range(1)]
    wvf4 = [[p.sb(f"wvf4_{j}{s}", [64, 128], BF16) for s in range(4)] for j in range(1)]
    wktm2 = [[p.sb(f"wktm2_{j}{b}", [128, 128], BF16) for b in range(2)] for j in range(1)]
    wvtm2 = [[p.sb(f"wvtm2_{j}{b}", [128, 128], BF16) for b in range(2)] for j in range(1)]
    zkb = [[p.sb(f"zkb_{j}{b}", [128, 128], F32) for b in range(2)] for j in range(1)]
    zvb = [[p.sb(f"zvb_{j}{b}", [128, 128], F32) for b in range(2)] for j in range(1)]
    rzT2 = [[p.sb(f"rzT2_{j}{b}", [128, 256], F32) for b in range(2)] for j in range(1)]
    rt4 = [p.sb(f"rt4_{j}", [128, 512], F32) for j in range(1)]
    rt4.append(rt4[0])
    zkc4 = [p.sb(f"zkc4_{s}", [128, 1], F32) for s in range(4)]
    zvc4 = [p.sb(f"zvc4_{s}", [128, 1], F32) for s in range(4)]
    KstF4 = [p.sb(f"KstF4_{s}", [128, 128], F32) for s in range(4)]
    KstB4 = [p.sb(f"KstB4_{s}", [128, 128], BF16) for s in range(4)]
    VstF4 = [p.sb(f"VstF4_{s}", [64, 256], F32) for s in range(4)]
    VstB4 = [p.sb(f"VstB4_{s}", [64, 256], BF16) for s in range(4)]
    st4 = [p.sb(f"st4_{s}", [128, 128], BF16) for s in range(4)]
    lg4 = [p.sb(f"lg4_{s}", [128, 64], F32) for s in range(4)]
    nmx4 = [p.sb(f"nmx4_{s}", [128, 1], F32) for s in range(4)]
    sume4 = [p.sb(f"sume4_{s}", [128, 1], F32) for s in range(4)]
    et4 = [p.sb(f"et4_{s}", [128, 64], BF16) for s in range(4)]
    pttm4 = [p.sb(f"pttm4_{s}", [128, 64], BF16) for s in range(4)]
    ptfm4 = [p.sb(f"ptfm4_{s}", [64, 128], BF16) for s in range(4)]
    at4 = [p.sb(f"at4_{s}", [128, 128], BF16) for s in range(4)]
    sqs4 = [p.sb(f"sqs4_{s}", [128, 256], BF16) for s in range(1)]
    ssq4 = [p.sb(f"ssq4_{s}", [128, 1], F32) for s in range(4)]
    rstd4 = [p.sb(f"rstd4_{s}", [128, 1], F32) for s in range(4)]
    ogtm4 = [p.sb(f"ogtm4_{s}", [128, 256], BF16) for s in range(4)]

    def xn_b2t(tc):
        return [xn[k].ap.rearrange("p (b t) -> p b t", b=2)[:, :, tc * 128:(tc + 1) * 128]
                for k in range(8)]

    for hp in range(2):
        Wkp = load_wpair(wbig[0], "L0_attn_Wk", hp)
        Wvp = load_wpair(wbig[1], "L0_attn_Wv", hp)
        Wgp = load_wpair(wbig[2], "L0_attn_Wg", hp)
        Wqp = load_wpair(wbig[3], "L0_attn_Wq", hp)
        for kk in range(4):
            p.dma("gpsimd", wo_t[kk][:, :],
                  inp["L0_attn_Wo"][(hp * 4 + kk) * 128:(hp * 4 + kk + 1) * 128, :], w=[wo_t[kk]])
            p.tss("vector", wo_t[kk][:, :], wo_t[kk][:, :], gnc[0:128, kk % 2:kk % 2 + 1],
                  ALU.mult, r=[wo_t[kk], gnc], w=[wo_t[kk]])
        for s2 in range(4):
            b, hh = s2 // 2, s2 % 2
            bh = b * 4 + hp * 2 + hh
            if prefix == "cc":
                rdeps = ccdeps
                p.dma("sync", KstF4[s2][:, :], ccKo[0:128, bh * 128:(bh + 1) * 128],
                      r=rdeps, w=[KstF4[s2]])
                p.dma("gpsimd", KstB4[s2][:, :], ccKo[0:128, bh * 128:(bh + 1) * 128],
                      r=rdeps, w=[KstB4[s2]])
                p.dma("sync", VstF4[s2][:, :], ccVzo[0:64, bh * 257:bh * 257 + 256],
                      r=rdeps, w=[VstF4[s2]])
                p.dma("gpsimd", VstB4[s2][:, :], ccVzo[0:64, bh * 257:bh * 257 + 256],
                      r=rdeps, w=[VstB4[s2]])
                p.dma("sync", zvc4[s2][0:64, :], ccVzo[0:64, bh * 257 + 256:bh * 257 + 257],
                      r=rdeps, w=[zvc4[s2]], slow=True)
                p.dma("sync", zkc4[s2][0:64, :], cczko[0:64, bh:bh + 1], r=rdeps,
                      w=[zkc4[s2]], slow=True)
            else:
                p.cp("vector", KstF4[s2][:, :], preK[:, bh * 128:(bh + 1) * 128], r=[preK], w=[KstF4[s2]])
                p.cp("vector", KstB4[s2][:, :], preK[:, bh * 128:(bh + 1) * 128], r=[preK], w=[KstB4[s2]])
                p.cp("vector", VstF4[s2][:, :], preVz[0:64, bh * 257:bh * 257 + 256], r=[preVz], w=[VstF4[s2]])
                p.cp("vector", VstB4[s2][:, :], preVz[0:64, bh * 257:bh * 257 + 256], r=[preVz], w=[VstB4[s2]])
                p.cp("vector", zvc4[s2][0:64, :], preVz[0:64, bh * 257 + 256:bh * 257 + 257],
                     r=[preVz], w=[zvc4[s2]])
                p.cp("vector", zkc4[s2][0:64, :], prezk[0:64, bh:bh + 1], r=[prezk], w=[zkc4[s2]])
        for tc in range(8):
            jj = tc % 2
            ts = slice(tc * 128, (tc + 1) * 128)
            rhsb = xn_b2t(tc)
            if scanstop <= 11:
                continue
            for b in range(2):
                cs = slice(b * TC + tc * 128, b * TC + (tc + 1) * 128)
                pk = psf[b]
                for k in range(8):
                    p.mm(pk[:, 0:512], xn[k][:, cs], Wkp[:, k * 512:(k + 1) * 512],
                         start=(k == 0), stop=(k == 7), r=[xn[k], Wkp], w=[pk])
                P4 = pk.ap[:, 0:512].rearrange("p (hh half t) -> p hh half t", hh=2, half=2)
                K4 = ktm2[0][b].ap.rearrange("p (hh half t) -> p hh half t", hh=2, half=2)
                cosb = cos_s.ap[:, ts].unsqueeze(1).to_broadcast([128, 2, 128])
                sinb = sin_s.ap[:, ts].unsqueeze(1).to_broadcast([128, 2, 128])
                ra = rt4[0]
                rav = ra.ap.rearrange("p (hh t) -> p hh t", hh=2)
                p.tt("vector", rav[:, :, 0:128], P4[:, :, 0, :], cosb, ALU.mult, r=[pk, cos_s], w=[ra])
                p.tt("vector", rav[:, :, 128:256], P4[:, :, 1, :], sinb, ALU.mult, r=[pk, sin_s], w=[ra])
                p.tt("vector", K4[:, :, 0, :], rav[:, :, 0:128], rav[:, :, 128:256], ALU.subtract,
                     r=[ra], w=[ktm2[0][b]])
                p.tt("vector", rav[:, :, 0:128], P4[:, :, 1, :], cosb, ALU.mult, r=[pk, cos_s], w=[ra])
                p.tt("vector", rav[:, :, 128:256], P4[:, :, 0, :], sinb, ALU.mult, r=[pk, sin_s], w=[ra])
                p.tt("vector", K4[:, :, 1, :], rav[:, :, 0:128], rav[:, :, 128:256], ALU.add,
                     r=[ra], w=[ktm2[0][b]])
                pv = psf[2 + b]
                for k in range(8):
                    p.mm(pv[:, 0:512], xn[k][:, cs], Wvp[:, k * 512:(k + 1) * 512],
                         start=(k == 0), stop=(k == 7), r=[xn[k], Wvp], w=[pv])
                p.cp("vector", vtm2[0][b][:, :], pv[:, 0:512], r=[pv], w=[vtm2[0][b]])
            if scanstop <= 12:
                continue
            for b in range(2):
                cs = slice(b * TC + tc * 128, b * TC + (tc + 1) * 128)
                pg = psf[4 + b]
                for k in range(8):
                    p.mm(pg[:, 0:512], xn[k][:, cs], Wgp[:, k * 512:(k + 1) * 512],
                         start=(k == 0), stop=(k == 7), r=[xn[k], Wgp], w=[pg])
                p.act(gsil2[0][b][:, :], pg[:, 0:512], AF.Silu, r=[pg], w=[gsil2[0][b]])
            for hh in range(2):
                pq = psf[hh]
                for dt_ in range(2):
                    for k in range(8):
                        p.mm(pq[:, dt_ * 256:(dt_ + 1) * 256],
                             Wqp[:, k * 512 + hh * 256 + dt_ * 128:k * 512 + hh * 256 + (dt_ + 1) * 128],
                             rhsb[k],
                             start=(dt_ == 0 and k == 0), stop=(dt_ == 1 and k == 7),
                             r=[Wqp, xn[k]], w=[pq])
                P4 = pq.ap[:, 0:512].rearrange("p (dt b t) -> p dt b t", dt=2, b=2)
                Q4 = qfm4[jj][hh].ap.rearrange("p (dt b t) -> p dt b t", dt=2, b=2)
                cosTb = cosT_s.ap[:, ts].unsqueeze(1).to_broadcast([128, 2, 128])
                sinTb = sinT_s.ap[:, ts].unsqueeze(1).to_broadcast([128, 2, 128])
                ra = rt4[0]
                rav = ra.ap.rearrange("p (x t) -> p x t", x=2)
                p.tt("vector", rav[:, :, 0:128], P4[:, 0, :, :], cosTb, ALU.mult, r=[pq, cosT_s], w=[ra])
                p.tt("vector", rav[:, :, 128:256], P4[:, 1, :, :], sinTb, ALU.mult, r=[pq, sinT_s], w=[ra])
                p.tt("vector", Q4[:, 0, :, :], rav[:, :, 0:128], rav[:, :, 128:256], ALU.subtract,
                     r=[ra], w=[qfm4[jj][hh]])
                p.tt("vector", rav[:, :, 0:128], P4[:, 1, :, :], cosTb, ALU.mult, r=[pq, cosT_s], w=[ra])
                p.tt("vector", rav[:, :, 128:256], P4[:, 0, :, :], sinTb, ALU.mult, r=[pq, sinT_s], w=[ra])
                p.tt("vector", Q4[:, 1, :, :], rav[:, :, 0:128], rav[:, :, 128:256], ALU.add,
                     r=[ra], w=[qfm4[jj][hh]])
            if scanstop <= 13:
                continue
            for b in range(2):
                cs = slice(b * TC + tc * 128, b * TC + (tc + 1) * 128)
                pw = psf[2 + b]
                for hh in range(2):
                    h = hp * 2 + hh
                    p.mm(pw[0:64, hh * 128:(hh + 1) * 128], skw2[:, h * 64:(h + 1) * 64],
                         t1k[:, cs], start=(hh == 0), stop=False, r=[skw2, t1k], w=[pw])
                    p.mm(pw[0:64, 256 + hh * 128:256 + (hh + 1) * 128], svw2[:, h * 64:(h + 1) * 64],
                         t1v[:, cs], start=False, stop=(hh == 1), r=[svw2, t1v], w=[pw])
                wtk = rt4[b % 2]
                for hh in range(2):
                    h = hp * 2 + hh
                    p.ts2("vector", wtk[0:64, hh * 128:(hh + 1) * 128],
                          pw[0:64, hh * 128:(hh + 1) * 128], skb2c[0:64, h:h + 1], 32.0,
                          ALU.add, ALU.min, r=[pw, skb2c], w=[wtk])
                    p.ts2("vector", wtk[0:64, 256 + hh * 128:256 + (hh + 1) * 128],
                          pw[0:64, 256 + hh * 128:256 + (hh + 1) * 128], svb2c[0:64, h:h + 1], 32.0,
                          ALU.add, ALU.min, r=[pw, svb2c], w=[wtk])
                p.tss("vector", wtk[0:64, 0:512], wtk[0:64, 0:512], -32.0, ALU.max, r=[wtk], w=[wtk])
                for hh in range(2):
                    s2 = b * 2 + hh
                    p.act(wkf4[0][s2][0:64, :], wtk[0:64, hh * 128:(hh + 1) * 128], AF.Exp,
                          r=[wtk], w=[wkf4[0][s2]])
                    p.act(wvf4[0][s2][0:64, :], wtk[0:64, 256 + hh * 128:256 + (hh + 1) * 128],
                          AF.Exp, r=[wtk], w=[wvf4[0][s2]])
                zk_ = zkb[0][b]
                zv_ = zvb[0][b]
                for hh in range(2):
                    s2 = b * 2 + hh
                    p.scan_cumsum(zk_[hh * 64:(hh + 1) * 64, :], wkf4[0][s2][0:64, :],
                                  zkc4[s2][0:64, 0:1], r=[wkf4[0][s2], zkc4[s2]], w=[zk_])
                    p.cp("vector", zkc4[s2][0:64, :], zk_[hh * 64:(hh + 1) * 64, 127:128],
                         r=[zk_], w=[zkc4[s2]])
                    p.scan_cumsum(zv_[hh * 64:(hh + 1) * 64, :], wvf4[0][s2][0:64, :],
                                  zvc4[s2][0:64, 0:1], r=[wvf4[0][s2], zvc4[s2]], w=[zv_])
                    p.cp("vector", zvc4[s2][0:64, :], zv_[hh * 64:(hh + 1) * 64, 127:128],
                         r=[zv_], w=[zvc4[s2]])
                p.recip(zk_[:, :], zk_[:, :], r=[zk_], w=[zk_])
                p.recip(zv_[:, :], zv_[:, :], r=[zv_], w=[zv_])
                prz = psf[2 + b]
                p.tr(prz[:, 0:128], zk_[:, :], identf[:, :], r=[zk_, identf], w=[prz])
                p.cp("vector", rzT2[0][b][:, 0:128], prz[:, 0:128], r=[prz], w=[rzT2[0][b]])
                p.tr(prz[:, 128:256], zv_[:, :], identf[:, :], r=[zv_, identf], w=[prz])
                p.cp("vector", rzT2[0][b][:, 128:256], prz[:, 128:256], r=[prz], w=[rzT2[0][b]])
            if scanstop <= 14:
                continue
            for b in range(2):
                pt = PBA if b == 0 else PBB
                for hh in range(2):
                    s2 = b * 2 + hh
                    p.trp(pt[:, hh * 64:(hh + 1) * 64], wkf4[0][s2][0:64, :], identb[0:64, 0:64],
                          start=(hh == 0), stop=False, r=[wkf4[0][s2], identb], w=[pt])
                    p.trp(pt[:, 128 + hh * 64:128 + (hh + 1) * 64], wvf4[0][s2][0:64, :],
                          identb[0:64, 0:64], start=False, stop=(hh == 1),
                          r=[wvf4[0][s2], identb], w=[pt])
                eng_b = "vector" if b == 0 else "scalar"
                p.cp(eng_b, wktm2[0][b][:, :], pt[:, 0:128], r=[pt], w=[wktm2[0][b]])
                p.cp(eng_b, wvtm2[0][b][:, :], pt[:, 128:256], r=[pt], w=[wvtm2[0][b]])
            if scanstop <= 15:
                continue
            for b in range(2):
                pt = PBA if b == 0 else PBB
                for c4 in range(4):
                    p.trp(pt[:, c4 * 128:(c4 + 1) * 128], ktm2[0][b][:, c4 * 128:(c4 + 1) * 128],
                          identb[:, :], start=(c4 == 0), stop=(c4 == 3),
                          r=[ktm2[0][b], identb], w=[pt])
                p.cp("vector", kfm4[jj][b][:, :], pt[:, 0:512], r=[pt], w=[kfm4[jj][b]])

            def S(s2):
                return s2 // 2, s2 % 2
            if scanstop <= 1:
                continue
            for s2 in range(4):
                b, hh = S(s2)
                pst = psf[s2]
                for dt_ in range(2):
                    p.mm(pst[:, 0:128], kfm4[jj][b][:, hh * 256 + dt_ * 128:hh * 256 + (dt_ + 1) * 128],
                         qfm4[jj][hh][:, dt_ * 256 + b * 128:dt_ * 256 + (b + 1) * 128],
                         start=(dt_ == 0), stop=(dt_ == 1), r=[kfm4[jj][b], qfm4[jj][hh]], w=[pst])
            for s2 in range(4):
                p.tt("vector", st4[s2][:, :], psf[s2][:, 0:128], utmask[:, :], ALU.mult,
                     r=[psf[s2], utmask], w=[st4[s2]])
            for s2 in range(4):
                b, hh = S(s2)
                plg = psf[4 + s2 % 2]
                off = (s2 // 2) * 128
                for dt_ in range(2):
                    p.mm(plg[:, off + 0:off + 64],
                         qfm4[jj][hh][:, dt_ * 256 + b * 128:dt_ * 256 + (b + 1) * 128],
                         KstB4[s2][:, dt_ * 64:(dt_ + 1) * 64],
                         start=(dt_ == 0 and s2 < 2), stop=False,
                         r=[qfm4[jj][hh], KstB4[s2]], w=[plg])
                p.mm(plg[:, off + 0:off + 64], st4[s2][:, :],
                     wktm2[0][b][:, hh * 64:(hh + 1) * 64],
                     start=False, stop=(s2 >= 2), r=[st4[s2], wktm2[0][b]], w=[plg])
            for s2 in range(4):
                b, hh = S(s2)
                plg = psf[4 + s2 % 2]
                off = (s2 // 2) * 128
                p.stt("vector", lg4[s2][:, :], plg[:, off:off + 64], SCALE,
                      rzT2[0][b][:, hh * 64:(hh + 1) * 64], ALU.mult, ALU.mult,
                      r=[plg, rzT2[0][b]], w=[lg4[s2]])
            for s2 in range(4):
                b, hh = S(s2)
                p.red("vector", nmx4[s2][:, :], lg4[s2][:, :], ALU.max, negate=True,
                      r=[lg4[s2]], w=[nmx4[s2]])
                p.act(et4[s2][:, :], lg4[s2][:, :], AF.Exp, bias=nmx4[s2][0:128, 0:1],
                      accum_ap=sume4[s2][:, :], r=[lg4[s2], nmx4[s2]], w=[et4[s2], sume4[s2]])
                p.recip(sume4[s2][:, :], sume4[s2][:, :], r=[sume4[s2]], w=[sume4[s2]])
                p.stt("vector", pttm4[s2][:, :], et4[s2][:, :], sume4[s2][0:128, 0:1],
                      rzT2[0][b][:, 128 + hh * 64:128 + (hh + 1) * 64], ALU.mult, ALU.mult,
                      r=[et4[s2], sume4[s2], rzT2[0][b]], w=[pttm4[s2]])
            for b in range(2):
                pt = PBA if b == 0 else PBB
                for hh in range(2):
                    s2 = b * 2 + hh
                    p.trp(pt[0:64, hh * 128:(hh + 1) * 128], pttm4[s2][:, :], identb[:, :],
                          start=(hh == 0), stop=(hh == 1), r=[pttm4[s2], identb], w=[pt])
                for hh in range(2):
                    s2 = b * 2 + hh
                    p.cp("vector" if b == 0 else "scalar", ptfm4[s2][:, :],
                         pt[0:64, hh * 128:(hh + 1) * 128], r=[pt], w=[ptfm4[s2]])
            if scanstop <= 2:
                continue
            for s2 in range(4):
                b, hh = S(s2)
                pat = psf[s2]
                p.mm(pat[:, 0:128], wvf4[0][s2][0:64, :], ptfm4[s2][:, :],
                     start=True, stop=True, r=[wvf4[0][s2], ptfm4[s2]], w=[pat])
            for s2 in range(4):
                p.tt("vector", at4[s2][:, :], psf[s2][:, 0:128], utmask[:, :], ALU.mult,
                     r=[psf[s2], utmask], w=[at4[s2]])
            for s2 in range(4):
                b, hh = S(s2)
                po = psf[4 + s2 % 2]
                off = (s2 // 2) * 256
                p.mm(po[:, off:off + 256], ptfm4[s2][:, :], VstB4[s2][:, :],
                     start=(s2 < 2), stop=False, r=[ptfm4[s2], VstB4[s2]], w=[po])
                p.mm(po[:, off:off + 256], at4[s2][:, :],
                     vtm2[0][b].ap[:, hh * 256:(hh + 1) * 256], start=False, stop=(s2 >= 2),
                     r=[at4[s2], vtm2[0][b]], w=[po])
            for s2 in range(4):
                b, hh = S(s2)
                po = psf[4 + s2 % 2]
                off = (s2 // 2) * 256
                p.act(sqs4[0][:, :], po[:, off:off + 256], AF.Square,
                      accum_ap=ssq4[s2][:, :], r=[po], w=[sqs4[0], ssq4[s2]])
                p.act(rstd4[s2][:, :], ssq4[s2][:, :], AF.Sqrt, bias=EPS, scale=1.0 / 256.0,
                      r=[ssq4[s2]], w=[rstd4[s2]])
                p.recip(rstd4[s2][:, :], rstd4[s2][:, :], r=[rstd4[s2]], w=[rstd4[s2]])
                p.stt("vector", ogtm4[s2][:, :], po[:, off:off + 256], rstd4[s2][0:128, 0:1],
                      gsil2[0][b].ap[:, hh * 256:(hh + 1) * 256], ALU.mult, ALU.mult,
                      r=[po, rstd4[s2], gsil2[0][b]], w=[ogtm4[s2]])
            for b in range(2):
                pt = PBA if b == 0 else PBB
                for hh in range(2):
                    s2 = b * 2 + hh
                    for dt_ in range(2):
                        p.trp(pt[:, hh * 256 + dt_ * 128:hh * 256 + (dt_ + 1) * 128],
                              ogtm4[s2][:, dt_ * 128:(dt_ + 1) * 128], identb[:, :],
                              start=(hh == 0 and dt_ == 0), stop=(hh == 1 and dt_ == 1),
                              r=[ogtm4[s2], identb], w=[pt])
                for hh in range(2):
                    s2 = b * 2 + hh
                    for dt_ in range(2):
                        p.cp("vector" if b == 0 else "scalar",
                             og4[b * 4 + hh * 2 + dt_][:, (tc % 2) * 128:(tc % 2 + 1) * 128],
                             pt[:, hh * 256 + dt_ * 128:hh * 256 + (dt_ + 1) * 128],
                             r=[pt], w=[og4[b * 4 + hh * 2 + dt_]])
            if scanstop <= 3:
                continue
            for s2 in range(4):
                b, hh = S(s2)
                pk4 = psf[s2]
                p.mm(pk4[:, 0:64], ktm2[0][b].ap[:, hh * 256:hh * 256 + 128],
                     wktm2[0][b][:, hh * 64:(hh + 1) * 64],
                     start=True, stop=False, r=[ktm2[0][b], wktm2[0][b]], w=[pk4])
                p.mm(pk4[:, 64:128], ktm2[0][b].ap[:, hh * 256 + 128:hh * 256 + 256],
                     wktm2[0][b][:, hh * 64:(hh + 1) * 64],
                     start=False, stop=True, r=[ktm2[0][b], wktm2[0][b]], w=[pk4])
            for s2 in range(4):
                p.tt("vector", KstF4[s2][:, :], KstF4[s2][:, :], psf[s2][:, 0:128], ALU.add,
                     r=[KstF4[s2], psf[s2]], w=[KstF4[s2]])
                p.cp("vector", KstB4[s2][:, :], KstF4[s2][:, :], r=[KstF4[s2]], w=[KstB4[s2]])
            for s2 in range(4):
                b, hh = S(s2)
                pv5 = psf[4 + s2 % 2]
                off = (s2 // 2) * 256
                p.mm(pv5[0:64, off:off + 256], wvtm2[0][b][:, hh * 64:(hh + 1) * 64],
                     vtm2[0][b].ap[:, hh * 256:(hh + 1) * 256],
                     start=(s2 < 2), stop=(s2 >= 2), r=[wvtm2[0][b], vtm2[0][b]], w=[pv5])
            for s2 in range(4):
                pv5 = psf[4 + s2 % 2]
                off = (s2 // 2) * 256
                p.tt("vector", VstF4[s2][:, :], VstF4[s2][:, :], pv5[0:64, off:off + 256], ALU.add,
                     r=[VstF4[s2], pv5], w=[VstF4[s2]])
                p.cp("vector", VstB4[s2][:, :], VstF4[s2][:, :], r=[VstF4[s2]], w=[VstB4[s2]])
            if tc % 2 == 1 and scanstop > 4:
                for b in range(2):
                    for dt_ in range(8):
                        pw = psf[(dt_) % 4]
                        for kk in range(4):
                            p.mm(pw[:, 0:256], wo_t[kk][:, dt_ * 128:(dt_ + 1) * 128],
                                 og4[b * 4 + kk][:, 0:256],
                                 start=(kk == 0), stop=(kk == 3), r=[wo_t[kk], og4[b * 4 + kk]], w=[pw])
                        ms = slice(b * TC + (tc - 1) * 128, b * TC + (tc + 1) * 128)
                        p.tt("vector", acc[dt_][:, ms], acc[dt_][:, ms], pw[:, 0:256], ALU.add,
                             r=[acc[dt_], pw], w=[acc[dt_]])
    p.pop_scope()
    p.pop_scope()
    p.pop_scope()
    dump_fm("x1", acc)
    if upto <= 6:
        return finish(p, nc, outs, inp)

    layernorm_fm(acc, xn, "ln1")
    dump_fm("xn1", xn)

    p.push_scope()
    def bias_cols(name, tag):
        br = p.sb(f"br_{tag}", [8, 128], F32)
        p.dma("sync", br[:, :], inp[name].rearrange("(a b) -> a b", b=128), w=[br])
        bc = p.sb(f"bc_{tag}", [128, 8], F32)
        p.tr(psf[5][0:128, 0:8], br[:, :], identf[0:8, 0:8], r=[br, identf], w=[psf[5]])
        p.cp("vector", bc[:, :], psf[5][0:128, 0:8], r=[psf[5]], w=[bc])
        return bc

    garowfm = p.sb("garowfm", [2, 16], F32)
    ga_in = inp["L1_attn_grep_a"].rearrange("a b c d -> (a c d) b")
    for t in range(2):
        dstv = garowfm.ap[t:t + 1, :].rearrange("a (h two) -> a h two", two=2)
        for j in range(2):
            p.dma("gpsimd", dstv[:, :, j:j + 1].squeeze(2), ga_in[0:1, t * 8:(t + 1) * 8],
                  w=[garowfm], slow=True)
    BUCKETS = [[0, 17, 18], [1, 0, 17]]
    pbrow6 = p.sb("pbrow6", [6, 16], F32)
    for js in range(3):
        for t in range(2):
            r_ = js * 2 + t
            dstv = pbrow6.ap[r_:r_ + 1, :].rearrange("a (h two) -> a h two", two=2)
            for parity in range(2):
                bk = BUCKETS[parity][js]
                p.dma("gpsimd", dstv[:, :, parity:parity + 1].squeeze(2),
                      inp["L1_attn_rel_emb"][bk:bk + 1, t * 8:(t + 1) * 8],
                      w=[pbrow6], slow=True)

    qbc = bias_cols("L1_attn_q_b", "qb")
    kbc = bias_cols("L1_attn_k_b", "kb")
    vbc = bias_cols("L1_attn_v_b", "vb")
    obc = bias_cols("L1_attn_out_b", "ob")

    dupm = p.sb("dupm_s", [128, 256], BF16)
    selm = p.sb("selm_s", [128, 512], BF16)
    segm = p.sb("segm_s", [128, 128], BF16)
    gmaskm = p.sb("gmaskm_s", [128, 128], F32)
    bcP = p.sb("bcP", [16, 2112], BF16)
    p.dma("sync", dupm[:, :], inp["dupm"], w=[dupm])
    p.dma("sync", selm[:, :], inp["selm"], w=[selm])
    p.dma("sync", segm[:, :], inp["segm"], w=[segm])
    p.dma("sync", gmaskm[:, :], inp["gmaskm"], w=[gmaskm])
    p.dma("sync", bcP[:, :], inp["bcastP"], w=[bcP])

    def bias_fm(name, tag):
        b64 = p.sb(f"b64_{tag}", [16, 64], F32)
        p.dma("sync", b64[:, :], inp[name].rearrange("a b c -> (a b) c"), w=[b64])
        p.tr(psf[5][0:64, 0:16], b64[:, :], identf[0:16, 0:16], r=[b64, identf], w=[psf[5]])
        bT = p.sb(f"bT_{tag}", [64, 16], BF16)
        p.cp("vector", bT[:, :], psf[5][0:64, 0:16], r=[psf[5]], w=[bT])
        p.mm(psf[4][0:128, 0:16], dupm[0:64, 0:128], bT[:, :], start=True, stop=True,
             r=[dupm, bT], w=[psf[4]])
        bfm = p.sb(f"bfm_{tag}", [128, 16], F32)
        p.cp("vector", bfm[:, :], psf[4][0:128, 0:16], r=[psf[4]], w=[bfm])
        return bfm

    kbias_fm = bias_fm("L1_attn_bias_k", "kb")
    vbias_fm = bias_fm("L1_attn_bias_v", "vb")

    gwsb = p.sb("gwsb", [64, 8], F32)
    p.dma("sync", gwsb[:, :], inp["L1_attn_grep_w"], w=[gwsb])
    gws = p.sb("gws", [64, 1], F32)
    p.red("vector", gws[:, :], gwsb[:, :], ALU.add, r=[gwsb], w=[gws])
    gws16 = p.sb("gws16", [64, 1], BF16)
    p.cp("vector", gws16[:, :], gws[:, :], r=[gws], w=[gws16])
    p.mm(psf[4][0:128, 0:1], dupm[0:64, 0:128], gws16[:, :], start=True, stop=True,
         r=[dupm, gws16], w=[psf[4]])
    gws2 = p.sb("gws2", [128, 1], F32)
    p.cp("vector", gws2[:, :], psf[4][0:128, 0:1], r=[psf[4]], w=[gws2])
    gsegs = p.sb("gsegs", [128, 128], BF16)
    p.tss("vector", gsegs[:, :], gmaskm[:, :], gws2[0:128, 0:1], ALU.mult,
          r=[gmaskm, gws2], w=[gsegs])
    gbr = p.sb("gbr", [1, 8], F32)
    p.dma("sync", gbr[:, :], inp["L1_attn_grep_b"].rearrange("(a b) -> a b", a=1), w=[gbr])
    gbs = p.sb("gbs", [1, 1], F32)
    p.red("vector", gbs[:, :], gbr[:, :], ALU.add, r=[gbr], w=[gbs])
    gb_col = p.sb("gb_col", [128, 1], F32)
    p.mm(psf[5][:, 0:1], onesf[0:1, :], gbs[0:1, :], start=True, stop=True, r=[onesf, gbs], w=[psf[5]])
    p.cp("vector", gb_col[:, :], psf[5][:, 0:1], r=[psf[5]], w=[gb_col])

    ga2 = p.sb("ga2", [16, 2], F32)
    pb_fm = p.sb("pb_fm", [16, 6], F32)

    def a1_gapb_setup():
        p.tr(psf[5][0:16, 0:2], garowfm[:, :], identf[0:2, 0:2], r=[garowfm, identf], w=[psf[5]])
        p.cp("vector", ga2[:, :], psf[5][0:16, 0:2], r=[psf[5]], w=[ga2])
        p.tr(psf[5][0:16, 0:6], pbrow6[:, :], identf[0:6, 0:6], r=[pbrow6, identf], w=[psf[5]])
        p.cp("vector", pb_fm[:, :], psf[5][0:16, 0:6], r=[psf[5]], w=[pb_fm])

    wq1 = [p.sb(f"wq1_{k}", [128, D], BF16) for k in range(8)]
    wk1 = [p.sb(f"wk1_{k}", [128, D], BF16) for k in range(8)]
    wv1 = [p.sb(f"wv1_{k}", [128, D], BF16) for k in range(8)]
    outw = [p.sb(f"outw{k}", [128, D], BF16) for k in range(8)]
    for k in range(8):
        p.dma("gpsimd", wq1[k][:, :], inp["L1_attn_q_w"][k * 128:(k + 1) * 128, :], w=[wq1[k]])
        p.dma("gpsimd", wk1[k][:, :], inp["L1_attn_k_w"][k * 128:(k + 1) * 128, :], w=[wk1[k]])
        p.dma("gpsimd", wv1[k][:, :], inp["L1_attn_v_w"][k * 128:(k + 1) * 128, :], w=[wv1[k]])
        p.dma("gpsimd", outw[k][:, :], inp["L1_attn_out_w"][k * 128:(k + 1) * 128, :], w=[outw[k]])

    qt = [p.sb(f"a1qt{ft}", [128, 256], BF16) for ft in range(8)]
    kt = [p.sb(f"a1kt{ft}", [128, 256], BF16) for ft in range(8)]
    vt = [[p.sb(f"a1vt{pr_}_{ft}", [128, 256], BF16) for ft in range(8)] for pr_ in range(2)]
    prodk = [p.sb(f"a1pk{i}", [128, 384], BF16) for i in range(3)]
    attfm = p.sb("a1attfm", [128, 2048], BF16)
    etmpB = [p.sb(f"a1etmp{i}", [128, 256], BF16) for i in range(2)]
    vselsb = [p.sb(f"a1vsel{i}", [128, 128], BF16) for i in range(2)]
    lg3 = p.sb("a1lg3", [16, 768], F32)
    glsb = p.sb("a1glsb", [16, 256], F32)
    mx2 = p.sb("a1mx2", [16, 256], F32)
    pj3 = [p.sb(f"a1pj{i}", [16, 768], BF16) for i in range(2)]

    def xn_b2(st_i):
        return [xn[k].ap.rearrange("p (b t) -> p b t", b=2)[:, :, st_i * 128:(st_i + 1) * 128]
                for k in range(8)]

    def a1_proj(wt, bc, dst, st_i):
        rhsv = xn_b2(st_i)
        for ft in range(8):
            pp = psf[ft % 2]
            for k in range(8):
                p.mm(pp[:, 0:256], wt[k][:, ft * 128:(ft + 1) * 128], rhsv[k],
                     start=(k == 0), stop=(k == 7), r=[wt[k], xn[k]], w=[pp])
            p.act(dst[ft][:, :], pp[:, 0:256], AF.Identity, bias=bc[0:128, ft:ft + 1],
                  r=[pp, bc], w=[dst[ft]])

    def a1_kdup(t, ft):
        hp = t * 8 + ft
        idx = t * 8 + ft
        bank = psf[2 + idx % 2]
        ndata = max(0, min(3, 32 - 3 * hp))
        eng = "vector"
        for js in range(ndata):
            ck = 3 * hp + js
            tk, hk = ck // 16, ck % 16
            p.mm(bank[:, js * 128:(js + 1) * 128],
                 dupm[:, (hk % 2) * 128:(hk % 2 + 1) * 128],
                 kt[hk // 2][:, tk * 128:(tk + 1) * 128],
                 start=(js == 0), stop=(js == ndata - 1), r=[dupm, kt[hk // 2]], w=[bank])
        q3 = qt[ft].ap[:, t * 128:(t + 1) * 128]
        pkb = prodk[(t * 8 + ft) % 3]
        pk3 = pkb.ap[:, 0:384].rearrange("p (j s) -> p j s", s=128)
        if ndata == 3:
            p.tt(eng, pk3, q3.unsqueeze(1).to_broadcast([128, 3, 128]),
                 bank.ap[:, 0:384].rearrange("p (j s) -> p j s", s=128),
                 ALU.mult, r=[qt[ft], bank], w=[pkb])
        elif ndata == 0:
            cb0 = 3 * hp - 32
            p.tt(eng, pk3, q3.unsqueeze(1).to_broadcast([128, 3, 128]),
                 kbias_fm.ap[:, cb0:cb0 + 3].unsqueeze(2).to_broadcast([128, 3, 128]),
                 ALU.mult, r=[qt[ft], kbias_fm], w=[pkb])
        else:
            p.tt(eng, pkb.ap[:, 0:256].rearrange("p (j s) -> p j s", s=128),
                 q3.unsqueeze(1).to_broadcast([128, 2, 128]),
                 bank.ap[:, 0:256].rearrange("p (j s) -> p j s", s=128),
                 ALU.mult, r=[qt[ft], bank], w=[pkb])
            cb = 3 * hp + 2 - 32
            p.tss(eng, pkb[:, 256:384], q3, kbias_fm[0:128, cb:cb + 1],
                  ALU.mult, r=[qt[ft], kbias_fm], w=[pkb])

    def a1_seg(t, ft):
        st_g = (t == 0 and ft == 0)
        sp_g = (t == 1 and ft == 7)
        for js in range(3):
            if js == 0:
                area, c0 = psf[4], t * 128
            elif js == 1:
                area, c0 = psf[5], t * 128
            else:
                area, c0 = psf[0], 256 + t * 128
            pkb = prodk[(t * 8 + ft) % 3]
            p.mm(area[0:16, c0:c0 + 128], segm[:, ft * 16:(ft + 1) * 16],
                 pkb[:, js * 128:(js + 1) * 128],
                 start=st_g, stop=sp_g, r=[segm, pkb], w=[area])
        p.mm(psf[1][0:16, 256 + t * 128:256 + (t + 1) * 128],
             gsegs[:, ft * 16:(ft + 1) * 16], qt[ft][:, t * 128:(t + 1) * 128],
             start=st_g, stop=sp_g, r=[gsegs, qt[ft]], w=[psf[1]])

    def a1_projall(st_i):
        a1_proj(wq1, qbc, qt, st_i)
        a1_proj(wk1, kbc, kt, st_i)
        a1_proj(wv1, vbc, vt[st_i % 2], st_i)

    def a1_logits(st_i):
        for t in range(2):
            a1_kdup(t, 0)
            for ft in range(1, 8):
                a1_kdup(t, ft)
                a1_seg(t, ft - 1)
            a1_seg(t, 7)

    def a1_softmax(st_i):
        par = st_i % 2
        p.cp("vector", lg3[:, 0:256], psf[4][0:16, 0:256], r=[psf[4]], w=[lg3])
        p.cp("vector", lg3[:, 256:512], psf[5][0:16, 0:256], r=[psf[5]], w=[lg3])
        p.cp("vector", lg3[:, 512:768], psf[0][0:16, 256:512], r=[psf[0]], w=[lg3])
        p.cp("scalar", glsb[:, :], psf[1][0:16, 256:512], r=[psf[1]], w=[glsb])
        p.act(glsb[:, :], glsb[:, :], AF.Sigmoid, bias=gb_col[0:16, 0:1],
              r=[glsb, gb_col], w=[glsb])
        for t in range(2):
            p.tss("vector", glsb[:, t * 128:(t + 1) * 128], glsb[:, t * 128:(t + 1) * 128],
                  ga2[0:16, t:t + 1], ALU.mult, r=[glsb, ga2], w=[glsb])
        for js in range(3):
            for t in range(2):
                sl = slice(js * 256 + t * 128, js * 256 + (t + 1) * 128)
                p.stt("vector", lg3.ap[:, sl], glsb[:, t * 128:(t + 1) * 128],
                      pb_fm[0:16, js * 2 + t:js * 2 + t + 1], lg3.ap[:, sl],
                      ALU.mult, ALU.add, r=[glsb, pb_fm, lg3], w=[lg3])
        p.tt("vector", mx2[:, :], lg3[:, 0:256], lg3[:, 256:512], ALU.max, r=[lg3], w=[mx2])
        p.tt("vector", mx2[:, :], mx2[:, :], lg3[:, 512:768], ALU.max, r=[lg3, mx2], w=[mx2])
        for js in range(3):
            p.tt("vector", lg3[:, js * 256:(js + 1) * 256], lg3[:, js * 256:(js + 1) * 256],
                 mx2[:, :], ALU.subtract, r=[lg3, mx2], w=[lg3])
        p.act(lg3[:, 0:768], lg3[:, 0:768], AF.Exp, r=[lg3], w=[lg3])
        p.tt("vector", mx2[:, :], lg3[:, 0:256], lg3[:, 256:512], ALU.add, r=[lg3], w=[mx2])
        p.tt("vector", mx2[:, :], mx2[:, :], lg3[:, 512:768], ALU.add, r=[lg3, mx2], w=[mx2])
        p.recip(mx2[:, :], mx2[:, :], r=[mx2], w=[mx2])
        for js in range(3):
            p.tt("vector", pj3[par][:, js * 256:(js + 1) * 256], lg3[:, js * 256:(js + 1) * 256],
                 mx2[:, :], ALU.mult, r=[lg3, mx2], w=[pj3[par]])

    def a1_passB(st_i):
        par = st_i % 2
        vts, pj = vt[par], pj3[par]
        for js in range(3):
            for fo in range(8):
                bank = psf[2 + fo % 2]
                eng = "vector"
                vss = vselsb[fo % 2]
                data = []
                for hh in range(2):
                    ck = 3 * (2 * fo + hh) + js
                    if ck < 32:
                        data.append((hh, ck % 16, ck // 16))
                nmm = 4 + len(data)
                i_mm = 0
                for jt in range(2):
                    for hh in range(2):
                        m = 4 * fo + 2 * hh + jt
                        t, hq = m // 16, m % 16
                        o_ = 128 * hq - 64 * hh + 64
                        p.mm(bank[:, jt * 128:(jt + 1) * 128],
                             bcP[:, o_:o_ + 128],
                             pj[0:16, js * 256 + t * 128:js * 256 + (t + 1) * 128],
                             start=(i_mm == 0), stop=(i_mm == nmm - 1), r=[bcP, pj], w=[bank])
                        i_mm += 1
                vc0 = 256 + (js % 2) * 128
                for (hh, hk, tk) in data:
                    p.mm(bank[:, vc0:vc0 + 128],
                         selm[:, (hh * 2 + hk % 2) * 128:(hh * 2 + hk % 2 + 1) * 128],
                         vts[hk // 2][:, tk * 128:(tk + 1) * 128],
                         start=(i_mm == 0), stop=(i_mm == nmm - 1),
                         r=[selm, vts[hk // 2]], w=[bank])
                    i_mm += 1
                if data:
                    p.cp(eng, vss[:, :], bank[:, vc0:vc0 + 128], r=[bank], w=[vss])
                tbuf = attfm if js == 0 else etmpB[fo % 2]
                toff = fo * 256 if js == 0 else 0
                if a1stop < 6:
                    continue
                if len(data) == 2:
                    dst = tbuf.ap[:, toff:toff + 256].rearrange("p (j s) -> p j s", s=128)
                    p.tt(eng, dst, bank.ap[:, 0:256].rearrange("p (j s) -> p j s", s=128),
                         vss.ap[:, :].unsqueeze(1).to_broadcast([128, 2, 128]),
                         ALU.mult, r=[bank, vss], w=[tbuf])
                else:
                    for hh in range(2):
                        lo, hi = hh * 64, (hh + 1) * 64
                        dst = tbuf.ap[lo:hi, toff:toff + 256].rearrange("p (j s) -> p j s", s=128)
                        src0 = bank.ap[lo:hi, 0:256].rearrange("p (j s) -> p j s", s=128)
                        if data and data[0][0] == hh:
                            in1 = vss.ap[lo:hi, :].unsqueeze(1).to_broadcast([64, 2, 128])
                            p.tt(eng, dst, src0, in1, ALU.mult, r=[bank, vss], w=[tbuf])
                        else:
                            cb = 3 * (2 * fo + hh) + js - 32
                            in1 = vbias_fm.ap[lo:hi, cb:cb + 1].unsqueeze(2).to_broadcast(
                                [64, 2, 128])
                            p.tt(eng, dst, src0, in1, ALU.mult, r=[bank, vbias_fm], w=[tbuf])
                if js > 0:
                    p.tt("gpsimd", attfm[:, fo * 256:(fo + 1) * 256],
                         attfm[:, fo * 256:(fo + 1) * 256], etmpB[fo % 2][:, 0:256],
                         ALU.add, r=[etmpB[fo % 2], attfm], w=[attfm])
        if a1stop < 7:
            return
        for dt_ in range(8):
            pp = psf[4 + dt_ % 2]
            c0 = 0
            for fo in range(8):
                p.mm(pp[:, c0:c0 + 256], outw[fo][:, dt_ * 128:(dt_ + 1) * 128],
                     attfm[:, fo * 256:(fo + 1) * 256],
                     start=(fo == 0), stop=(fo == 7), r=[outw[fo], attfm], w=[pp])
            accv = acc[dt_].ap.rearrange("p (b t) -> p b t", b=2)[:, :, st_i * 128:(st_i + 1) * 128]
            ppv = pp.ap[:, c0:c0 + 256].rearrange("p (b t) -> p b t", b=2)
            p.stt("vector", accv, ppv, obc[0:128, dt_:dt_ + 1],
                  accv, ALU.add, ALU.add, r=[pp, obc, acc[dt_]], w=[acc[dt_]])

    if a1stop >= 2:
        for st_i in range(8):
            a1_projall(st_i)
            if st_i == 0:
                a1_gapb_setup()
            if a1stop >= 5 and st_i > 0:
                a1_passB(st_i - 1)
            if a1stop >= 3:
                a1_logits(st_i)
            if a1stop >= 4:
                a1_softmax(st_i)
        if a1stop >= 5:
            a1_passB(7)
    p.pop_scope()
    dump_fm("attn1", acc)
    if upto <= 8:
        return finish(p, nc, outs, inp)

    p.push_scope()
    msc_fm = p.sb("msc_fm", [16, NT], BF16)
    p.push_scope()
    gwsb2 = p.sb("gwsb2", [8, D], BF16)
    p.dma("gpsimd", gwsb2[:, :], inp["L1_ffn_gate_w"], w=[gwsb2])
    gwT = p.sb("gwT", [128, 64], BF16)
    for k in range(8):
        pt = PBA
        p.tr(pt[0:128, 0:8], gwsb2[0:8, k * 128:(k + 1) * 128], identb[0:8, 0:8],
             r=[gwsb2, identb], w=[pt])
        p.cp("vector", gwT[:, k * 8:(k + 1) * 8], pt[0:128, 0:8], r=[pt], w=[gwT])
    lgm = p.sb("lgm", [128, 8], F32)
    scm = p.sb("scm", [128, 8], F32)
    mx1 = p.sb("mx1", [128, 1], F32)
    m1m = p.sb("m1m", [128, 8], F32)
    tm8 = p.sb("tm8", [128, 8], F32)
    msc = p.sb("msc", [128, 16], F32)
    for tt in range(16):
        cs = slice(tt * 128, (tt + 1) * 128)
        pp = psf[0]
        for k in range(8):
            p.mm(pp[:, 0:8], xn[k][:, cs], gwT[:, k * 8:(k + 1) * 8],
                 start=(k == 0), stop=(k == 7), r=[xn[k], gwT], w=[pp])
        p.cp("vector", lgm[:, :], pp[:, 0:8], r=[pp], w=[lgm])
        p.red("vector", mx1[:, :], lgm[:, :], ALU.max, negate=True, r=[lgm], w=[mx1])
        p.act(scm[:, :], lgm[:, :], AF.Exp, bias=mx1[0:128, 0:1], r=[lgm, mx1], w=[scm])
        p.red("vector", mx1[:, :], scm[:, :], ALU.add, r=[scm], w=[mx1])
        p.recip(mx1[:, :], mx1[:, :], r=[mx1], w=[mx1])
        p.tss("vector", scm[:, :], scm[:, :], mx1[0:128, 0:1], ALU.mult, r=[scm, mx1], w=[scm])
        p.red("vector", mx1[:, :], scm[:, :], ALU.max, r=[scm], w=[mx1])
        p.tss("vector", m1m[:, :], scm[:, :], mx1[0:128, 0:1], ALU.is_ge, r=[scm, mx1], w=[m1m])
        p.stt("vector", tm8[:, :], m1m[:, :], -1e30, scm[:, :], ALU.mult, ALU.add,
              r=[m1m, scm], w=[tm8])
        p.red("vector", mx1[:, :], tm8[:, :], ALU.max, r=[tm8], w=[mx1])
        p.tss("vector", msc[:, 8:16], scm[:, :], mx1[0:128, 0:1], ALU.is_ge, r=[scm, mx1], w=[msc])
        p.tt("vector", msc[:, 0:8], msc[:, 8:16], scm[:, :], ALU.mult, r=[scm, msc], w=[msc])
        p.tr(psf[3][0:16, 0:128], msc[:, :], identf[:, :], r=[msc, identf], w=[psf[3]])
        p.cp("vector", msc_fm[0:16, cs], psf[3][0:16, 0:128], r=[psf[3]], w=[msc_fm])
    p.pop_scope()
    if "moemsc" in dbg:
        p.dma("gpsimd", dbg["moemsc"], msc_fm[:, :], r=[msc_fm])
    mrow = p.sb("mrow", [1, NT], BF16)
    krow = p.sb("krow", [1, NT], BF16)
    m_bc = p.sb("m_bc", [128, NT], F32)
    k_bc = p.sb("k_bc", [128, NT], BF16)
    h1 = [p.sb(f"h1_{i}", [128, NT], BF16) for i in range(8)]
    x3m = [p.sb(f"x3m_{i}", [128, NT], BF16) for i in range(8)]
    wstr2 = [p.sb(f"wstr2_{j}", [128, D], BF16) for j in range(2)]
    w1T = [p.sb(f"w1T_{j}", [128, 8 * 128], BF16) for j in range(2)]
    bcol = p.sb("bcol", [8, 128], F32)
    b1c = p.sb("b1c", [128, 8], F32)
    b2c = p.sb("b2c", [128, 8], F32)
    b3c = p.sb("b3c", [128, 8], F32)
    etmps = [p.sb(f"etmp{i}", [128, 512], F32) for i in range(2)]

    def load_w1T(dst, wname, e, rows, j):
        wn = wstr2[j]
        p.dma("gpsimd", wn[:, :], inp[wname][e, rows:rows + 128, :], w=[wn])
        pt = PBA if j == 0 else PBB
        for k in range(8):
            p.trp(pt[:, k * 128:(k + 1) * 128], wn[:, k * 128:(k + 1) * 128], identb[:, :],
                  start=(k == 0), stop=(k == 7), r=[wn, identb], w=[pt])
        p.cp("vector" if j == 0 else "scalar", dst[:, 0:1024], pt[:, 0:1024], r=[pt], w=[dst])

    def load_bcol(dst, name, e):
        p.dma("sync", bcol[:, :], inp[name][e].rearrange("(a b) -> a b", b=128), w=[bcol])
        p.tr(psf[5][0:128, 0:8], bcol[:, :], identf[0:8, 0:8], r=[bcol, identf], w=[psf[5]])
        p.cp("vector", dst[:, :], psf[5][0:128, 0:8], r=[psf[5]], w=[dst])

    b2c2 = [b2c, p.sb("b2cB", [128, 8], F32)]

    def prep_expert(e):
        p.dma("gpsimd", mrow[0:1, :], msc_fm[e:e + 1, :], r=[msc_fm], w=[mrow])
        p.dma("gpsimd", krow[0:1, :], msc_fm[8 + e:9 + e, :], r=[msc_fm], w=[krow])
        for m2 in range(4):
            ms = slice(m2 * 512, (m2 + 1) * 512)
            pbx = psf[4]
            p.mm(pbx[:, 0:512], onesb[0:1, :], mrow[0:1, ms], start=True, stop=True,
                 r=[onesb, mrow], w=[pbx])
            p.cp("vector", m_bc[:, ms], pbx[:, 0:512], r=[pbx], w=[m_bc])
            pby = psf[5]
            p.mm(pby[:, 0:512], onesb[0:1, :], krow[0:1, ms], start=True, stop=True,
                 r=[onesb, krow], w=[pby])
            p.cp("scalar", k_bc[:, ms], pby[:, 0:512], r=[pby], w=[k_bc])
        load_bcol(b1c, "L1_ffn_b1", e)
        load_bcol(b2c2[e % 2], "L1_ffn_b2", e)
        load_bcol(b3c, "L1_ffn_b3", e)

    prep_expert(0)
    for e in range(8):
        m_bc_, k_bc_ = m_bc, k_bc
        b1c_, b2c_, b3c_ = b1c, b2c2[e % 2], b3c
        load_w1T(w1T[0], "L1_ffn_w1", e, 0, 0)
        for ht in range(8):
            for m2 in range(4):
                ms = slice(m2 * 512, (m2 + 1) * 512)
                pp = psf[0 + 3 * (m2 % 2)]
                for k in range(8):
                    p.mm(pp[:, 0:512], w1T[ht % 2][:, k * 128:(k + 1) * 128], xn[k][:, ms],
                         start=(k == 0), stop=(k == 7), r=[w1T[ht % 2], xn[k]], w=[pp])
                p.tt("vector", etmps[m2 % 2][:, 0:512], pp[:, 0:512], m_bc_[:, ms], ALU.mult,
                     r=[pp, m_bc_], w=[etmps[m2 % 2]])
                p.act(h1[ht][:, ms], etmps[m2 % 2][:, 0:512], AF.Silu, bias=b1c_[0:128, ht:ht + 1],
                      r=[etmps[m2 % 2], b1c_], w=[h1[ht]])
            if ht + 1 < 8:
                load_w1T(w1T[(ht + 1) % 2], "L1_ffn_w1", e, (ht + 1) * 128, (ht + 1) % 2)
        load_w1T(w1T[0], "L1_ffn_w3", e, 0, 0)
        for ht in range(8):
            for m2 in range(4):
                ms = slice(m2 * 512, (m2 + 1) * 512)
                pp = psf[1 + 3 * (m2 % 2)]
                for k in range(8):
                    p.mm(pp[:, 0:512], w1T[ht % 2][:, k * 128:(k + 1) * 128], xn[k][:, ms],
                         start=(k == 0), stop=(k == 7), r=[w1T[ht % 2], xn[k]], w=[pp])
                p.tt("vector", etmps[m2 % 2][:, 0:512], pp[:, 0:512], m_bc_[:, ms], ALU.mult,
                     r=[pp, m_bc_], w=[etmps[m2 % 2]])
                p.stt("vector", x3m[ht][:, ms], k_bc_[:, ms], b3c_[0:128, ht:ht + 1],
                      etmps[m2 % 2][:, 0:512], ALU.mult, ALU.add,
                      r=[k_bc_, b3c_, etmps[m2 % 2]], w=[x3m[ht]])
            if ht + 1 < 8:
                load_w1T(w1T[(ht + 1) % 2], "L1_ffn_w3", e, (ht + 1) * 128, (ht + 1) % 2)
        if e + 1 < 8:
            prep_expert(e + 1)
        load_w1T(w1T[0], "L1_ffn_w2", e, 0, 0)
        for dt_ in range(8):
            for m2 in range(4):
                ms = slice(m2 * 512, (m2 + 1) * 512)
                pp = psf[2 + 3 * (m2 % 2)]
                for k in range(8):
                    p.mm(pp[:, 0:512], w1T[dt_ % 2][:, k * 128:(k + 1) * 128], h1[k][:, ms],
                         start=(k == 0), stop=(k == 7), r=[w1T[dt_ % 2], h1[k]], w=[pp])
                p.stt("vector", etmps[m2 % 2][:, 0:512], pp[:, 0:512], b2c_[0:128, dt_:dt_ + 1],
                      x3m[dt_][:, ms], ALU.add, ALU.mult, r=[pp, b2c_, x3m[dt_]], w=[etmps[m2 % 2]])
                p.tt("vector" if (dt_ + m2) % 2 == 0 else "gpsimd",
                     acc[dt_][:, ms], acc[dt_][:, ms], etmps[m2 % 2][:, 0:512], ALU.add,
                     r=[etmps[m2 % 2], acc[dt_]], w=[acc[dt_]])
            if dt_ + 1 < 8:
                load_w1T(w1T[(dt_ + 1) % 2], "L1_ffn_w2", e, (dt_ + 1) * 128, (dt_ + 1) % 2)
    p.pop_scope()
    dump_fm("moe", acc)
    if upto <= 9:
        return finish(p, nc, outs, inp)

    p.push_scope()
    hh = [p.sb(f"hh{i}", [128, 1024], BF16) for i in range(16)]
    wsn = [p.sb(f"wsn{j}", [128, D], BF16) for j in range(2)]
    wsT = [p.sb(f"wsT{j}", [128, 8 * 128], BF16) for j in range(2)]
    gs2s = [p.sb(f"gs2_{i}", [128, 512], F32) for i in range(2)]

    def load_wsT(dst, wname, rows, cols0, ncols, j):
        wn = wsn[j]
        assert ncols <= 8
        p.dma("gpsimd", wn[:, 0:ncols * 128], inp[wname][rows:rows + 128, cols0:cols0 + ncols * 128],
              w=[wn])
        pt = PBA if j == 0 else PBB
        for k in range(ncols):
            p.trp(pt[:, k * 128:(k + 1) * 128], wn[:, k * 128:(k + 1) * 128], identb[:, :],
                  start=(k == 0), stop=(k == ncols - 1), r=[wn, identb], w=[pt])
        p.cp("vector" if j == 0 else "scalar", dst[:, 0:ncols * 128], pt[:, 0:ncols * 128],
             r=[pt], w=[dst])

    for c2 in range(2):
        tbase = c2 * 1024
        load_wsT(wsT[0], "L1_ffn_sw1", 0, 0, 8, 0)
        load_wsT(wsT[1], "L1_ffn_sw2", 0, 0, 8, 1)
        for ht in range(16):
            for m2 in range(2):
                ms = slice(tbase + m2 * 512, tbase + (m2 + 1) * 512)
                pg = psf[0 + 2 * m2]
                for k in range(8):
                    p.mm(pg[:, 0:512], wsT[0][:, k * 128:(k + 1) * 128], xn[k][:, ms],
                         start=(k == 0), stop=(k == 7), r=[wsT[0], xn[k]], w=[pg])
                p.act(gs2s[m2][:, 0:512], pg[:, 0:512], AF.Silu, r=[pg], w=[gs2s[m2]])
                pu = psf[1 + 2 * m2]
                for k in range(8):
                    p.mm(pu[:, 0:512], wsT[1][:, k * 128:(k + 1) * 128], xn[k][:, ms],
                         start=(k == 0), stop=(k == 7), r=[wsT[1], xn[k]], w=[pu])
                p.tt("vector", hh[ht][:, m2 * 512:(m2 + 1) * 512], pu[:, 0:512], gs2s[m2][:, 0:512],
                     ALU.mult, r=[pu, gs2s[m2]], w=[hh[ht]])
            if ht + 1 < 16:
                load_wsT(wsT[2 + (ht + 1) % 2] if False else wsT[0], "L1_ffn_sw1", (ht + 1) * 128, 0, 8, 0)
                load_wsT(wsT[1], "L1_ffn_sw2", (ht + 1) * 128, 0, 8, 1)
        for half in range(2):
            load_wsT(wsT[half], "L1_ffn_sw3", 0, half * 1024, 8, half)
        for dt_ in range(8):
            for m2 in range(2):
                ms = slice(tbase + m2 * 512, tbase + (m2 + 1) * 512)
                pd = psf[2 + dt_ % 2]
                for k in range(16):
                    wsrc = wsT[k // 8]
                    p.mm(pd[:, 0:512], wsrc[:, (k % 8) * 128:((k % 8) + 1) * 128],
                         hh[k][:, m2 * 512:(m2 + 1) * 512],
                         start=(k == 0), stop=(k == 15), r=[wsrc, hh[k]], w=[pd])
                p.tt("vector", acc[dt_][:, ms], acc[dt_][:, ms], pd[:, 0:512], ALU.add,
                     r=[pd, acc[dt_]], w=[acc[dt_]])
            if dt_ + 1 < 8:
                for half in range(2):
                    load_wsT(wsT[half], "L1_ffn_sw3", (dt_ + 1) * 128, half * 1024, 8, half)
    p.pop_scope()
    dump_fm("zsh", acc)

    p.push_scope()
    onat = p.sb("onat", [128, D], F32)
    for tt in range(16):
        for k in range(8):
            pb = psf[k % 4]
            p.tr(pb[:, 0:128], acc[k][:, tt * 128:(tt + 1) * 128], identf[:, :],
                 r=[acc[k], identf], w=[pb])
            p.cp("vector" if k % 2 == 0 else "scalar", onat[:, k * 128:(k + 1) * 128],
                 pb[:, 0:128], r=[pb], w=[onat])
        p.dma("sync", o_ys[tt * 128:(tt + 1) * 128, :], onat[:, :], r=[onat])
    p.pop_scope()

    return finish(p, nc, outs, inp)


def finish(p, nc, outs, inp):
    p.emit()
    return nc, outs, list(inp.cache)


from concourse.bass_utils import run_bass_kernel_spmd
import ml_dtypes

_T_FULL = 8192


def _rope_tables():
    half = 128
    inv_freq = 1.0 / 10000.0 ** (np.arange(half, dtype=np.float64) / half)
    ang = np.arange(_T_FULL, dtype=np.float64)[:, None] * inv_freq[None, :]
    return np.cos(ang).astype(np.float32), np.sin(ang).astype(np.float32)


def _attn1_consts():
    bf = ml_dtypes.bfloat16
    SC1 = (1.0 / 8.0) / 32.0
    dupm = np.zeros((128, 256), np.float32)
    selm = np.zeros((128, 512), np.float32)
    segm = np.zeros((128, 128), np.float32)
    gmaskm = np.zeros((128, 128), np.float32)
    bcastP = np.zeros((16, 2112), np.float32)
    for e in range(64):
        dupm[e, e] = dupm[e, 64 + e] = 1.0
        dupm[64 + e, 128 + e] = dupm[64 + e, 192 + e] = 1.0
    for dh in range(2):
        for sh in range(2):
            g = dh * 2 + sh
            for e in range(64):
                selm[sh * 64 + e, g * 128 + dh * 64 + e] = 1.0
    for ft in range(8):
        for pp in range(128):
            cc_ = 2 * ft + (1 if pp >= 64 else 0)
            segm[pp, ft * 16 + cc_] = SC1
            gmaskm[pp, ft * 16 + cc_] = 1.0
    for pq in range(16):
        bcastP[pq, 128 * pq + 64:128 * pq + 128] = 1.0
    return {"dupm": dupm.astype(bf), "selm": selm.astype(bf),
            "segm": segm.astype(bf), "gmaskm": gmaskm,
            "bcastP": bcastP.astype(bf)}


def _core_inputs(inputs, c, cos_f, sin_f, prefix_blobs=None):
    x = inputs["L0_attn_hidden_states"]
    sl = slice(c * TC, (c + 1) * TC)
    xs = np.ascontiguousarray(
        np.concatenate([x[0, sl], x[1, sl]], axis=0)).astype(np.float32, copy=False)
    pm = np.zeros((128, 8), np.float32)
    for j in range(8):
        pm[:, j] = 1.0 if c < j else 0.0
    m = {
        "xs": xs,
        "identf": np.eye(128, dtype=np.float32),
        "identb": np.eye(128, dtype=np.float32).astype(ml_dtypes.bfloat16),
        "utmask": np.triu(np.ones((128, 128), np.float32)),
        "cos": np.ascontiguousarray(cos_f[sl]),
        "sin": np.ascontiguousarray(sin_f[sl]),
        "pmask": pm,
    }
    m.update(_attn1_consts())
    for k, v in inputs.items():
        if k != "L0_attn_hidden_states":
            m[k] = np.ascontiguousarray(v)
    if prefix_blobs is not None:
        m["blobsK"], m["blobsVz"], m["blobszk"] = prefix_blobs
    return m


FUSED = True


def kernel(**inputs):
    cos_f, sin_f = _rope_tables()
    if FUSED:
        ncB, outsB, innB = build(dict(prefix="cc"))
        in_mapsB = []
        for c in range(8):
            m = _core_inputs(inputs, c, cos_f, sin_f)
            in_mapsB.append({k: m[k] for k in innB})
        resB = run_bass_kernel_spmd(ncB, in_mapsB, core_ids=list(range(8)))
    else:
        ncA, outsA, innA = build(dict(seg_only=True, prefix="zero"))
        in_mapsA = []
        for c in range(8):
            m = _core_inputs(inputs, c, cos_f, sin_f)
            in_mapsA.append({k: m[k] for k in innA})
        resA = run_bass_kernel_spmd(ncA, in_mapsA, core_ids=list(range(8)))
        blobsK = np.concatenate([resA.results[c]["blobK"] for c in range(8)], axis=0)
        blobsVz = np.concatenate([resA.results[c]["blobVz"] for c in range(8)], axis=0)
        blobszk = np.concatenate([resA.results[c]["blobzk"] for c in range(8)], axis=0)
        ncB, outsB, innB = build(dict(prefix="input"))
        in_mapsB = []
        for c in range(8):
            m = _core_inputs(inputs, c, cos_f, sin_f,
                             prefix_blobs=(blobsK, blobsVz, blobszk))
            in_mapsB.append({k: m[k] for k in innB})
        resB = run_bass_kernel_spmd(ncB, in_mapsB, core_ids=list(range(8)))
    out = np.empty((2, _T_FULL, 1024), np.float32)
    for c in range(8):
        ys = resB.results[c]["ys"]
        out[0, c * TC:(c + 1) * TC] = ys[0:TC]
        out[1, c * TC:(c + 1) * TC] = ys[TC:2 * TC]
    return out


# version 19 (final)
# speedup vs baseline: 1.1229x; 1.1229x over previous
import numpy as np
from contextlib import ExitStack
import concourse.bass as bass
import concourse.mybir as mybir

F32 = mybir.dt.float32
BF16 = mybir.dt.bfloat16
I32 = mybir.dt.int32
AF = mybir.ActivationFunctionType
ALU = mybir.AluOpType
AX = mybir.AxisListType

TC = 1024
NT = 2 * TC
H, M, DK = 4, 64, 256
D = 1024
EPS = 1e-5
SCALE = 1.0 / 16.0

ENGINES = ("tensor", "vector", "scalar", "gpsimd", "sync")


class Buf:
    __slots__ = ("t", "writers", "readers", "name")

    def __init__(self, t, name):
        self.t = t
        self.name = name
        self.writers = []
        self.readers = []

    def __getitem__(self, key):
        return self.t.ap()[key] if not isinstance(self.t, bass.AP) else self.t[key]

    @property
    def ap(self):
        return self.t.ap() if not isinstance(self.t, bass.AP) else self.t


class Prog:
    def __init__(self, nc):
        self.nc = nc
        self.ops = []
        self.NDMASEM = 8
        self.counts = {f"eng_{e}": 0 for e in ENGINES}
        for q in ("gpsimd", "sync", "scalar"):
            for s in range(self.NDMASEM):
                self.counts[f"dma_{q}_{s}"] = 0
        self.counts["cc"] = 0
        self.dma_rr = {"gpsimd": 0, "sync": 0, "scalar": 0}
        self.floor = {}
        self.scopes = [[]]
        g = nc.sbuf_tensor("constpool", [128, 16], F32)
        self.constpool = Buf(g.__enter__(), "constpool")
        self.scopes[0].append(g)
        self.constvals = {}

    def push_scope(self):
        self.scopes.append([])

    def pop_scope(self):
        for g in reversed(self.scopes.pop()):
            g.__exit__(None, None, None)
        self.barrier()

    def op(self, engine, fn, r=(), w=(), dma=False, cc=False, tag=""):
        deps = dict(self.floor)

        def add(ev):
            s, v = ev
            if deps.get(s, 0) < v:
                deps[s] = v
        for b in r:
            for ev in b.writers:
                add(ev)
        for b in w:
            for ev in b.writers:
                add(ev)
            for ev in b.readers:
                add(ev)
        if cc:
            self.counts["cc"] += 1
            ev = ("cc", self.counts["cc"])
            amt = 1
        elif dma:
            s = self.dma_rr[engine]
            self.dma_rr[engine] = (s + 1) % self.NDMASEM
            k = f"dma_{engine}_{s}"
            if deps.get(k, 0) < self.counts[k]:
                deps[k] = self.counts[k]
            self.counts[k] += 16
            ev = (k, self.counts[k])
            amt = 16
        else:
            k = f"eng_{engine}"
            self.counts[k] += 1
            ev = (k, self.counts[k])
            amt = 1
        self.ops.append(dict(engine=engine, fn=fn, waits=deps, inc=(ev[0], ev[1], amt), tag=tag))
        for b in r:
            b.readers.append(ev)
        for b in w:
            b.writers = [ev]
            b.readers = []
        return ev

    def barrier(self):
        self.floor = {k: v for k, v in self.counts.items() if v > 0 and k != "cc"}

    def sb(self, name, shape, dtype):
        g = self.nc.sbuf_tensor(name, shape, dtype)
        t = g.__enter__()
        self.scopes[-1].append(g)
        return Buf(t, name)

    def ps(self, name, shape, dtype=F32):
        t = self.nc.alloc_psum_tensor(name, shape, dtype)
        return Buf(t, name)

    def dma(self, engine, out_ap, in_ap, r=(), w=(), tag="", slow=False):
        if slow:
            return self.op(engine,
                           lambda e: e.dma_start(out=out_ap, in_=in_ap,
                                                 allow_slow_non_contiguous=True),
                           r=r, w=w, dma=True, tag=tag)
        return self.op(engine, lambda e: e.dma_start(out=out_ap, in_=in_ap),
                       r=r, w=w, dma=True, tag=tag)

    def mm(self, out_ap, lhsT_ap, rhs_ap, start, stop, r=(), w=(), tag=""):
        return self.op("tensor",
                       lambda e: e.matmul(out=out_ap, lhsT=lhsT_ap, rhs=rhs_ap,
                                          start=start, stop=stop),
                       r=r, w=w, tag=tag)

    def tr(self, out_ap, in_ap, ident_ap, r=(), w=(), tag=""):
        return self.op("tensor",
                       lambda e: e.transpose(out=out_ap, in_=in_ap, identity=ident_ap),
                       r=r, w=w, tag=tag)

    def trp(self, out_ap, in_ap, ident_ap, start, stop, r=(), w=(), tag=""):
        return self.op("tensor",
                       lambda e: e.matmul(out=out_ap, lhsT=in_ap, rhs=ident_ap,
                                          start=start, stop=stop, is_transpose=True),
                       r=r, w=w, tag=tag)

    def cp(self, eng, out_ap, in_ap, r=(), w=(), tag=""):
        if eng == "scalar":
            return self.op(eng, lambda e: e.copy(out=out_ap, in_=in_ap), r=r, w=w, tag=tag)
        return self.op(eng, lambda e: e.tensor_copy(out=out_ap, in_=in_ap), r=r, w=w, tag=tag)

    def tt(self, eng, out_ap, a_ap, b_ap, alu, r=(), w=(), tag=""):
        return self.op(eng, lambda e: e.tensor_tensor(out=out_ap, in0=a_ap, in1=b_ap, op=alu),
                       r=r, w=w, tag=tag)

    def tss(self, eng, out_ap, in_ap, scalar, alu, r=(), w=(), tag=""):
        return self.op(eng, lambda e: e.tensor_single_scalar(out=out_ap, in_=in_ap,
                                                             scalar=scalar, op=alu),
                       r=r, w=w, tag=tag)

    def ts2(self, eng, out_ap, in_ap, s1, s2, op0, op1, r=(), w=(), tag=""):
        return self.op(eng, lambda e: e.tensor_scalar(out=out_ap, in0=in_ap, scalar1=s1,
                                                      scalar2=s2, op0=op0, op1=op1),
                       r=r, w=w, tag=tag)

    def stt(self, eng, out_ap, in0_ap, scalar, in1_ap, op0, op1, r=(), w=(), tag=""):
        return self.op(eng, lambda e: e.scalar_tensor_tensor(out=out_ap, in0=in0_ap,
                                                             scalar=scalar, in1=in1_ap,
                                                             op0=op0, op1=op1),
                       r=r, w=w, tag=tag)

    def register_const(self, val):
        if val in self.constvals:
            return
        i = len(self.constvals)
        col = self.constpool[:, i:i + 1]
        self.memset("vector", col, val, w=[self.constpool])
        self.nc.const_aps.register(F32, val, col)
        self.constvals[val] = col

    def act(self, out_ap, in_ap, func, bias=0.0, scale=1.0, accum_ap=None, r=(), w=(), tag=""):
        if isinstance(bias, float) and bias != 0.0:
            self.register_const(bias)
            r = list(r) + [self.constpool]
        return self.op("scalar",
                       lambda e: e.activation(out=out_ap, in_=in_ap, func=func, bias=bias,
                                              scale=scale, accum_out=accum_ap),
                       r=r, w=w, tag=tag)

    def red(self, eng, out_ap, in_ap, alu, negate=False, r=(), w=(), tag=""):
        return self.op(eng, lambda e: e.tensor_reduce(out=out_ap, in_=in_ap, axis=AX.X,
                                                      op=alu, negate=negate),
                       r=r, w=w, tag=tag)

    def scan_cumsum(self, out_ap, data_ap, init, r=(), w=(), tag=""):
        return self.op("vector",
                       lambda e: e.tensor_tensor_scan(out=out_ap, data0=data_ap, data1=data_ap,
                                                      initial=init, op0=ALU.add, op1=ALU.bypass),
                       r=r, w=w, tag=tag)

    def memset(self, eng, ap, val, w=(), tag=""):
        return self.op(eng, lambda e: e.memset(ap, val), w=w, tag=tag)

    def recip(self, out_ap, in_ap, r=(), w=(), tag=""):
        return self.op("vector", lambda e: e.reciprocal(out=out_ap, in_=in_ap), r=r, w=w, tag=tag)

    def collective(self, kind, alu, cc_in_ap, cc_out_ap, r=(), w=(), tag=""):
        return self.op("gpsimd",
                       lambda e: e.collective_compute(kind, alu,
                                                      replica_groups=[list(range(8))],
                                                      ins=[cc_in_ap], outs=[cc_out_ap]),
                       r=r, w=w, cc=True, tag=tag)

    def emit(self):
        nc = self.nc
        with ExitStack() as ctx:
            sems = {}
            for k, v in self.counts.items():
                if v > 0:
                    sems[k] = ctx.enter_context(nc.semaphore(k))
            block = ctx.enter_context(nc.Block())
            by_engine = {e: [o for o in self.ops if o["engine"] == e] for e in ENGINES}
            finals = {k: v for k, v in self.counts.items() if v > 0}

            def section(eng_obj, engine):
                guaranteed = {}
                own = f"eng_{engine}"
                own_count = 0
                for o in by_engine[engine]:
                    for s, v in sorted(o["waits"].items()):
                        if s == own and engine == "tensor":
                            continue
                        if s == own and v <= own_count and engine != "tensor":
                            pass
                        if guaranteed.get(s, 0) < v:
                            eng_obj.wait_ge(sems[s], v)
                            guaranteed[s] = v
                    inst = o["fn"](eng_obj)
                    s, v, amt = o["inc"]
                    inst.then_inc(sems[s], amt)
                    if s == own:
                        own_count = v
                if engine == "sync":
                    for s, v in finals.items():
                        if guaranteed.get(s, 0) < v:
                            eng_obj.wait_ge(sems[s], v)

            for e in ENGINES:
                if by_engine[e] or e == "sync":
                    getattr(block, e)(lambda obj, e=e: section(obj, e))
        return nc




WNAMES_F32 = [
    "L0_attn_Wq", "L0_attn_Wk", "L0_attn_Wv", "L0_attn_Wg", "L0_attn_Wo",
    "L0_attn_sk_w1", "L0_attn_sk_w2", "L0_attn_sk_b2",
    "L0_attn_sv_w1", "L0_attn_sv_w2", "L0_attn_sv_b2", "L0_attn_gnorm_w",
    "L1_attn_q_w", "L1_attn_q_b", "L1_attn_k_w", "L1_attn_k_b",
    "L1_attn_v_w", "L1_attn_v_b", "L1_attn_out_w", "L1_attn_out_b",
    "L1_attn_rel_emb", "L1_attn_grep_w", "L1_attn_grep_b", "L1_attn_grep_a",
    "L1_attn_bias_k", "L1_attn_bias_v",
    "L1_ffn_gate_w", "L1_ffn_w1", "L1_ffn_b1", "L1_ffn_w2", "L1_ffn_b2",
    "L1_ffn_w3", "L1_ffn_b3", "L1_ffn_sw1", "L1_ffn_sw2", "L1_ffn_sw3",
]
WNAMES_I32 = ["L0_ffn_w_gate", "L0_ffn_w_up", "L0_ffn_w_down"]

WSHAPES = {
    "L0_attn_Wq": (1024, 1024), "L0_attn_Wk": (1024, 1024), "L0_attn_Wv": (1024, 1024),
    "L0_attn_Wg": (1024, 1024), "L0_attn_Wo": (1024, 1024),
    "L0_attn_sk_w1": (1024, 16), "L0_attn_sk_w2": (16, 256), "L0_attn_sk_b2": (256,),
    "L0_attn_sv_w1": (1024, 16), "L0_attn_sv_w2": (16, 256), "L0_attn_sv_b2": (256,),
    "L0_attn_gnorm_w": (256,),
    "L0_ffn_w_gate": (3072, 1024), "L0_ffn_w_up": (3072, 1024), "L0_ffn_w_down": (1024, 3072),
    "L1_attn_q_w": (1024, 1024), "L1_attn_q_b": (1024,),
    "L1_attn_k_w": (1024, 1024), "L1_attn_k_b": (1024,),
    "L1_attn_v_w": (1024, 1024), "L1_attn_v_b": (1024,),
    "L1_attn_out_w": (1024, 1024), "L1_attn_out_b": (1024,),
    "L1_attn_rel_emb": (32, 16), "L1_attn_grep_w": (64, 8), "L1_attn_grep_b": (8,),
    "L1_attn_grep_a": (1, 16, 1, 1), "L1_attn_bias_k": (1, 1, 1024), "L1_attn_bias_v": (1, 1, 1024),
    "L1_ffn_gate_w": (8, 1024), "L1_ffn_w1": (8, 1024, 1024), "L1_ffn_b1": (8, 1024),
    "L1_ffn_w2": (8, 1024, 1024), "L1_ffn_b2": (8, 1024),
    "L1_ffn_w3": (8, 1024, 1024), "L1_ffn_b3": (8, 1024),
    "L1_ffn_sw1": (2048, 1024), "L1_ffn_sw2": (2048, 1024), "L1_ffn_sw3": (1024, 2048),
}

BLOBK_W = 8 * 128
BLOBVZ_W = 8 * 257
BLOBZK_W = 8


def build(cfg):
    upto = cfg.get("upto", 99)
    prefix = cfg.get("prefix", "cc")
    seg_only = cfg.get("seg_only", False)
    scanstop = cfg.get("scanstop", 99)
    a1stop = cfg.get("a1stop", 99)
    debug = cfg.get("debug", set())

    nc = bass.Bass("TRN2", target_bir_lowering=False)
    p = Prog(nc)

    SPECS = dict(WSHAPES)
    SPECS.update({"xs": (NT, D), "identf": (128, 128), "identb": (128, 128),
                  "utmask": (128, 128), "cos": (TC, 128), "sin": (TC, 128),
                  "pmask": (128, 8), "blobsK": (8 * 128, BLOBK_W),
                  "blobsVz": (8 * 64, BLOBVZ_W), "blobszk": (8 * 64, BLOBZK_W),
                  "dupm": (128, 256), "selm": (128, 512), "segm": (128, 128),
                  "gmaskm": (128, 128), "bcastP": (16, 2112)})
    BF16_IN = {"identb", "dupm", "selm", "segm", "bcastP"}

    class LazyInputs:
        def __init__(self):
            self.cache = {}

        def __getitem__(self, nm):
            if nm not in self.cache:
                dt = I32 if nm in WNAMES_I32 else (BF16 if nm in BF16_IN else F32)
                self.cache[nm] = nc.dram_tensor(nm, list(SPECS[nm]), dt,
                                                kind="ExternalInput").ap()
            return self.cache[nm]

    inp = LazyInputs()

    outs = []

    def out_t(name, shape, dtype=F32):
        outs.append(name)
        return nc.dram_tensor(name, shape, dtype, kind="ExternalOutput").ap()

    if seg_only:
        o_blobK = out_t("blobK", [128, BLOBK_W])
        o_blobVz = out_t("blobVz", [64, BLOBVZ_W])
        o_blobzk = out_t("blobzk", [64, BLOBZK_W])
    else:
        o_ys = out_t("ys", [NT, D])

    dbg = {}

    def dbg_t(name, shape, dtype=F32):
        if name in debug:
            dbg[name] = out_t("dbg_" + name, shape, dtype)

    dbg_t("xn", [8 * 128, NT])
    dbg_t("t1", [16, NT])
    dbg_t("segK", [128, BLOBK_W])
    dbg_t("segVz", [64, BLOBVZ_W])
    dbg_t("segzk", [64, BLOBZK_W])
    dbg_t("preK", [128, BLOBK_W])
    dbg_t("preVz", [64, BLOBVZ_W])
    dbg_t("prezk", [64, BLOBZK_W])
    dbg_t("ffn0", [8 * 128, NT])
    dbg_t("og", [16 * 128, TC])
    dbg_t("attn0", [8 * 128, NT])
    dbg_t("x1", [8 * 128, NT])
    dbg_t("xn1", [8 * 128, NT])
    dbg_t("attn1", [8 * 128, NT])
    dbg_t("moemsc", [16, NT])
    dbg_t("moe", [8 * 128, NT])
    dbg_t("zsh", [8 * 128, NT])

    identf = p.sb("identf_s", [128, 128], F32)
    identb = p.sb("identb_s", [128, 128], BF16)
    utmask = p.sb("utmask_s", [128, 128], F32)
    cos_s = p.sb("cos_s", [128, 8 * 128], BF16)
    sin_s = p.sb("sin_s", [128, 8 * 128], BF16)
    pmask_s = p.sb("pmask_s", [128, 8], F32)
    onesf = p.sb("onesf", [128, 128], F32)
    onesb = p.sb("onesb", [128, 128], BF16)

    p.dma("sync", identf[:, :], inp["identf"], w=[identf])
    p.dma("sync", identb[:, :], inp["identb"], w=[identb])
    p.dma("sync", utmask[:, :], inp["utmask"], w=[utmask])
    p.dma("sync", pmask_s[:, :], inp["pmask"], w=[pmask_s])
    for tt in range(8):
        p.dma("gpsimd", cos_s[:, tt * 128:(tt + 1) * 128], inp["cos"][tt * 128:(tt + 1) * 128, :], w=[cos_s])
        p.dma("gpsimd", sin_s[:, tt * 128:(tt + 1) * 128], inp["sin"][tt * 128:(tt + 1) * 128, :], w=[sin_s])
    p.memset("vector", onesf[:, :], 1.0, w=[onesf])
    p.memset("vector", onesb[:, :], 1.0, w=[onesb])

    psf = [p.ps(f"psf{i}", [128, 512]) for i in range(6)]
    PBA = p.ps("pbA", [128, 1024], BF16)
    PBB = p.ps("pbB", [128, 1024], BF16)

    acc = [p.sb(f"acc{k}", [128, NT], F32) for k in range(8)]
    p.push_scope()
    xnat = [p.sb(f"xnat{j}", [128, D], F32) for j in range(3)]
    for tt in range(16):
        xb = xnat[tt % 3]
        p.dma("sync", xb[:, :], inp["xs"][tt * 128:(tt + 1) * 128, :], w=[xb])
        for k in range(8):
            pb = psf[(tt * 8 + k) % 4]
            p.tr(pb[:, 0:128], xb[:, k * 128:(k + 1) * 128], identf[:, :], r=[xb, identf], w=[pb])
            p.cp("vector" if k % 2 == 0 else "scalar",
                 acc[k][:, tt * 128:(tt + 1) * 128], pb[:, 0:128], r=[pb], w=[acc[k]])
    p.pop_scope()

    def layernorm_fm(src_tiles, xn_tiles, tag):
        p.push_scope()
        stmp = [p.sb(f"stmp_{tag}{j}", [128, NT], F32) for j in range(2)]
        rowA = p.sb(f"rowA_{tag}", [1, NT], F32)
        rowB = p.sb(f"rowB_{tag}", [1, NT], F32)
        rowC = p.sb(f"rowC_{tag}", [1, NT], F32)
        xsq = [p.sb(f"xsq_{tag}{j}", [128, 512], BF16) for j in range(2)]
        for k in range(8):
            p.cp("vector" if k % 2 == 0 else "gpsimd", xn_tiles[k][:, :], src_tiles[k][:, :],
                 r=[src_tiles[k]], w=[xn_tiles[k]])
        for mch in range(4):
            ms = slice(mch * 512, (mch + 1) * 512)
            pbx = psf[4]
            for k in range(8):
                p.mm(pbx[0:1, 0:512], onesb[:, 0:1], xn_tiles[k][:, ms],
                     start=(k == 0), stop=(k == 7), r=[onesb, xn_tiles[k]], w=[pbx])
            p.cp("vector", rowA[0:1, ms], pbx[0:1, 0:512], r=[pbx], w=[rowA])
            pbq = psf[5]
            for k in range(8):
                xq = xsq[k % 2]
                p.act(xq[:, 0:512], src_tiles[k][:, ms], AF.Square, r=[src_tiles[k]], w=[xq])
                p.mm(pbq[0:1, 0:512], onesb[:, 0:1], xq[:, 0:512],
                     start=(k == 0), stop=(k == 7), r=[onesb, xq], w=[pbq])
            p.cp("vector", rowB[0:1, ms], pbq[0:1, 0:512], r=[pbq], w=[rowB])
        p.tss("vector", rowA[0:1, :], rowA[0:1, :], 1.0 / D, ALU.mult, r=[rowA], w=[rowA])
        p.tt("vector", rowC[0:1, :], rowA[0:1, :], rowA[0:1, :], ALU.mult, r=[rowA], w=[rowC])
        p.stt("vector", rowB[0:1, :], rowB[0:1, :], 1.0 / D, rowC[0:1, :], ALU.mult, ALU.subtract,
              r=[rowB, rowC], w=[rowB])
        p.act(rowB[0:1, :], rowB[0:1, :], AF.Sqrt, bias=EPS, scale=1.0, r=[rowB], w=[rowB])
        p.recip(rowB[0:1, :], rowB[0:1, :], r=[rowB], w=[rowB])
        p.tt("vector", rowC[0:1, :], rowA[0:1, :], rowB[0:1, :], ALU.mult, r=[rowA, rowB], w=[rowC])
        rstd_bc = p.sb(f"rstdbc_{tag}", [128, NT], F32)
        mr_bc = p.sb(f"mrbc_{tag}", [128, NT], F32)
        for mch in range(4):
            ms = slice(mch * 512, (mch + 1) * 512)
            pbx = psf[4]
            p.mm(pbx[:, 0:512], onesf[0:1, :], rowB[0:1, ms], start=True, stop=True,
                 r=[onesf, rowB], w=[pbx])
            p.cp("vector", rstd_bc[:, ms], pbx[:, 0:512], r=[pbx], w=[rstd_bc])
            pbq = psf[5]
            p.mm(pbq[:, 0:512], onesf[0:1, :], rowC[0:1, ms], start=True, stop=True,
                 r=[onesf, rowC], w=[pbq])
            p.cp("scalar", mr_bc[:, ms], pbq[:, 0:512], r=[pbq], w=[mr_bc])
        for k in range(8):
            e_ = "vector" if k % 2 == 0 else "gpsimd"
            p.tt(e_, stmp[k % 2][:, :], src_tiles[k][:, :], rstd_bc[:, :], ALU.mult,
                 r=[src_tiles[k], rstd_bc], w=[stmp[k % 2]])
            p.tt(e_, xn_tiles[k][:, :], stmp[k % 2][:, :], mr_bc[:, :], ALU.subtract,
                 r=[stmp[k % 2], mr_bc], w=[xn_tiles[k]])
        p.pop_scope()

    xn = [p.sb(f"xn{k}", [128, NT], BF16) for k in range(8)]
    layernorm_fm(acc, xn, "ln0")

    def dump_fm(name, tiles, conv=True):
        if name not in dbg:
            return
        for k in range(len(tiles)):
            if tiles[k].t.dtype == BF16:
                p.dma("gpsimd", dbg[name][k * 128:(k + 1) * 128, :], tiles[k][:, :], r=[tiles[k]])
            else:
                p.dma("sync", dbg[name][k * 128:(k + 1) * 128, :], tiles[k][:, :], r=[tiles[k]])

    dump_fm("xn", xn)
    if upto <= 1:
        return finish(p, nc, outs, inp)

    p.push_scope()
    skw1 = p.sb("skw1", [128, 8 * 16], BF16)
    svw1 = p.sb("svw1", [128, 8 * 16], BF16)
    for k in range(8):
        p.dma("gpsimd", skw1[:, k * 16:(k + 1) * 16], inp["L0_attn_sk_w1"][k * 128:(k + 1) * 128, :], w=[skw1])
        p.dma("gpsimd", svw1[:, k * 16:(k + 1) * 16], inp["L0_attn_sv_w1"][k * 128:(k + 1) * 128, :], w=[svw1])
    t1k = p.sb("t1k", [16, NT], BF16)
    t1v = p.sb("t1v", [16, NT], BF16)
    for mch in range(4):
        ms = slice(mch * 512, (mch + 1) * 512)
        pbx = psf[0]
        for k in range(8):
            p.mm(pbx[0:16, 0:512], skw1[:, k * 16:(k + 1) * 16], xn[k][:, ms],
                 start=(k == 0), stop=(k == 7), r=[skw1, xn[k]], w=[pbx])
        p.cp("vector", t1k[0:16, ms], pbx[0:16, 0:512], r=[pbx], w=[t1k])
        pby = psf[1]
        for k in range(8):
            p.mm(pby[0:16, 0:512], svw1[:, k * 16:(k + 1) * 16], xn[k][:, ms],
                 start=(k == 0), stop=(k == 7), r=[svw1, xn[k]], w=[pby])
        p.cp("vector", t1v[0:16, ms], pby[0:16, 0:512], r=[pby], w=[t1v])
    if "t1" in dbg:
        tf = p.sb("t1f", [16, NT], F32)
        p.cp("vector", tf[:, :], t1k[:, :], r=[t1k], w=[tf])
        p.dma("sync", dbg["t1"], tf[:, :], r=[tf])

    skw2 = p.sb("skw2", [16, 256], BF16)
    svw2 = p.sb("svw2", [16, 256], BF16)
    p.dma("gpsimd", skw2[:, :], inp["L0_attn_sk_w2"], w=[skw2])
    p.dma("gpsimd", svw2[:, :], inp["L0_attn_sv_w2"], w=[svw2])
    skb2r = p.sb("skb2r", [4, 64], F32)
    svb2r = p.sb("svb2r", [4, 64], F32)
    p.dma("gpsimd", skb2r[:, :], inp["L0_attn_sk_b2"].rearrange("(a b) -> a b", b=64), w=[skb2r])
    p.dma("gpsimd", svb2r[:, :], inp["L0_attn_sv_b2"].rearrange("(a b) -> a b", b=64), w=[svb2r])
    skb2c = p.sb("skb2c", [64, 4], F32)
    svb2c = p.sb("svb2c", [64, 4], F32)
    pbx = psf[0]
    p.tr(pbx[0:64, 0:4], skb2r[:, :], identf[0:4, 0:4], r=[skb2r, identf], w=[pbx])
    p.cp("vector", skb2c[:, :], pbx[0:64, 0:4], r=[pbx], w=[skb2c])
    pby = psf[1]
    p.tr(pby[0:64, 0:4], svb2r[:, :], identf[0:4, 0:4], r=[svb2r, identf], w=[pby])
    p.cp("vector", svb2c[:, :], pby[0:64, 0:4], r=[pby], w=[svb2c])

    def load_wslice(buf, wname, h, k):
        p.dma("gpsimd", buf[:, k * 256:(k + 1) * 256],
              inp[wname][k * 128:(k + 1) * 128, h * 256:(h + 1) * 256], w=[buf])

    ktm_b = [p.sb(f"ktm{j}", [128, 256], BF16) for j in range(2)]
    vtm_b = [p.sb(f"vtm{j}", [128, 256], BF16) for j in range(2)]
    wkf_b = [p.sb(f"wkf{j}", [64, 128], BF16) for j in range(2)]
    wvf_b = [p.sb(f"wvf{j}", [64, 128], BF16) for j in range(2)]
    wktm_b = [p.sb(f"wktm{j}", [128, 64], BF16) for j in range(2)]
    wvtm_b = [p.sb(f"wvtm{j}", [128, 64], BF16) for j in range(2)]
    rtmp = [p.sb(f"rtmp{j}", [128, 256], F32) for j in range(2)]
    rtmp += [p.sb(f"rtmp{j+2}", [64, 128], F32) for j in range(2)]

    def proj_chunk(b, h, tc, j, Wkh=None, Wvh=None, need=("k", "v", "wk", "wv")):
        col = b * TC + tc * 128
        cs = slice(col, col + 128)
        ts = slice(tc * 128, (tc + 1) * 128)
        res = {}
        if "k" in need:
            pk = psf[0]
            for k in range(8):
                p.mm(pk[:, 0:256], xn[k][:, cs], Wkh[:, k * 256:(k + 1) * 256],
                     start=(k == 0), stop=(k == 7), r=[xn[k], Wkh], w=[pk])
            t0 = rtmp[0]
            t1_ = rtmp[1]
            p.tt("vector", t0[:, 0:128], pk[:, 0:128], cos_s[:, ts], ALU.mult, r=[pk, cos_s], w=[t0])
            p.tt("vector", t0[:, 128:256], pk[:, 128:256], sin_s[:, ts], ALU.mult, r=[pk, sin_s], w=[t0])
            p.tt("vector", t1_[:, 0:128], pk[:, 128:256], cos_s[:, ts], ALU.mult, r=[pk, cos_s], w=[t1_])
            p.tt("vector", t1_[:, 128:256], pk[:, 0:128], sin_s[:, ts], ALU.mult, r=[pk, sin_s], w=[t1_])
            p.tt("vector", ktm_b[j][:, 0:128], t0[:, 0:128], t0[:, 128:256], ALU.subtract,
                 r=[t0], w=[ktm_b[j]])
            p.tt("vector", ktm_b[j][:, 128:256], t1_[:, 0:128], t1_[:, 128:256], ALU.add,
                 r=[t1_], w=[ktm_b[j]])
            res["ktm"] = ktm_b[j]
        if "v" in need:
            pv = psf[1]
            for k in range(8):
                p.mm(pv[:, 0:256], xn[k][:, cs], Wvh[:, k * 256:(k + 1) * 256],
                     start=(k == 0), stop=(k == 7), r=[xn[k], Wvh], w=[pv])
            p.cp("vector", vtm_b[j][:, :], pv[:, 0:256], r=[pv], w=[vtm_b[j]])
            res["vtm"] = vtm_b[j]
        if "wk" in need:
            pw = psf[2]
            p.mm(pw[0:64, 0:128], skw2[:, h * 64:(h + 1) * 64], t1k[:, cs],
                 start=True, stop=True, r=[skw2, t1k], w=[pw])
            hc = skb2c[0:64, h:h + 1]
            wtmp = rtmp[2]
            p.ts2("vector", wtmp[0:64, 0:128], pw[0:64, 0:128], hc, 32.0, ALU.add, ALU.min,
                  r=[pw, skb2c], w=[wtmp])
            p.tss("vector", wtmp[0:64, 0:128], wtmp[0:64, 0:128], -32.0, ALU.max,
                  r=[wtmp], w=[wtmp])
            p.act(wkf_b[j][0:64, 0:128], wtmp[0:64, 0:128], AF.Exp, r=[wtmp], w=[wkf_b[j]])
            p.tr(PBA[0:128, 0:64], wkf_b[j][0:64, 0:128], identb[0:64, 0:64],
                 r=[wkf_b[j], identb], w=[PBA])
            p.cp("vector", wktm_b[j][:, :], PBA[0:128, 0:64], r=[PBA], w=[wktm_b[j]])
            res["wkf"] = wkf_b[j]
            res["wktm"] = wktm_b[j]
        if "wv" in need:
            pw = psf[2]
            p.mm(pw[0:64, 0:128], svw2[:, h * 64:(h + 1) * 64], t1v[:, cs],
                 start=True, stop=True, r=[svw2, t1v], w=[pw])
            hc = svb2c[0:64, h:h + 1]
            wtmp = rtmp[3]
            p.ts2("vector", wtmp[0:64, 0:128], pw[0:64, 0:128], hc, 32.0, ALU.add, ALU.min,
                  r=[pw, svb2c], w=[wtmp])
            p.tss("vector", wtmp[0:64, 0:128], wtmp[0:64, 0:128], -32.0, ALU.max,
                  r=[wtmp], w=[wtmp])
            p.act(wvf_b[j][0:64, 0:128], wtmp[0:64, 0:128], AF.Exp, r=[wtmp], w=[wvf_b[j]])
            p.tr(PBB[0:128, 0:64], wvf_b[j][0:64, 0:128], identb[0:64, 0:64],
                 r=[wvf_b[j], identb], w=[PBB])
            p.cp("vector", wvtm_b[j][:, :], PBB[0:128, 0:64], r=[PBB], w=[wvtm_b[j]])
            res["wvf"] = wvf_b[j]
            res["wvtm"] = wvtm_b[j]
        return res

    if prefix != "cc":
        preK = p.sb("preK", [128, BLOBK_W], F32)
        preVz = p.sb("preVz", [64, BLOBVZ_W], F32)
        prezk = p.sb("prezk", [64, BLOBZK_W], F32)
    else:
        preK = preVz = prezk = None

    p.push_scope()
    blobK = p.sb("blobK_s", [128, BLOBK_W], F32)
    blobVz = p.sb("blobVz_s", [64, BLOBVZ_W], F32)
    blobzk = p.sb("blobzk_s", [64, BLOBZK_W], F32)
    onecol = p.sb("onecol", [128, 1], BF16)
    p.memset("vector", onecol[:, :], 1.0, w=[onecol])

    p.push_scope()
    Wkh_bufs = [p.sb(f"Wkh{j}", [128, 8 * 256], BF16) for j in range(2)]
    Wvh_bufs = [p.sb(f"Wvh{j}", [128, 8 * 256], BF16) for j in range(2)]
    for h in range(4):
        Wkh, Wvh = Wkh_bufs[h % 2], Wvh_bufs[h % 2]
        for k in range(8):
            load_wslice(Wkh, "L0_attn_Wk", h, k)
            load_wslice(Wvh, "L0_attn_Wv", h, k)
        for b in range(2):
            bh = b * 4 + h
            nvz = 0
            for tc in range(8):
                j = tc % 2
                r = proj_chunk(b, h, tc, j, Wkh=Wkh, Wvh=Wvh)
                st, sp = (tc == 0), (tc == 7)
                p.mm(psf[4][:, 0:64], r["ktm"][:, 0:128], r["wktm"][:, :],
                     start=st, stop=sp, r=[r["ktm"], r["wktm"]], w=[psf[4]])
                p.mm(psf[5][:, 0:64], r["ktm"][:, 128:256], r["wktm"][:, :],
                     start=st, stop=sp, r=[r["ktm"], r["wktm"]], w=[psf[5]])
                p.mm(psf[3][0:64, 0:256], r["wvtm"][:, :], r["vtm"][:, :],
                     start=(nvz == 0), stop=False, r=[r["wvtm"], r["vtm"]], w=[psf[3]])
                nvz += 1
                p.mm(psf[3][0:64, 256:257], r["wvtm"][:, :], onecol[:, :],
                     start=False, stop=False, r=[r["wvtm"], onecol], w=[psf[3]])
                p.mm(psf[3][0:64, 260:261], r["wktm"][:, :], onecol[:, :],
                     start=False, stop=(tc == 7), r=[r["wktm"], onecol], w=[psf[3]])
            p.cp("vector", blobK[:, bh * 128:bh * 128 + 64], psf[4][:, 0:64], r=[psf[4]], w=[blobK])
            p.cp("vector", blobK[:, bh * 128 + 64:(bh + 1) * 128], psf[5][:, 0:64], r=[psf[5]], w=[blobK])
            p.cp("vector", blobVz[0:64, bh * 257:(bh + 1) * 257], psf[3][0:64, 0:257],
                 r=[psf[3]], w=[blobVz])
            p.cp("vector", blobzk[0:64, bh:bh + 1], psf[3][0:64, 260:261], r=[psf[3]], w=[blobzk])
    p.pop_scope()

    if "segK" in dbg:
        p.dma("sync", dbg["segK"], blobK[:, :], r=[blobK])
    if "segVz" in dbg:
        p.dma("sync", dbg["segVz"], blobVz[:, :], r=[blobVz])
    if "segzk" in dbg:
        p.dma("sync", dbg["segzk"], blobzk[:, :], r=[blobzk])
    if seg_only:
        p.dma("sync", o_blobK, blobK[:, :], r=[blobK])
        p.dma("sync", o_blobVz, blobVz[:, :], r=[blobVz])
        p.dma("sync", o_blobzk, blobzk[:, :], r=[blobzk])
        return finish(p, nc, outs, inp)
    if upto <= 2:
        return finish(p, nc, outs, inp)

    if prefix == "zero":
        p.memset("vector", preK[:, :], 0.0, w=[preK])
        p.memset("vector", preVz[:, :], 0.0, w=[preVz])
        p.memset("vector", prezk[:, :], 0.0, w=[prezk])
    elif prefix == "input":
        p.memset("vector", preK[:, :], 0.0, w=[preK])
        p.memset("vector", preVz[:, :], 0.0, w=[preVz])
        p.memset("vector", prezk[:, :], 0.0, w=[prezk])
        bK = p.sb("binK", [128, BLOBK_W], F32)
        bVz = p.sb("binVz", [64, BLOBVZ_W], F32)
        bzk = p.sb("binzk", [64, BLOBZK_W], F32)
        for cc_ in range(8):
            p.dma("sync", bK[:, :], inp["blobsK"][cc_ * 128:(cc_ + 1) * 128, :], w=[bK])
            p.dma("sync", bVz[:, :], inp["blobsVz"][cc_ * 64:(cc_ + 1) * 64, :], w=[bVz])
            p.dma("sync", bzk[:, :], inp["blobszk"][cc_ * 64:(cc_ + 1) * 64, :], w=[bzk])
            p.stt("vector", preK[:, :], bK[:, :], pmask_s[0:128, cc_:cc_ + 1], preK[:, :],
                  ALU.mult, ALU.add, r=[bK, pmask_s, preK], w=[preK])
            p.stt("vector", preVz[:, :], bVz[:, :], pmask_s[0:64, cc_:cc_ + 1], preVz[:, :],
                  ALU.mult, ALU.add, r=[bVz, pmask_s, preVz], w=[preVz])
            p.stt("vector", prezk[:, :], bzk[:, :], pmask_s[0:64, cc_:cc_ + 1], prezk[:, :],
                  ALU.mult, ALU.add, r=[bzk, pmask_s, prezk], w=[prezk])
    else:
        ccK_in = nc.dram_tensor("ccK_in", [8 * 128, BLOBK_W], F32).ap()
        ccK_out = nc.dram_tensor("ccK_out", [128, BLOBK_W], F32).ap()
        ccVz_in = nc.dram_tensor("ccVz_in", [8 * 64, BLOBVZ_W], F32).ap()
        ccVz_out = nc.dram_tensor("ccVz_out", [64, BLOBVZ_W], F32).ap()
        cczk_in = nc.dram_tensor("cczk_in", [8 * 64, BLOBZK_W], F32).ap()
        cczk_out = nc.dram_tensor("cczk_out", [64, BLOBZK_W], F32).ap()
        bK_d = Buf(None, "ccKin_d")
        bVz_d = Buf(None, "ccVzin_d")
        bzk_d = Buf(None, "cczkin_d")
        oK_d = Buf(None, "ccKout_d")
        oVz_d = Buf(None, "ccVzout_d")
        ozk_d = Buf(None, "cczkout_d")
        msK = p.sb("msK", [128, BLOBK_W], F32)
        msVz = p.sb("msVz", [64, BLOBVZ_W], F32)
        mszk = p.sb("mszk", [64, BLOBZK_W], F32)
        for jj in range(8):
            p.tss("vector", msK[:, :], blobK[:, :], pmask_s[0:128, jj:jj + 1], ALU.mult,
                  r=[blobK, pmask_s], w=[msK])
            p.dma("gpsimd", ccK_in[jj * 128:(jj + 1) * 128, :], msK[:, :], r=[msK], w=[bK_d])
            p.tss("vector", msVz[:, :], blobVz[:, :], pmask_s[0:64, jj:jj + 1], ALU.mult,
                  r=[blobVz, pmask_s], w=[msVz])
            p.dma("gpsimd", ccVz_in[jj * 64:(jj + 1) * 64, :], msVz[:, :], r=[msVz], w=[bVz_d])
            p.tss("vector", mszk[:, :], blobzk[:, :], pmask_s[0:64, jj:jj + 1], ALU.mult,
                  r=[blobzk, pmask_s], w=[mszk])
            p.dma("gpsimd", cczk_in[jj * 64:(jj + 1) * 64, :], mszk[:, :], r=[mszk], w=[bzk_d])
        p.collective("ReduceScatter", ALU.add, ccK_in.opt(), ccK_out.opt(),
                     r=[bK_d], w=[oK_d], tag="rsK")
        p.collective("ReduceScatter", ALU.add, ccVz_in.opt(), ccVz_out.opt(),
                     r=[bVz_d], w=[oVz_d], tag="rsVz")
        p.collective("ReduceScatter", ALU.add, cczk_in.opt(), cczk_out.opt(),
                     r=[bzk_d], w=[ozk_d], tag="rszk")
        readback = (ccK_out, ccVz_out, cczk_out, [oK_d, oVz_d, ozk_d])
    p.pop_scope()

    if "preK" in dbg and prefix != "cc":
        p.dma("sync", dbg["preK"], preK[:, :], r=[preK])
        p.dma("sync", dbg["preVz"], preVz[:, :], r=[preVz])
        p.dma("sync", dbg["prezk"], prezk[:, :], r=[prezk])
    if upto <= 3:
        return finish(p, nc, outs, inp)

    S127 = 1.0 / 127.0
    p.push_scope()
    hbuf = [p.sb(f"hb{i}", [128, 1024], BF16) for i in range(24)]
    wnat = [p.sb(f"wnat{j}", [128, 1024], F32) for j in range(2)]
    wnb = [p.sb(f"wnb{j}", [128, 1024], BF16) for j in range(1)]
    wT = [p.sb(f"wTf{j}", [128, 8 * 128], BF16) for j in range(4)]
    wTd8 = [p.sb(f"wTd8_{j}", [128, 8 * 128], BF16) for j in range(2)]
    gsil0s = [p.sb(f"gsil0_{i}", [128, 512], BF16) for i in range(2)]

    def load_wT(dst, wname, rows, kk_n, j, col0=0):
        for blk in range(0, kk_n, 8):
            nb = min(8, kk_n - blk)
            wn = wnat[j]
            wb = wnb[0]
            p.dma("gpsimd", wn[:, 0:nb * 128],
                  inp[wname][rows:rows + 128, col0 + blk * 128:col0 + (blk + nb) * 128], w=[wn])
            p.cp("vector" if j % 2 == 0 else "gpsimd", wb[:, 0:nb * 128], wn[:, 0:nb * 128],
                 r=[wn], w=[wb])
            pb = PBA if (j + blk // 8) % 2 == 0 else PBB
            for k in range(nb):
                p.trp(pb[:, k * 128:(k + 1) * 128], wb[:, k * 128:(k + 1) * 128], identb[:, :],
                      start=(k == 0), stop=(k == nb - 1), r=[wb, identb], w=[pb])
            p.cp("vector" if j % 2 == 0 else "scalar",
                 dst[:, blk * 128:(blk + nb) * 128], pb[:, 0:nb * 128], r=[pb], w=[dst])

    for c2 in range(2):
        tbase = c2 * 1024
        load_wT(wT[0], "L0_ffn_w_gate", 0, 8, 0)
        load_wT(wT[1], "L0_ffn_w_up", 0, 8, 1)
        for ht in range(24):
            jg, ju = 2 * (ht % 2), 2 * (ht % 2) + 1
            if False:
                pass
            for m2 in range(2):
                ms = slice(tbase + m2 * 512, tbase + (m2 + 1) * 512)
                pg = psf[0 + 2 * m2]
                for k in range(8):
                    p.mm(pg[:, 0:512], wT[jg][:, k * 128:(k + 1) * 128], xn[k][:, ms],
                         start=(k == 0), stop=(k == 7), r=[wT[jg], xn[k]], w=[pg])
                p.act(gsil0s[m2][:, 0:512], pg[:, 0:512], AF.Silu, scale=S127, r=[pg], w=[gsil0s[m2]])
                pu = psf[1 + 2 * m2]
                for k in range(8):
                    p.mm(pu[:, 0:512], wT[ju][:, k * 128:(k + 1) * 128], xn[k][:, ms],
                         start=(k == 0), stop=(k == 7), r=[wT[ju], xn[k]], w=[pu])
                p.stt("vector", hbuf[ht][:, m2 * 512:(m2 + 1) * 512], pu[:, 0:512], S127,
                      gsil0s[m2][:, 0:512], ALU.mult, ALU.mult, r=[pu, gsil0s[m2]], w=[hbuf[ht]])
            if ht + 1 < 24:
                load_wT(wT[2 * ((ht + 1) % 2)], "L0_ffn_w_gate", (ht + 1) * 128, 8, 0)
                load_wT(wT[2 * ((ht + 1) % 2) + 1], "L0_ffn_w_up", (ht + 1) * 128, 8, 1)
        load_wT(wTd8[0], "L0_ffn_w_down", 0, 8, 0, col0=0)
        for dt_ in range(8):
            pd0, pd1 = psf[2], psf[3]
            for blk in range(3):
                gidx = dt_ * 3 + blk
                wsl = wTd8[gidx % 2]
                if blk < 2:
                    load_wT(wTd8[(gidx + 1) % 2], "L0_ffn_w_down", dt_ * 128, 8,
                            (gidx + 1) % 2, col0=(blk + 1) * 1024)
                elif dt_ + 1 < 8:
                    load_wT(wTd8[(gidx + 1) % 2], "L0_ffn_w_down", (dt_ + 1) * 128, 8,
                            (gidx + 1) % 2, col0=0)
                for m2 in range(2):
                    pd = pd0 if m2 == 0 else pd1
                    ms0 = tbase + m2 * 512
                    for k in range(8):
                        p.mm(pd[:, 0:512], wsl[:, k * 128:(k + 1) * 128],
                             hbuf[blk * 8 + k][:, ms0 - tbase + tbase:ms0 + 512]
                             if False else hbuf[blk * 8 + k][:, m2 * 512:(m2 + 1) * 512],
                             start=(blk == 0 and k == 0), stop=(blk == 2 and k == 7),
                             r=[wsl, hbuf[blk * 8 + k]], w=[pd])
            for m2 in range(2):
                ms = slice(tbase + m2 * 512, tbase + (m2 + 1) * 512)
                pd = pd0 if m2 == 0 else pd1
                p.stt("vector", acc[dt_][:, ms], pd[:, 0:512], S127, acc[dt_][:, ms],
                      ALU.mult, ALU.add, r=[pd, acc[dt_]], w=[acc[dt_]])
    p.pop_scope()
    dump_fm("ffn0", acc)
    if upto <= 4:
        return finish(p, nc, outs, inp)

    if prefix == "cc":
        ccKo, ccVzo, cczko, ccdeps = readback

    p.push_scope()
    p.push_scope()
    cosT_s = p.sb("cosT_s", [128, 8 * 128], BF16)
    sinT_s = p.sb("sinT_s", [128, 8 * 128], BF16)
    for tt in range(8):
        pb = PBA if tt % 2 == 0 else PBB
        p.trp(pb[:, 0:128], cos_s[:, tt * 128:(tt + 1) * 128], identb[:, :], start=True,
              stop=False, r=[cos_s, identb], w=[pb])
        p.trp(pb[:, 128:256], sin_s[:, tt * 128:(tt + 1) * 128], identb[:, :], start=False,
              stop=True, r=[sin_s, identb], w=[pb])
        p.cp("vector", cosT_s[:, tt * 128:(tt + 1) * 128], pb[:, 0:128], r=[pb], w=[cosT_s])
        p.cp("vector", sinT_s[:, tt * 128:(tt + 1) * 128], pb[:, 128:256], r=[pb], w=[sinT_s])

    gnr = p.sb("gnr", [2, 128], F32)
    p.dma("gpsimd", gnr[:, :], inp["L0_attn_gnorm_w"].rearrange("(a b) -> a b", b=128), w=[gnr])
    gnc = p.sb("gnc", [128, 2], F32)
    p.tr(psf[0][:, 0:2], gnr[:, :], identf[0:2, 0:2], r=[gnr, identf], w=[psf[0]])
    p.cp("vector", gnc[:, :], psf[0][:, 0:2], r=[psf[0]], w=[gnc])
    og4 = [p.sb(f"og4_{i}", [128, 256], BF16) for i in range(8)]
    wo_t = [p.sb(f"wo{kk}", [128, D], BF16) for kk in range(4)]
    wbig = [p.sb(f"wbig{j}", [128, 8 * 512], BF16) for j in range(4)]

    def load_wpair(dst, wname, hp):
        for k in range(8):
            p.dma("gpsimd", dst[:, k * 512:(k + 1) * 512],
                  inp[wname][k * 128:(k + 1) * 128, hp * 512:(hp + 1) * 512], w=[dst])
        return dst
    ktm2 = [[p.sb(f"ktm2_{j}{b}", [128, 512], BF16) for b in range(2)] for j in range(1)]
    vtm2 = [[p.sb(f"vtm2_{j}{b}", [128, 512], BF16) for b in range(2)] for j in range(1)]
    gsil2 = [[p.sb(f"gsil2_{j}{b}", [128, 512], BF16) for b in range(2)] for j in range(1)]
    qfm4 = [[p.sb(f"qfm4_{j}{hh}", [128, 512], BF16) for hh in range(2)] for j in range(2)]
    kfm4 = [[p.sb(f"kfm4_{j}{b}", [128, 512], BF16) for b in range(2)] for j in range(2)]
    wkf4 = [[p.sb(f"wkf4_{j}{s}", [64, 128], BF16) for s in range(4)] for j in range(1)]
    wvf4 = [[p.sb(f"wvf4_{j}{s}", [64, 128], BF16) for s in range(4)] for j in range(1)]
    wktm2 = [[p.sb(f"wktm2_{j}{b}", [128, 128], BF16) for b in range(2)] for j in range(1)]
    wvtm2 = [[p.sb(f"wvtm2_{j}{b}", [128, 128], BF16) for b in range(2)] for j in range(1)]
    zkb = [[p.sb(f"zkb_{j}{b}", [128, 128], F32) for b in range(2)] for j in range(1)]
    zvb = [[p.sb(f"zvb_{j}{b}", [128, 128], F32) for b in range(2)] for j in range(1)]
    rzT2 = [[p.sb(f"rzT2_{j}{b}", [128, 256], F32) for b in range(2)] for j in range(1)]
    rt4 = [p.sb(f"rt4_{j}", [128, 512], F32) for j in range(1)]
    rt4.append(rt4[0])
    zkc4 = [p.sb(f"zkc4_{s}", [128, 1], F32) for s in range(4)]
    zvc4 = [p.sb(f"zvc4_{s}", [128, 1], F32) for s in range(4)]
    KstF4 = [p.sb(f"KstF4_{s}", [128, 128], F32) for s in range(4)]
    KstB4 = [p.sb(f"KstB4_{s}", [128, 128], BF16) for s in range(4)]
    VstF4 = [p.sb(f"VstF4_{s}", [64, 256], F32) for s in range(4)]
    VstB4 = [p.sb(f"VstB4_{s}", [64, 256], BF16) for s in range(4)]
    st4 = [p.sb(f"st4_{s}", [128, 128], BF16) for s in range(4)]
    lg4 = [p.sb(f"lg4_{s}", [128, 64], F32) for s in range(4)]
    nmx4 = [p.sb(f"nmx4_{s}", [128, 1], F32) for s in range(4)]
    sume4 = [p.sb(f"sume4_{s}", [128, 1], F32) for s in range(4)]
    et4 = [p.sb(f"et4_{s}", [128, 64], BF16) for s in range(4)]
    pttm4 = [p.sb(f"pttm4_{s}", [128, 64], BF16) for s in range(4)]
    ptfm4 = [p.sb(f"ptfm4_{s}", [64, 128], BF16) for s in range(4)]
    at4 = [p.sb(f"at4_{s}", [128, 128], BF16) for s in range(4)]
    sqs4 = [p.sb(f"sqs4_{s}", [128, 256], BF16) for s in range(1)]
    ssq4 = [p.sb(f"ssq4_{s}", [128, 1], F32) for s in range(4)]
    rstd4 = [p.sb(f"rstd4_{s}", [128, 1], F32) for s in range(4)]
    ogtm4 = [p.sb(f"ogtm4_{s}", [128, 256], BF16) for s in range(4)]

    def xn_b2t(tc):
        return [xn[k].ap.rearrange("p (b t) -> p b t", b=2)[:, :, tc * 128:(tc + 1) * 128]
                for k in range(8)]

    for hp in range(2):
        Wkp = load_wpair(wbig[0], "L0_attn_Wk", hp)
        Wvp = load_wpair(wbig[1], "L0_attn_Wv", hp)
        Wgp = load_wpair(wbig[2], "L0_attn_Wg", hp)
        Wqp = load_wpair(wbig[3], "L0_attn_Wq", hp)
        for kk in range(4):
            p.dma("gpsimd", wo_t[kk][:, :],
                  inp["L0_attn_Wo"][(hp * 4 + kk) * 128:(hp * 4 + kk + 1) * 128, :], w=[wo_t[kk]])
            p.tss("vector", wo_t[kk][:, :], wo_t[kk][:, :], gnc[0:128, kk % 2:kk % 2 + 1],
                  ALU.mult, r=[wo_t[kk], gnc], w=[wo_t[kk]])
        for s2 in range(4):
            b, hh = s2 // 2, s2 % 2
            bh = b * 4 + hp * 2 + hh
            if prefix == "cc":
                rdeps = ccdeps
                p.dma("sync", KstF4[s2][:, :], ccKo[0:128, bh * 128:(bh + 1) * 128],
                      r=rdeps, w=[KstF4[s2]])
                p.dma("gpsimd", KstB4[s2][:, :], ccKo[0:128, bh * 128:(bh + 1) * 128],
                      r=rdeps, w=[KstB4[s2]])
                p.dma("sync", VstF4[s2][:, :], ccVzo[0:64, bh * 257:bh * 257 + 256],
                      r=rdeps, w=[VstF4[s2]])
                p.dma("gpsimd", VstB4[s2][:, :], ccVzo[0:64, bh * 257:bh * 257 + 256],
                      r=rdeps, w=[VstB4[s2]])
                p.dma("sync", zvc4[s2][0:64, :], ccVzo[0:64, bh * 257 + 256:bh * 257 + 257],
                      r=rdeps, w=[zvc4[s2]], slow=True)
                p.dma("sync", zkc4[s2][0:64, :], cczko[0:64, bh:bh + 1], r=rdeps,
                      w=[zkc4[s2]], slow=True)
            else:
                p.cp("vector", KstF4[s2][:, :], preK[:, bh * 128:(bh + 1) * 128], r=[preK], w=[KstF4[s2]])
                p.cp("vector", KstB4[s2][:, :], preK[:, bh * 128:(bh + 1) * 128], r=[preK], w=[KstB4[s2]])
                p.cp("vector", VstF4[s2][:, :], preVz[0:64, bh * 257:bh * 257 + 256], r=[preVz], w=[VstF4[s2]])
                p.cp("vector", VstB4[s2][:, :], preVz[0:64, bh * 257:bh * 257 + 256], r=[preVz], w=[VstB4[s2]])
                p.cp("vector", zvc4[s2][0:64, :], preVz[0:64, bh * 257 + 256:bh * 257 + 257],
                     r=[preVz], w=[zvc4[s2]])
                p.cp("vector", zkc4[s2][0:64, :], prezk[0:64, bh:bh + 1], r=[prezk], w=[zkc4[s2]])
        for tc in range(8):
            jj = tc % 2
            ts = slice(tc * 128, (tc + 1) * 128)
            rhsb = xn_b2t(tc)
            if scanstop <= 11:
                continue
            for b in range(2):
                cs = slice(b * TC + tc * 128, b * TC + (tc + 1) * 128)
                pk = psf[b]
                for k in range(8):
                    p.mm(pk[:, 0:512], xn[k][:, cs], Wkp[:, k * 512:(k + 1) * 512],
                         start=(k == 0), stop=(k == 7), r=[xn[k], Wkp], w=[pk])
                P4 = pk.ap[:, 0:512].rearrange("p (hh half t) -> p hh half t", hh=2, half=2)
                K4 = ktm2[0][b].ap.rearrange("p (hh half t) -> p hh half t", hh=2, half=2)
                cosb = cos_s.ap[:, ts].unsqueeze(1).to_broadcast([128, 2, 128])
                sinb = sin_s.ap[:, ts].unsqueeze(1).to_broadcast([128, 2, 128])
                ra = rt4[0]
                rav = ra.ap.rearrange("p (hh t) -> p hh t", hh=2)
                p.tt("vector", rav[:, :, 0:128], P4[:, :, 0, :], cosb, ALU.mult, r=[pk, cos_s], w=[ra])
                p.tt("vector", rav[:, :, 128:256], P4[:, :, 1, :], sinb, ALU.mult, r=[pk, sin_s], w=[ra])
                p.tt("vector", K4[:, :, 0, :], rav[:, :, 0:128], rav[:, :, 128:256], ALU.subtract,
                     r=[ra], w=[ktm2[0][b]])
                p.tt("vector", rav[:, :, 0:128], P4[:, :, 1, :], cosb, ALU.mult, r=[pk, cos_s], w=[ra])
                p.tt("vector", rav[:, :, 128:256], P4[:, :, 0, :], sinb, ALU.mult, r=[pk, sin_s], w=[ra])
                p.tt("vector", K4[:, :, 1, :], rav[:, :, 0:128], rav[:, :, 128:256], ALU.add,
                     r=[ra], w=[ktm2[0][b]])
                pv = psf[2 + b]
                for k in range(8):
                    p.mm(pv[:, 0:512], xn[k][:, cs], Wvp[:, k * 512:(k + 1) * 512],
                         start=(k == 0), stop=(k == 7), r=[xn[k], Wvp], w=[pv])
                p.cp("vector", vtm2[0][b][:, :], pv[:, 0:512], r=[pv], w=[vtm2[0][b]])
            if scanstop <= 12:
                continue
            for b in range(2):
                cs = slice(b * TC + tc * 128, b * TC + (tc + 1) * 128)
                pg = psf[4 + b]
                for k in range(8):
                    p.mm(pg[:, 0:512], xn[k][:, cs], Wgp[:, k * 512:(k + 1) * 512],
                         start=(k == 0), stop=(k == 7), r=[xn[k], Wgp], w=[pg])
                p.act(gsil2[0][b][:, :], pg[:, 0:512], AF.Silu, r=[pg], w=[gsil2[0][b]])
            for hh in range(2):
                pq = psf[hh]
                for dt_ in range(2):
                    for k in range(8):
                        p.mm(pq[:, dt_ * 256:(dt_ + 1) * 256],
                             Wqp[:, k * 512 + hh * 256 + dt_ * 128:k * 512 + hh * 256 + (dt_ + 1) * 128],
                             rhsb[k],
                             start=(dt_ == 0 and k == 0), stop=(dt_ == 1 and k == 7),
                             r=[Wqp, xn[k]], w=[pq])
                P4 = pq.ap[:, 0:512].rearrange("p (dt b t) -> p dt b t", dt=2, b=2)
                Q4 = qfm4[jj][hh].ap.rearrange("p (dt b t) -> p dt b t", dt=2, b=2)
                cosTb = cosT_s.ap[:, ts].unsqueeze(1).to_broadcast([128, 2, 128])
                sinTb = sinT_s.ap[:, ts].unsqueeze(1).to_broadcast([128, 2, 128])
                ra = rt4[0]
                rav = ra.ap.rearrange("p (x t) -> p x t", x=2)
                p.tt("vector", rav[:, :, 0:128], P4[:, 0, :, :], cosTb, ALU.mult, r=[pq, cosT_s], w=[ra])
                p.tt("vector", rav[:, :, 128:256], P4[:, 1, :, :], sinTb, ALU.mult, r=[pq, sinT_s], w=[ra])
                p.tt("vector", Q4[:, 0, :, :], rav[:, :, 0:128], rav[:, :, 128:256], ALU.subtract,
                     r=[ra], w=[qfm4[jj][hh]])
                p.tt("vector", rav[:, :, 0:128], P4[:, 1, :, :], cosTb, ALU.mult, r=[pq, cosT_s], w=[ra])
                p.tt("vector", rav[:, :, 128:256], P4[:, 0, :, :], sinTb, ALU.mult, r=[pq, sinT_s], w=[ra])
                p.tt("vector", Q4[:, 1, :, :], rav[:, :, 0:128], rav[:, :, 128:256], ALU.add,
                     r=[ra], w=[qfm4[jj][hh]])
            if scanstop <= 13:
                continue
            for b in range(2):
                cs = slice(b * TC + tc * 128, b * TC + (tc + 1) * 128)
                pw = psf[2 + b]
                for hh in range(2):
                    h = hp * 2 + hh
                    p.mm(pw[0:64, hh * 128:(hh + 1) * 128], skw2[:, h * 64:(h + 1) * 64],
                         t1k[:, cs], start=(hh == 0), stop=False, r=[skw2, t1k], w=[pw])
                    p.mm(pw[0:64, 256 + hh * 128:256 + (hh + 1) * 128], svw2[:, h * 64:(h + 1) * 64],
                         t1v[:, cs], start=False, stop=(hh == 1), r=[svw2, t1v], w=[pw])
                wtk = rt4[b % 2]
                for hh in range(2):
                    h = hp * 2 + hh
                    p.ts2("vector", wtk[0:64, hh * 128:(hh + 1) * 128],
                          pw[0:64, hh * 128:(hh + 1) * 128], skb2c[0:64, h:h + 1], 32.0,
                          ALU.add, ALU.min, r=[pw, skb2c], w=[wtk])
                    p.ts2("vector", wtk[0:64, 256 + hh * 128:256 + (hh + 1) * 128],
                          pw[0:64, 256 + hh * 128:256 + (hh + 1) * 128], svb2c[0:64, h:h + 1], 32.0,
                          ALU.add, ALU.min, r=[pw, svb2c], w=[wtk])
                p.tss("vector", wtk[0:64, 0:512], wtk[0:64, 0:512], -32.0, ALU.max, r=[wtk], w=[wtk])
                for hh in range(2):
                    s2 = b * 2 + hh
                    p.act(wkf4[0][s2][0:64, :], wtk[0:64, hh * 128:(hh + 1) * 128], AF.Exp,
                          r=[wtk], w=[wkf4[0][s2]])
                    p.act(wvf4[0][s2][0:64, :], wtk[0:64, 256 + hh * 128:256 + (hh + 1) * 128],
                          AF.Exp, r=[wtk], w=[wvf4[0][s2]])
                zk_ = zkb[0][b]
                zv_ = zvb[0][b]
                for hh in range(2):
                    s2 = b * 2 + hh
                    p.scan_cumsum(zk_[hh * 64:(hh + 1) * 64, :], wkf4[0][s2][0:64, :],
                                  zkc4[s2][0:64, 0:1], r=[wkf4[0][s2], zkc4[s2]], w=[zk_])
                    p.cp("vector", zkc4[s2][0:64, :], zk_[hh * 64:(hh + 1) * 64, 127:128],
                         r=[zk_], w=[zkc4[s2]])
                    p.scan_cumsum(zv_[hh * 64:(hh + 1) * 64, :], wvf4[0][s2][0:64, :],
                                  zvc4[s2][0:64, 0:1], r=[wvf4[0][s2], zvc4[s2]], w=[zv_])
                    p.cp("vector", zvc4[s2][0:64, :], zv_[hh * 64:(hh + 1) * 64, 127:128],
                         r=[zv_], w=[zvc4[s2]])
                p.recip(zk_[:, :], zk_[:, :], r=[zk_], w=[zk_])
                p.recip(zv_[:, :], zv_[:, :], r=[zv_], w=[zv_])
                prz = psf[2 + b]
                p.tr(prz[:, 0:128], zk_[:, :], identf[:, :], r=[zk_, identf], w=[prz])
                p.cp("vector", rzT2[0][b][:, 0:128], prz[:, 0:128], r=[prz], w=[rzT2[0][b]])
                p.tr(prz[:, 128:256], zv_[:, :], identf[:, :], r=[zv_, identf], w=[prz])
                p.cp("vector", rzT2[0][b][:, 128:256], prz[:, 128:256], r=[prz], w=[rzT2[0][b]])
            if scanstop <= 14:
                continue
            for b in range(2):
                pt = PBA if b == 0 else PBB
                for hh in range(2):
                    s2 = b * 2 + hh
                    p.trp(pt[:, hh * 64:(hh + 1) * 64], wkf4[0][s2][0:64, :], identb[0:64, 0:64],
                          start=(hh == 0), stop=False, r=[wkf4[0][s2], identb], w=[pt])
                    p.trp(pt[:, 128 + hh * 64:128 + (hh + 1) * 64], wvf4[0][s2][0:64, :],
                          identb[0:64, 0:64], start=False, stop=(hh == 1),
                          r=[wvf4[0][s2], identb], w=[pt])
                eng_b = "vector" if b == 0 else "scalar"
                p.cp(eng_b, wktm2[0][b][:, :], pt[:, 0:128], r=[pt], w=[wktm2[0][b]])
                p.cp(eng_b, wvtm2[0][b][:, :], pt[:, 128:256], r=[pt], w=[wvtm2[0][b]])
            if scanstop <= 15:
                continue
            for b in range(2):
                pt = PBA if b == 0 else PBB
                for c4 in range(4):
                    p.trp(pt[:, c4 * 128:(c4 + 1) * 128], ktm2[0][b][:, c4 * 128:(c4 + 1) * 128],
                          identb[:, :], start=(c4 == 0), stop=(c4 == 3),
                          r=[ktm2[0][b], identb], w=[pt])
                p.cp("vector", kfm4[jj][b][:, :], pt[:, 0:512], r=[pt], w=[kfm4[jj][b]])

            def S(s2):
                return s2 // 2, s2 % 2
            if scanstop <= 1:
                continue
            for s2 in range(4):
                b, hh = S(s2)
                pst = psf[s2]
                for dt_ in range(2):
                    p.mm(pst[:, 0:128], kfm4[jj][b][:, hh * 256 + dt_ * 128:hh * 256 + (dt_ + 1) * 128],
                         qfm4[jj][hh][:, dt_ * 256 + b * 128:dt_ * 256 + (b + 1) * 128],
                         start=(dt_ == 0), stop=(dt_ == 1), r=[kfm4[jj][b], qfm4[jj][hh]], w=[pst])
            for s2 in range(4):
                p.tt("vector", st4[s2][:, :], psf[s2][:, 0:128], utmask[:, :], ALU.mult,
                     r=[psf[s2], utmask], w=[st4[s2]])
            for s2 in range(4):
                b, hh = S(s2)
                plg = psf[4 + s2 % 2]
                off = (s2 // 2) * 128
                for dt_ in range(2):
                    p.mm(plg[:, off + 0:off + 64],
                         qfm4[jj][hh][:, dt_ * 256 + b * 128:dt_ * 256 + (b + 1) * 128],
                         KstB4[s2][:, dt_ * 64:(dt_ + 1) * 64],
                         start=(dt_ == 0 and s2 < 2), stop=False,
                         r=[qfm4[jj][hh], KstB4[s2]], w=[plg])
                p.mm(plg[:, off + 0:off + 64], st4[s2][:, :],
                     wktm2[0][b][:, hh * 64:(hh + 1) * 64],
                     start=False, stop=(s2 >= 2), r=[st4[s2], wktm2[0][b]], w=[plg])
            for s2 in range(4):
                b, hh = S(s2)
                plg = psf[4 + s2 % 2]
                off = (s2 // 2) * 128
                p.stt("vector", lg4[s2][:, :], plg[:, off:off + 64], SCALE,
                      rzT2[0][b][:, hh * 64:(hh + 1) * 64], ALU.mult, ALU.mult,
                      r=[plg, rzT2[0][b]], w=[lg4[s2]])
            for s2 in range(4):
                b, hh = S(s2)
                p.red("vector", nmx4[s2][:, :], lg4[s2][:, :], ALU.max, negate=True,
                      r=[lg4[s2]], w=[nmx4[s2]])
                p.act(et4[s2][:, :], lg4[s2][:, :], AF.Exp, bias=nmx4[s2][0:128, 0:1],
                      accum_ap=sume4[s2][:, :], r=[lg4[s2], nmx4[s2]], w=[et4[s2], sume4[s2]])
                p.recip(sume4[s2][:, :], sume4[s2][:, :], r=[sume4[s2]], w=[sume4[s2]])
                p.stt("vector", pttm4[s2][:, :], et4[s2][:, :], sume4[s2][0:128, 0:1],
                      rzT2[0][b][:, 128 + hh * 64:128 + (hh + 1) * 64], ALU.mult, ALU.mult,
                      r=[et4[s2], sume4[s2], rzT2[0][b]], w=[pttm4[s2]])
            for b in range(2):
                pt = PBA if b == 0 else PBB
                for hh in range(2):
                    s2 = b * 2 + hh
                    p.trp(pt[0:64, hh * 128:(hh + 1) * 128], pttm4[s2][:, :], identb[:, :],
                          start=(hh == 0), stop=(hh == 1), r=[pttm4[s2], identb], w=[pt])
                for hh in range(2):
                    s2 = b * 2 + hh
                    p.cp("vector" if b == 0 else "scalar", ptfm4[s2][:, :],
                         pt[0:64, hh * 128:(hh + 1) * 128], r=[pt], w=[ptfm4[s2]])
            if scanstop <= 2:
                continue
            for s2 in range(4):
                b, hh = S(s2)
                pat = psf[s2]
                p.mm(pat[:, 0:128], wvf4[0][s2][0:64, :], ptfm4[s2][:, :],
                     start=True, stop=True, r=[wvf4[0][s2], ptfm4[s2]], w=[pat])
            for s2 in range(4):
                p.tt("vector", at4[s2][:, :], psf[s2][:, 0:128], utmask[:, :], ALU.mult,
                     r=[psf[s2], utmask], w=[at4[s2]])
            for s2 in range(4):
                b, hh = S(s2)
                po = psf[4 + s2 % 2]
                off = (s2 // 2) * 256
                p.mm(po[:, off:off + 256], ptfm4[s2][:, :], VstB4[s2][:, :],
                     start=(s2 < 2), stop=False, r=[ptfm4[s2], VstB4[s2]], w=[po])
                p.mm(po[:, off:off + 256], at4[s2][:, :],
                     vtm2[0][b].ap[:, hh * 256:(hh + 1) * 256], start=False, stop=(s2 >= 2),
                     r=[at4[s2], vtm2[0][b]], w=[po])
            for s2 in range(4):
                b, hh = S(s2)
                po = psf[4 + s2 % 2]
                off = (s2 // 2) * 256
                p.act(sqs4[0][:, :], po[:, off:off + 256], AF.Square,
                      accum_ap=ssq4[s2][:, :], r=[po], w=[sqs4[0], ssq4[s2]])
                p.act(rstd4[s2][:, :], ssq4[s2][:, :], AF.Sqrt, bias=EPS, scale=1.0 / 256.0,
                      r=[ssq4[s2]], w=[rstd4[s2]])
                p.recip(rstd4[s2][:, :], rstd4[s2][:, :], r=[rstd4[s2]], w=[rstd4[s2]])
                p.stt("vector", ogtm4[s2][:, :], po[:, off:off + 256], rstd4[s2][0:128, 0:1],
                      gsil2[0][b].ap[:, hh * 256:(hh + 1) * 256], ALU.mult, ALU.mult,
                      r=[po, rstd4[s2], gsil2[0][b]], w=[ogtm4[s2]])
            for b in range(2):
                pt = PBA if b == 0 else PBB
                for hh in range(2):
                    s2 = b * 2 + hh
                    for dt_ in range(2):
                        p.trp(pt[:, hh * 256 + dt_ * 128:hh * 256 + (dt_ + 1) * 128],
                              ogtm4[s2][:, dt_ * 128:(dt_ + 1) * 128], identb[:, :],
                              start=(hh == 0 and dt_ == 0), stop=(hh == 1 and dt_ == 1),
                              r=[ogtm4[s2], identb], w=[pt])
                for hh in range(2):
                    s2 = b * 2 + hh
                    for dt_ in range(2):
                        p.cp("vector" if b == 0 else "scalar",
                             og4[b * 4 + hh * 2 + dt_][:, (tc % 2) * 128:(tc % 2 + 1) * 128],
                             pt[:, hh * 256 + dt_ * 128:hh * 256 + (dt_ + 1) * 128],
                             r=[pt], w=[og4[b * 4 + hh * 2 + dt_]])
            if scanstop <= 3:
                continue
            for s2 in range(4):
                b, hh = S(s2)
                pk4 = psf[s2]
                p.mm(pk4[:, 0:64], ktm2[0][b].ap[:, hh * 256:hh * 256 + 128],
                     wktm2[0][b][:, hh * 64:(hh + 1) * 64],
                     start=True, stop=False, r=[ktm2[0][b], wktm2[0][b]], w=[pk4])
                p.mm(pk4[:, 64:128], ktm2[0][b].ap[:, hh * 256 + 128:hh * 256 + 256],
                     wktm2[0][b][:, hh * 64:(hh + 1) * 64],
                     start=False, stop=True, r=[ktm2[0][b], wktm2[0][b]], w=[pk4])
            for s2 in range(4):
                p.tt("vector", KstF4[s2][:, :], KstF4[s2][:, :], psf[s2][:, 0:128], ALU.add,
                     r=[KstF4[s2], psf[s2]], w=[KstF4[s2]])
                p.cp("vector", KstB4[s2][:, :], KstF4[s2][:, :], r=[KstF4[s2]], w=[KstB4[s2]])
            for s2 in range(4):
                b, hh = S(s2)
                pv5 = psf[4 + s2 % 2]
                off = (s2 // 2) * 256
                p.mm(pv5[0:64, off:off + 256], wvtm2[0][b][:, hh * 64:(hh + 1) * 64],
                     vtm2[0][b].ap[:, hh * 256:(hh + 1) * 256],
                     start=(s2 < 2), stop=(s2 >= 2), r=[wvtm2[0][b], vtm2[0][b]], w=[pv5])
            for s2 in range(4):
                pv5 = psf[4 + s2 % 2]
                off = (s2 // 2) * 256
                p.tt("vector", VstF4[s2][:, :], VstF4[s2][:, :], pv5[0:64, off:off + 256], ALU.add,
                     r=[VstF4[s2], pv5], w=[VstF4[s2]])
                p.cp("vector", VstB4[s2][:, :], VstF4[s2][:, :], r=[VstF4[s2]], w=[VstB4[s2]])
            if tc % 2 == 1 and scanstop > 4:
                for b in range(2):
                    for dt_ in range(8):
                        pw = psf[(dt_) % 4]
                        for kk in range(4):
                            p.mm(pw[:, 0:256], wo_t[kk][:, dt_ * 128:(dt_ + 1) * 128],
                                 og4[b * 4 + kk][:, 0:256],
                                 start=(kk == 0), stop=(kk == 3), r=[wo_t[kk], og4[b * 4 + kk]], w=[pw])
                        ms = slice(b * TC + (tc - 1) * 128, b * TC + (tc + 1) * 128)
                        p.tt("vector", acc[dt_][:, ms], acc[dt_][:, ms], pw[:, 0:256], ALU.add,
                             r=[acc[dt_], pw], w=[acc[dt_]])
    p.pop_scope()
    p.pop_scope()
    p.pop_scope()
    dump_fm("x1", acc)
    if upto <= 6:
        return finish(p, nc, outs, inp)

    layernorm_fm(acc, xn, "ln1")
    dump_fm("xn1", xn)

    p.push_scope()
    def bias_cols(name, tag):
        br = p.sb(f"br_{tag}", [8, 128], F32)
        p.dma("sync", br[:, :], inp[name].rearrange("(a b) -> a b", b=128), w=[br])
        bc = p.sb(f"bc_{tag}", [128, 8], F32)
        p.tr(psf[5][0:128, 0:8], br[:, :], identf[0:8, 0:8], r=[br, identf], w=[psf[5]])
        p.cp("vector", bc[:, :], psf[5][0:128, 0:8], r=[psf[5]], w=[bc])
        return bc

    garowfm = p.sb("garowfm", [2, 16], F32)
    ga_in = inp["L1_attn_grep_a"].rearrange("a b c d -> (a c d) b")
    for t in range(2):
        dstv = garowfm.ap[t:t + 1, :].rearrange("a (h two) -> a h two", two=2)
        for j in range(2):
            p.dma("gpsimd", dstv[:, :, j:j + 1].squeeze(2), ga_in[0:1, t * 8:(t + 1) * 8],
                  w=[garowfm], slow=True)
    BUCKETS = [[0, 17, 18], [1, 0, 17]]
    pbrow6 = p.sb("pbrow6", [6, 16], F32)
    for js in range(3):
        for t in range(2):
            r_ = js * 2 + t
            dstv = pbrow6.ap[r_:r_ + 1, :].rearrange("a (h two) -> a h two", two=2)
            for parity in range(2):
                bk = BUCKETS[parity][js]
                p.dma("gpsimd", dstv[:, :, parity:parity + 1].squeeze(2),
                      inp["L1_attn_rel_emb"][bk:bk + 1, t * 8:(t + 1) * 8],
                      w=[pbrow6], slow=True)

    qbc = bias_cols("L1_attn_q_b", "qb")
    kbc = bias_cols("L1_attn_k_b", "kb")
    vbc = bias_cols("L1_attn_v_b", "vb")
    obc = bias_cols("L1_attn_out_b", "ob")

    dupm = p.sb("dupm_s", [128, 256], BF16)
    selm = p.sb("selm_s", [128, 512], BF16)
    segm = p.sb("segm_s", [128, 128], BF16)
    gmaskm = p.sb("gmaskm_s", [128, 128], F32)
    bcP = p.sb("bcP", [16, 2112], BF16)
    p.dma("sync", dupm[:, :], inp["dupm"], w=[dupm])
    p.dma("sync", selm[:, :], inp["selm"], w=[selm])
    p.dma("sync", segm[:, :], inp["segm"], w=[segm])
    p.dma("sync", gmaskm[:, :], inp["gmaskm"], w=[gmaskm])
    p.dma("sync", bcP[:, :], inp["bcastP"], w=[bcP])

    def bias_fm(name, tag):
        b64 = p.sb(f"b64_{tag}", [16, 64], F32)
        p.dma("sync", b64[:, :], inp[name].rearrange("a b c -> (a b) c"), w=[b64])
        p.tr(psf[5][0:64, 0:16], b64[:, :], identf[0:16, 0:16], r=[b64, identf], w=[psf[5]])
        bT = p.sb(f"bT_{tag}", [64, 16], BF16)
        p.cp("vector", bT[:, :], psf[5][0:64, 0:16], r=[psf[5]], w=[bT])
        p.mm(psf[4][0:128, 0:16], dupm[0:64, 0:128], bT[:, :], start=True, stop=True,
             r=[dupm, bT], w=[psf[4]])
        bfm = p.sb(f"bfm_{tag}", [128, 16], F32)
        p.cp("vector", bfm[:, :], psf[4][0:128, 0:16], r=[psf[4]], w=[bfm])
        return bfm

    kbias_fm = bias_fm("L1_attn_bias_k", "kb")
    vbias_fm = bias_fm("L1_attn_bias_v", "vb")

    gwsb = p.sb("gwsb", [64, 8], F32)
    p.dma("sync", gwsb[:, :], inp["L1_attn_grep_w"], w=[gwsb])
    gws = p.sb("gws", [64, 1], F32)
    p.red("vector", gws[:, :], gwsb[:, :], ALU.add, r=[gwsb], w=[gws])
    gws16 = p.sb("gws16", [64, 1], BF16)
    p.cp("vector", gws16[:, :], gws[:, :], r=[gws], w=[gws16])
    p.mm(psf[4][0:128, 0:1], dupm[0:64, 0:128], gws16[:, :], start=True, stop=True,
         r=[dupm, gws16], w=[psf[4]])
    gws2 = p.sb("gws2", [128, 1], F32)
    p.cp("vector", gws2[:, :], psf[4][0:128, 0:1], r=[psf[4]], w=[gws2])
    gsegs = p.sb("gsegs", [128, 128], BF16)
    p.tss("vector", gsegs[:, :], gmaskm[:, :], gws2[0:128, 0:1], ALU.mult,
          r=[gmaskm, gws2], w=[gsegs])
    gbr = p.sb("gbr", [1, 8], F32)
    p.dma("sync", gbr[:, :], inp["L1_attn_grep_b"].rearrange("(a b) -> a b", a=1), w=[gbr])
    gbs = p.sb("gbs", [1, 1], F32)
    p.red("vector", gbs[:, :], gbr[:, :], ALU.add, r=[gbr], w=[gbs])
    gb_col = p.sb("gb_col", [128, 1], F32)
    p.mm(psf[5][:, 0:1], onesf[0:1, :], gbs[0:1, :], start=True, stop=True, r=[onesf, gbs], w=[psf[5]])
    p.cp("vector", gb_col[:, :], psf[5][:, 0:1], r=[psf[5]], w=[gb_col])

    ga2 = p.sb("ga2", [16, 2], F32)
    pb_fm = p.sb("pb_fm", [16, 6], F32)

    def a1_gapb_setup():
        p.tr(psf[5][0:16, 0:2], garowfm[:, :], identf[0:2, 0:2], r=[garowfm, identf], w=[psf[5]])
        p.cp("vector", ga2[:, :], psf[5][0:16, 0:2], r=[psf[5]], w=[ga2])
        p.tr(psf[5][0:16, 0:6], pbrow6[:, :], identf[0:6, 0:6], r=[pbrow6, identf], w=[psf[5]])
        p.cp("vector", pb_fm[:, :], psf[5][0:16, 0:6], r=[psf[5]], w=[pb_fm])

    wq1 = [p.sb(f"wq1_{k}", [128, D], BF16) for k in range(8)]
    wk1 = [p.sb(f"wk1_{k}", [128, D], BF16) for k in range(8)]
    wv1 = [p.sb(f"wv1_{k}", [128, D], BF16) for k in range(8)]
    outw = [p.sb(f"outw{k}", [128, D], BF16) for k in range(8)]
    for k in range(8):
        p.dma("gpsimd", wq1[k][:, :], inp["L1_attn_q_w"][k * 128:(k + 1) * 128, :], w=[wq1[k]])
        p.dma("gpsimd", wk1[k][:, :], inp["L1_attn_k_w"][k * 128:(k + 1) * 128, :], w=[wk1[k]])
        p.dma("gpsimd", wv1[k][:, :], inp["L1_attn_v_w"][k * 128:(k + 1) * 128, :], w=[wv1[k]])
        p.dma("gpsimd", outw[k][:, :], inp["L1_attn_out_w"][k * 128:(k + 1) * 128, :], w=[outw[k]])

    qt = [p.sb(f"a1qt{ft}", [128, 256], BF16) for ft in range(8)]
    kt = [p.sb(f"a1kt{ft}", [128, 256], BF16) for ft in range(8)]
    vt = [[p.sb(f"a1vt{pr_}_{ft}", [128, 256], BF16) for ft in range(8)] for pr_ in range(2)]
    prodk = [p.sb(f"a1pk{i}", [128, 384], BF16) for i in range(3)]
    attfm = p.sb("a1attfm", [128, 2048], BF16)
    etmpB = [p.sb(f"a1etmp{i}", [128, 256], BF16) for i in range(2)]
    vselsb = [p.sb(f"a1vsel{i}", [128, 128], BF16) for i in range(2)]
    lg3 = p.sb("a1lg3", [16, 768], F32)
    glsb = p.sb("a1glsb", [16, 256], F32)
    mx2 = p.sb("a1mx2", [16, 256], F32)
    pj3 = [p.sb(f"a1pj{i}", [16, 768], BF16) for i in range(2)]

    def xn_b2(st_i):
        return [xn[k].ap.rearrange("p (b t) -> p b t", b=2)[:, :, st_i * 128:(st_i + 1) * 128]
                for k in range(8)]

    def a1_proj(wt, bc, dst, st_i):
        rhsv = xn_b2(st_i)
        for ft in range(8):
            pp = psf[ft % 2]
            for k in range(8):
                p.mm(pp[:, 0:256], wt[k][:, ft * 128:(ft + 1) * 128], rhsv[k],
                     start=(k == 0), stop=(k == 7), r=[wt[k], xn[k]], w=[pp])
            if ft % 2 == 0:
                p.tss("vector", dst[ft][:, :], pp[:, 0:256], bc[0:128, ft:ft + 1],
                      ALU.add, r=[pp, bc], w=[dst[ft]])
            else:
                p.act(dst[ft][:, :], pp[:, 0:256], AF.Identity, bias=bc[0:128, ft:ft + 1],
                      r=[pp, bc], w=[dst[ft]])

    def a1_kdup(t, ft):
        hp = t * 8 + ft
        idx = t * 8 + ft
        bank = psf[2 + idx % 2]
        ndata = max(0, min(3, 32 - 3 * hp))
        eng = "vector"
        for js in range(ndata):
            ck = 3 * hp + js
            tk, hk = ck // 16, ck % 16
            p.mm(bank[:, js * 128:(js + 1) * 128],
                 dupm[:, (hk % 2) * 128:(hk % 2 + 1) * 128],
                 kt[hk // 2][:, tk * 128:(tk + 1) * 128],
                 start=(js == 0), stop=(js == ndata - 1), r=[dupm, kt[hk // 2]], w=[bank])
        q3 = qt[ft].ap[:, t * 128:(t + 1) * 128]
        pkb = prodk[(t * 8 + ft) % 3]
        pk3 = pkb.ap[:, 0:384].rearrange("p (j s) -> p j s", s=128)
        if ndata == 3:
            p.tt(eng, pk3, q3.unsqueeze(1).to_broadcast([128, 3, 128]),
                 bank.ap[:, 0:384].rearrange("p (j s) -> p j s", s=128),
                 ALU.mult, r=[qt[ft], bank], w=[pkb])
        elif ndata == 0:
            cb0 = 3 * hp - 32
            p.tt(eng, pk3, q3.unsqueeze(1).to_broadcast([128, 3, 128]),
                 kbias_fm.ap[:, cb0:cb0 + 3].unsqueeze(2).to_broadcast([128, 3, 128]),
                 ALU.mult, r=[qt[ft], kbias_fm], w=[pkb])
        else:
            p.tt(eng, pkb.ap[:, 0:256].rearrange("p (j s) -> p j s", s=128),
                 q3.unsqueeze(1).to_broadcast([128, 2, 128]),
                 bank.ap[:, 0:256].rearrange("p (j s) -> p j s", s=128),
                 ALU.mult, r=[qt[ft], bank], w=[pkb])
            cb = 3 * hp + 2 - 32
            p.tss(eng, pkb[:, 256:384], q3, kbias_fm[0:128, cb:cb + 1],
                  ALU.mult, r=[qt[ft], kbias_fm], w=[pkb])

    def a1_seg(t, ft):
        st_g = (t == 0 and ft == 0)
        sp_g = (t == 1 and ft == 7)
        for js in range(3):
            if js == 0:
                area, c0 = psf[4], t * 128
            elif js == 1:
                area, c0 = psf[5], t * 128
            else:
                area, c0 = psf[0], 256 + t * 128
            pkb = prodk[(t * 8 + ft) % 3]
            p.mm(area[0:16, c0:c0 + 128], segm[:, ft * 16:(ft + 1) * 16],
                 pkb[:, js * 128:(js + 1) * 128],
                 start=st_g, stop=sp_g, r=[segm, pkb], w=[area])
        p.mm(psf[1][0:16, 256 + t * 128:256 + (t + 1) * 128],
             gsegs[:, ft * 16:(ft + 1) * 16], qt[ft][:, t * 128:(t + 1) * 128],
             start=st_g, stop=sp_g, r=[gsegs, qt[ft]], w=[psf[1]])

    def a1_projall(st_i):
        a1_proj(wq1, qbc, qt, st_i)
        a1_proj(wk1, kbc, kt, st_i)
        a1_proj(wv1, vbc, vt[st_i % 2], st_i)

    def a1_logits(st_i):
        for t in range(2):
            a1_kdup(t, 0)
            for ft in range(1, 8):
                a1_kdup(t, ft)
                a1_seg(t, ft - 1)
            a1_seg(t, 7)

    def a1_softmax(st_i):
        par = st_i % 2
        p.cp("vector", lg3[:, 0:256], psf[4][0:16, 0:256], r=[psf[4]], w=[lg3])
        p.cp("vector", lg3[:, 256:512], psf[5][0:16, 0:256], r=[psf[5]], w=[lg3])
        p.cp("vector", lg3[:, 512:768], psf[0][0:16, 256:512], r=[psf[0]], w=[lg3])
        p.cp("scalar", glsb[:, :], psf[1][0:16, 256:512], r=[psf[1]], w=[glsb])
        p.act(glsb[:, :], glsb[:, :], AF.Sigmoid, bias=gb_col[0:16, 0:1],
              r=[glsb, gb_col], w=[glsb])
        for t in range(2):
            p.tss("vector", glsb[:, t * 128:(t + 1) * 128], glsb[:, t * 128:(t + 1) * 128],
                  ga2[0:16, t:t + 1], ALU.mult, r=[glsb, ga2], w=[glsb])
        for js in range(3):
            for t in range(2):
                sl = slice(js * 256 + t * 128, js * 256 + (t + 1) * 128)
                p.stt("vector", lg3.ap[:, sl], glsb[:, t * 128:(t + 1) * 128],
                      pb_fm[0:16, js * 2 + t:js * 2 + t + 1], lg3.ap[:, sl],
                      ALU.mult, ALU.add, r=[glsb, pb_fm, lg3], w=[lg3])
        p.tt("vector", mx2[:, :], lg3[:, 0:256], lg3[:, 256:512], ALU.max, r=[lg3], w=[mx2])
        p.tt("vector", mx2[:, :], mx2[:, :], lg3[:, 512:768], ALU.max, r=[lg3, mx2], w=[mx2])
        for js in range(3):
            p.tt("vector", lg3[:, js * 256:(js + 1) * 256], lg3[:, js * 256:(js + 1) * 256],
                 mx2[:, :], ALU.subtract, r=[lg3, mx2], w=[lg3])
        p.act(lg3[:, 0:768], lg3[:, 0:768], AF.Exp, r=[lg3], w=[lg3])
        p.tt("vector", mx2[:, :], lg3[:, 0:256], lg3[:, 256:512], ALU.add, r=[lg3], w=[mx2])
        p.tt("vector", mx2[:, :], mx2[:, :], lg3[:, 512:768], ALU.add, r=[lg3, mx2], w=[mx2])
        p.recip(mx2[:, :], mx2[:, :], r=[mx2], w=[mx2])
        for js in range(3):
            p.tt("vector", pj3[par][:, js * 256:(js + 1) * 256], lg3[:, js * 256:(js + 1) * 256],
                 mx2[:, :], ALU.mult, r=[lg3, mx2], w=[pj3[par]])

    def a1_passB(st_i):
        par = st_i % 2
        vts, pj = vt[par], pj3[par]
        for js in range(3):
            for fo in range(8):
                bank = psf[2 + fo % 2]
                eng = "vector"
                vss = vselsb[fo % 2]
                data = []
                for hh in range(2):
                    ck = 3 * (2 * fo + hh) + js
                    if ck < 32:
                        data.append((hh, ck % 16, ck // 16))
                nmm = 4 + len(data)
                i_mm = 0
                for jt in range(2):
                    for hh in range(2):
                        m = 4 * fo + 2 * hh + jt
                        t, hq = m // 16, m % 16
                        o_ = 128 * hq - 64 * hh + 64
                        p.mm(bank[:, jt * 128:(jt + 1) * 128],
                             bcP[:, o_:o_ + 128],
                             pj[0:16, js * 256 + t * 128:js * 256 + (t + 1) * 128],
                             start=(i_mm == 0), stop=(i_mm == nmm - 1), r=[bcP, pj], w=[bank])
                        i_mm += 1
                vc0 = 256 + (js % 2) * 128
                for (hh, hk, tk) in data:
                    p.mm(bank[:, vc0:vc0 + 128],
                         selm[:, (hh * 2 + hk % 2) * 128:(hh * 2 + hk % 2 + 1) * 128],
                         vts[hk // 2][:, tk * 128:(tk + 1) * 128],
                         start=(i_mm == 0), stop=(i_mm == nmm - 1),
                         r=[selm, vts[hk // 2]], w=[bank])
                    i_mm += 1
                if data:
                    p.cp(eng, vss[:, :], bank[:, vc0:vc0 + 128], r=[bank], w=[vss])
                tbuf = attfm if js == 0 else etmpB[fo % 2]
                toff = fo * 256 if js == 0 else 0
                if a1stop < 6:
                    continue
                if len(data) == 2:
                    dst = tbuf.ap[:, toff:toff + 256].rearrange("p (j s) -> p j s", s=128)
                    p.tt(eng, dst, bank.ap[:, 0:256].rearrange("p (j s) -> p j s", s=128),
                         vss.ap[:, :].unsqueeze(1).to_broadcast([128, 2, 128]),
                         ALU.mult, r=[bank, vss], w=[tbuf])
                else:
                    for hh in range(2):
                        lo, hi = hh * 64, (hh + 1) * 64
                        dst = tbuf.ap[lo:hi, toff:toff + 256].rearrange("p (j s) -> p j s", s=128)
                        src0 = bank.ap[lo:hi, 0:256].rearrange("p (j s) -> p j s", s=128)
                        if data and data[0][0] == hh:
                            in1 = vss.ap[lo:hi, :].unsqueeze(1).to_broadcast([64, 2, 128])
                            p.tt(eng, dst, src0, in1, ALU.mult, r=[bank, vss], w=[tbuf])
                        else:
                            cb = 3 * (2 * fo + hh) + js - 32
                            in1 = vbias_fm.ap[lo:hi, cb:cb + 1].unsqueeze(2).to_broadcast(
                                [64, 2, 128])
                            p.tt(eng, dst, src0, in1, ALU.mult, r=[bank, vbias_fm], w=[tbuf])
                if js > 0:
                    p.tt("gpsimd", attfm[:, fo * 256:(fo + 1) * 256],
                         attfm[:, fo * 256:(fo + 1) * 256], etmpB[fo % 2][:, 0:256],
                         ALU.add, r=[etmpB[fo % 2], attfm], w=[attfm])
        if a1stop < 7:
            return
        for dt_ in range(8):
            pp = psf[4 + dt_ % 2]
            c0 = 0
            for fo in range(8):
                p.mm(pp[:, c0:c0 + 256], outw[fo][:, dt_ * 128:(dt_ + 1) * 128],
                     attfm[:, fo * 256:(fo + 1) * 256],
                     start=(fo == 0), stop=(fo == 7), r=[outw[fo], attfm], w=[pp])
            accv = acc[dt_].ap.rearrange("p (b t) -> p b t", b=2)[:, :, st_i * 128:(st_i + 1) * 128]
            ppv = pp.ap[:, c0:c0 + 256].rearrange("p (b t) -> p b t", b=2)
            p.stt("vector", accv, ppv, obc[0:128, dt_:dt_ + 1],
                  accv, ALU.add, ALU.add, r=[pp, obc, acc[dt_]], w=[acc[dt_]])

    if a1stop >= 2:
        for st_i in range(8):
            a1_projall(st_i)
            if st_i == 0:
                a1_gapb_setup()
            if a1stop >= 5 and st_i > 0:
                a1_passB(st_i - 1)
            if a1stop >= 3:
                a1_logits(st_i)
            if a1stop >= 4:
                a1_softmax(st_i)
        if a1stop >= 5:
            a1_passB(7)
    p.pop_scope()
    dump_fm("attn1", acc)
    if upto <= 8:
        return finish(p, nc, outs, inp)

    p.push_scope()
    msc_fm = p.sb("msc_fm", [16, NT], BF16)
    p.push_scope()
    gwsb2 = p.sb("gwsb2", [8, D], BF16)
    p.dma("gpsimd", gwsb2[:, :], inp["L1_ffn_gate_w"], w=[gwsb2])
    gwT = p.sb("gwT", [128, 64], BF16)
    for k in range(8):
        pt = PBA
        p.tr(pt[0:128, 0:8], gwsb2[0:8, k * 128:(k + 1) * 128], identb[0:8, 0:8],
             r=[gwsb2, identb], w=[pt])
        p.cp("vector", gwT[:, k * 8:(k + 1) * 8], pt[0:128, 0:8], r=[pt], w=[gwT])
    lgm = p.sb("lgm", [128, 8], F32)
    scm = p.sb("scm", [128, 8], F32)
    mx1 = p.sb("mx1", [128, 1], F32)
    m1m = p.sb("m1m", [128, 8], F32)
    tm8 = p.sb("tm8", [128, 8], F32)
    msc = p.sb("msc", [128, 16], F32)
    for tt in range(16):
        cs = slice(tt * 128, (tt + 1) * 128)
        pp = psf[0]
        for k in range(8):
            p.mm(pp[:, 0:8], xn[k][:, cs], gwT[:, k * 8:(k + 1) * 8],
                 start=(k == 0), stop=(k == 7), r=[xn[k], gwT], w=[pp])
        p.cp("vector", lgm[:, :], pp[:, 0:8], r=[pp], w=[lgm])
        p.red("vector", mx1[:, :], lgm[:, :], ALU.max, negate=True, r=[lgm], w=[mx1])
        p.act(scm[:, :], lgm[:, :], AF.Exp, bias=mx1[0:128, 0:1], r=[lgm, mx1], w=[scm])
        p.red("vector", mx1[:, :], scm[:, :], ALU.add, r=[scm], w=[mx1])
        p.recip(mx1[:, :], mx1[:, :], r=[mx1], w=[mx1])
        p.tss("vector", scm[:, :], scm[:, :], mx1[0:128, 0:1], ALU.mult, r=[scm, mx1], w=[scm])
        p.red("vector", mx1[:, :], scm[:, :], ALU.max, r=[scm], w=[mx1])
        p.tss("vector", m1m[:, :], scm[:, :], mx1[0:128, 0:1], ALU.is_ge, r=[scm, mx1], w=[m1m])
        p.stt("vector", tm8[:, :], m1m[:, :], -1e30, scm[:, :], ALU.mult, ALU.add,
              r=[m1m, scm], w=[tm8])
        p.red("vector", mx1[:, :], tm8[:, :], ALU.max, r=[tm8], w=[mx1])
        p.tss("vector", msc[:, 8:16], scm[:, :], mx1[0:128, 0:1], ALU.is_ge, r=[scm, mx1], w=[msc])
        p.tt("vector", msc[:, 0:8], msc[:, 8:16], scm[:, :], ALU.mult, r=[scm, msc], w=[msc])
        p.tr(psf[3][0:16, 0:128], msc[:, :], identf[:, :], r=[msc, identf], w=[psf[3]])
        p.cp("vector", msc_fm[0:16, cs], psf[3][0:16, 0:128], r=[psf[3]], w=[msc_fm])
    p.pop_scope()
    if "moemsc" in dbg:
        p.dma("gpsimd", dbg["moemsc"], msc_fm[:, :], r=[msc_fm])
    mrow = p.sb("mrow", [1, NT], BF16)
    krow = p.sb("krow", [1, NT], BF16)
    m_bc = p.sb("m_bc", [128, NT], F32)
    k_bc = p.sb("k_bc", [128, NT], BF16)
    h1 = [p.sb(f"h1_{i}", [128, NT], BF16) for i in range(8)]
    x3m = [p.sb(f"x3m_{i}", [128, NT], BF16) for i in range(8)]
    wstr2 = [p.sb(f"wstr2_{j}", [128, D], BF16) for j in range(2)]
    w1T = [p.sb(f"w1T_{j}", [128, 8 * 128], BF16) for j in range(2)]
    bcol = p.sb("bcol", [8, 128], F32)
    b1c = p.sb("b1c", [128, 8], F32)
    b2c = p.sb("b2c", [128, 8], F32)
    b3c = p.sb("b3c", [128, 8], F32)
    etmps = [p.sb(f"etmp{i}", [128, 512], F32) for i in range(2)]

    def load_w1T(dst, wname, e, rows, j):
        wn = wstr2[j]
        p.dma("gpsimd", wn[:, :], inp[wname][e, rows:rows + 128, :], w=[wn])
        pt = PBA if j == 0 else PBB
        for k in range(8):
            p.trp(pt[:, k * 128:(k + 1) * 128], wn[:, k * 128:(k + 1) * 128], identb[:, :],
                  start=(k == 0), stop=(k == 7), r=[wn, identb], w=[pt])
        p.cp("vector" if j == 0 else "scalar", dst[:, 0:1024], pt[:, 0:1024], r=[pt], w=[dst])

    def load_bcol(dst, name, e):
        p.dma("sync", bcol[:, :], inp[name][e].rearrange("(a b) -> a b", b=128), w=[bcol])
        p.tr(psf[5][0:128, 0:8], bcol[:, :], identf[0:8, 0:8], r=[bcol, identf], w=[psf[5]])
        p.cp("vector", dst[:, :], psf[5][0:128, 0:8], r=[psf[5]], w=[dst])

    b2c2 = [b2c, p.sb("b2cB", [128, 8], F32)]

    def prep_expert(e):
        p.dma("gpsimd", mrow[0:1, :], msc_fm[e:e + 1, :], r=[msc_fm], w=[mrow])
        p.dma("gpsimd", krow[0:1, :], msc_fm[8 + e:9 + e, :], r=[msc_fm], w=[krow])
        for m2 in range(4):
            ms = slice(m2 * 512, (m2 + 1) * 512)
            pbx = psf[4]
            p.mm(pbx[:, 0:512], onesb[0:1, :], mrow[0:1, ms], start=True, stop=True,
                 r=[onesb, mrow], w=[pbx])
            p.cp("vector", m_bc[:, ms], pbx[:, 0:512], r=[pbx], w=[m_bc])
            pby = psf[5]
            p.mm(pby[:, 0:512], onesb[0:1, :], krow[0:1, ms], start=True, stop=True,
                 r=[onesb, krow], w=[pby])
            p.cp("scalar", k_bc[:, ms], pby[:, 0:512], r=[pby], w=[k_bc])
        load_bcol(b1c, "L1_ffn_b1", e)
        load_bcol(b2c2[e % 2], "L1_ffn_b2", e)
        load_bcol(b3c, "L1_ffn_b3", e)

    prep_expert(0)
    for e in range(8):
        m_bc_, k_bc_ = m_bc, k_bc
        b1c_, b2c_, b3c_ = b1c, b2c2[e % 2], b3c
        load_w1T(w1T[0], "L1_ffn_w1", e, 0, 0)
        for ht in range(8):
            for m2 in range(4):
                ms = slice(m2 * 512, (m2 + 1) * 512)
                pp = psf[0 + 3 * (m2 % 2)]
                for k in range(8):
                    p.mm(pp[:, 0:512], w1T[ht % 2][:, k * 128:(k + 1) * 128], xn[k][:, ms],
                         start=(k == 0), stop=(k == 7), r=[w1T[ht % 2], xn[k]], w=[pp])
                p.tt("vector", etmps[m2 % 2][:, 0:512], pp[:, 0:512], m_bc_[:, ms], ALU.mult,
                     r=[pp, m_bc_], w=[etmps[m2 % 2]])
                p.act(h1[ht][:, ms], etmps[m2 % 2][:, 0:512], AF.Silu, bias=b1c_[0:128, ht:ht + 1],
                      r=[etmps[m2 % 2], b1c_], w=[h1[ht]])
            if ht + 1 < 8:
                load_w1T(w1T[(ht + 1) % 2], "L1_ffn_w1", e, (ht + 1) * 128, (ht + 1) % 2)
        load_w1T(w1T[0], "L1_ffn_w3", e, 0, 0)
        for ht in range(8):
            for m2 in range(4):
                ms = slice(m2 * 512, (m2 + 1) * 512)
                pp = psf[1 + 3 * (m2 % 2)]
                for k in range(8):
                    p.mm(pp[:, 0:512], w1T[ht % 2][:, k * 128:(k + 1) * 128], xn[k][:, ms],
                         start=(k == 0), stop=(k == 7), r=[w1T[ht % 2], xn[k]], w=[pp])
                p.tt("vector", etmps[m2 % 2][:, 0:512], pp[:, 0:512], m_bc_[:, ms], ALU.mult,
                     r=[pp, m_bc_], w=[etmps[m2 % 2]])
                p.stt("vector", x3m[ht][:, ms], k_bc_[:, ms], b3c_[0:128, ht:ht + 1],
                      etmps[m2 % 2][:, 0:512], ALU.mult, ALU.add,
                      r=[k_bc_, b3c_, etmps[m2 % 2]], w=[x3m[ht]])
            if ht + 1 < 8:
                load_w1T(w1T[(ht + 1) % 2], "L1_ffn_w3", e, (ht + 1) * 128, (ht + 1) % 2)
        if e + 1 < 8:
            prep_expert(e + 1)
        load_w1T(w1T[0], "L1_ffn_w2", e, 0, 0)
        for dt_ in range(8):
            for m2 in range(4):
                ms = slice(m2 * 512, (m2 + 1) * 512)
                pp = psf[2 + 3 * (m2 % 2)]
                for k in range(8):
                    p.mm(pp[:, 0:512], w1T[dt_ % 2][:, k * 128:(k + 1) * 128], h1[k][:, ms],
                         start=(k == 0), stop=(k == 7), r=[w1T[dt_ % 2], h1[k]], w=[pp])
                p.stt("vector", etmps[m2 % 2][:, 0:512], pp[:, 0:512], b2c_[0:128, dt_:dt_ + 1],
                      x3m[dt_][:, ms], ALU.add, ALU.mult, r=[pp, b2c_, x3m[dt_]], w=[etmps[m2 % 2]])
                p.tt("vector" if (dt_ + m2) % 2 == 0 else "gpsimd",
                     acc[dt_][:, ms], acc[dt_][:, ms], etmps[m2 % 2][:, 0:512], ALU.add,
                     r=[etmps[m2 % 2], acc[dt_]], w=[acc[dt_]])
            if dt_ + 1 < 8:
                load_w1T(w1T[(dt_ + 1) % 2], "L1_ffn_w2", e, (dt_ + 1) * 128, (dt_ + 1) % 2)
    p.pop_scope()
    dump_fm("moe", acc)
    if upto <= 9:
        return finish(p, nc, outs, inp)

    p.push_scope()
    hh = [p.sb(f"hh{i}", [128, 1024], BF16) for i in range(16)]
    wsn = [p.sb(f"wsn{j}", [128, D], BF16) for j in range(2)]
    wsT = [p.sb(f"wsT{j}", [128, 8 * 128], BF16) for j in range(2)]
    gs2s = [p.sb(f"gs2_{i}", [128, 512], F32) for i in range(2)]

    def load_wsT(dst, wname, rows, cols0, ncols, j):
        wn = wsn[j]
        assert ncols <= 8
        p.dma("gpsimd", wn[:, 0:ncols * 128], inp[wname][rows:rows + 128, cols0:cols0 + ncols * 128],
              w=[wn])
        pt = PBA if j == 0 else PBB
        for k in range(ncols):
            p.trp(pt[:, k * 128:(k + 1) * 128], wn[:, k * 128:(k + 1) * 128], identb[:, :],
                  start=(k == 0), stop=(k == ncols - 1), r=[wn, identb], w=[pt])
        p.cp("vector" if j == 0 else "scalar", dst[:, 0:ncols * 128], pt[:, 0:ncols * 128],
             r=[pt], w=[dst])

    for c2 in range(2):
        tbase = c2 * 1024
        load_wsT(wsT[0], "L1_ffn_sw1", 0, 0, 8, 0)
        load_wsT(wsT[1], "L1_ffn_sw2", 0, 0, 8, 1)
        for ht in range(16):
            for m2 in range(2):
                ms = slice(tbase + m2 * 512, tbase + (m2 + 1) * 512)
                pg = psf[0 + 2 * m2]
                for k in range(8):
                    p.mm(pg[:, 0:512], wsT[0][:, k * 128:(k + 1) * 128], xn[k][:, ms],
                         start=(k == 0), stop=(k == 7), r=[wsT[0], xn[k]], w=[pg])
                p.act(gs2s[m2][:, 0:512], pg[:, 0:512], AF.Silu, r=[pg], w=[gs2s[m2]])
                pu = psf[1 + 2 * m2]
                for k in range(8):
                    p.mm(pu[:, 0:512], wsT[1][:, k * 128:(k + 1) * 128], xn[k][:, ms],
                         start=(k == 0), stop=(k == 7), r=[wsT[1], xn[k]], w=[pu])
                p.tt("vector", hh[ht][:, m2 * 512:(m2 + 1) * 512], pu[:, 0:512], gs2s[m2][:, 0:512],
                     ALU.mult, r=[pu, gs2s[m2]], w=[hh[ht]])
            if ht + 1 < 16:
                load_wsT(wsT[2 + (ht + 1) % 2] if False else wsT[0], "L1_ffn_sw1", (ht + 1) * 128, 0, 8, 0)
                load_wsT(wsT[1], "L1_ffn_sw2", (ht + 1) * 128, 0, 8, 1)
        for half in range(2):
            load_wsT(wsT[half], "L1_ffn_sw3", 0, half * 1024, 8, half)
        for dt_ in range(8):
            for m2 in range(2):
                ms = slice(tbase + m2 * 512, tbase + (m2 + 1) * 512)
                pd = psf[2 + dt_ % 2]
                for k in range(16):
                    wsrc = wsT[k // 8]
                    p.mm(pd[:, 0:512], wsrc[:, (k % 8) * 128:((k % 8) + 1) * 128],
                         hh[k][:, m2 * 512:(m2 + 1) * 512],
                         start=(k == 0), stop=(k == 15), r=[wsrc, hh[k]], w=[pd])
                p.tt("vector", acc[dt_][:, ms], acc[dt_][:, ms], pd[:, 0:512], ALU.add,
                     r=[pd, acc[dt_]], w=[acc[dt_]])
            if dt_ + 1 < 8:
                for half in range(2):
                    load_wsT(wsT[half], "L1_ffn_sw3", (dt_ + 1) * 128, half * 1024, 8, half)
    p.pop_scope()
    dump_fm("zsh", acc)

    p.push_scope()
    onat = p.sb("onat", [128, D], F32)
    for tt in range(16):
        for k in range(8):
            pb = psf[k % 4]
            p.tr(pb[:, 0:128], acc[k][:, tt * 128:(tt + 1) * 128], identf[:, :],
                 r=[acc[k], identf], w=[pb])
            p.cp("vector" if k % 2 == 0 else "scalar", onat[:, k * 128:(k + 1) * 128],
                 pb[:, 0:128], r=[pb], w=[onat])
        p.dma("sync", o_ys[tt * 128:(tt + 1) * 128, :], onat[:, :], r=[onat])
    p.pop_scope()

    return finish(p, nc, outs, inp)


def finish(p, nc, outs, inp):
    p.emit()
    return nc, outs, list(inp.cache)


from concourse.bass_utils import run_bass_kernel_spmd
import ml_dtypes

_T_FULL = 8192


def _rope_tables():
    half = 128
    inv_freq = 1.0 / 10000.0 ** (np.arange(half, dtype=np.float64) / half)
    ang = np.arange(_T_FULL, dtype=np.float64)[:, None] * inv_freq[None, :]
    return np.cos(ang).astype(np.float32), np.sin(ang).astype(np.float32)


def _attn1_consts():
    bf = ml_dtypes.bfloat16
    SC1 = (1.0 / 8.0) / 32.0
    dupm = np.zeros((128, 256), np.float32)
    selm = np.zeros((128, 512), np.float32)
    segm = np.zeros((128, 128), np.float32)
    gmaskm = np.zeros((128, 128), np.float32)
    bcastP = np.zeros((16, 2112), np.float32)
    for e in range(64):
        dupm[e, e] = dupm[e, 64 + e] = 1.0
        dupm[64 + e, 128 + e] = dupm[64 + e, 192 + e] = 1.0
    for dh in range(2):
        for sh in range(2):
            g = dh * 2 + sh
            for e in range(64):
                selm[sh * 64 + e, g * 128 + dh * 64 + e] = 1.0
    for ft in range(8):
        for pp in range(128):
            cc_ = 2 * ft + (1 if pp >= 64 else 0)
            segm[pp, ft * 16 + cc_] = SC1
            gmaskm[pp, ft * 16 + cc_] = 1.0
    for pq in range(16):
        bcastP[pq, 128 * pq + 64:128 * pq + 128] = 1.0
    return {"dupm": dupm.astype(bf), "selm": selm.astype(bf),
            "segm": segm.astype(bf), "gmaskm": gmaskm,
            "bcastP": bcastP.astype(bf)}


def _core_inputs(inputs, c, cos_f, sin_f, prefix_blobs=None):
    x = inputs["L0_attn_hidden_states"]
    sl = slice(c * TC, (c + 1) * TC)
    xs = np.ascontiguousarray(
        np.concatenate([x[0, sl], x[1, sl]], axis=0)).astype(np.float32, copy=False)
    pm = np.zeros((128, 8), np.float32)
    for j in range(8):
        pm[:, j] = 1.0 if c < j else 0.0
    m = {
        "xs": xs,
        "identf": np.eye(128, dtype=np.float32),
        "identb": np.eye(128, dtype=np.float32).astype(ml_dtypes.bfloat16),
        "utmask": np.triu(np.ones((128, 128), np.float32)),
        "cos": np.ascontiguousarray(cos_f[sl]),
        "sin": np.ascontiguousarray(sin_f[sl]),
        "pmask": pm,
    }
    m.update(_attn1_consts())
    for k, v in inputs.items():
        if k != "L0_attn_hidden_states":
            m[k] = np.ascontiguousarray(v)
    if prefix_blobs is not None:
        m["blobsK"], m["blobsVz"], m["blobszk"] = prefix_blobs
    return m


FUSED = True


def kernel(**inputs):
    cos_f, sin_f = _rope_tables()
    if FUSED:
        ncB, outsB, innB = build(dict(prefix="cc"))
        in_mapsB = []
        for c in range(8):
            m = _core_inputs(inputs, c, cos_f, sin_f)
            in_mapsB.append({k: m[k] for k in innB})
        resB = run_bass_kernel_spmd(ncB, in_mapsB, core_ids=list(range(8)))
    else:
        ncA, outsA, innA = build(dict(seg_only=True, prefix="zero"))
        in_mapsA = []
        for c in range(8):
            m = _core_inputs(inputs, c, cos_f, sin_f)
            in_mapsA.append({k: m[k] for k in innA})
        resA = run_bass_kernel_spmd(ncA, in_mapsA, core_ids=list(range(8)))
        blobsK = np.concatenate([resA.results[c]["blobK"] for c in range(8)], axis=0)
        blobsVz = np.concatenate([resA.results[c]["blobVz"] for c in range(8)], axis=0)
        blobszk = np.concatenate([resA.results[c]["blobzk"] for c in range(8)], axis=0)
        ncB, outsB, innB = build(dict(prefix="input"))
        in_mapsB = []
        for c in range(8):
            m = _core_inputs(inputs, c, cos_f, sin_f,
                             prefix_blobs=(blobsK, blobsVz, blobszk))
            in_mapsB.append({k: m[k] for k in innB})
        resB = run_bass_kernel_spmd(ncB, in_mapsB, core_ids=list(range(8)))
    out = np.empty((2, _T_FULL, 1024), np.float32)
    for c in range(8):
        ys = resB.results[c]["ys"]
        out[0, c * TC:(c + 1) * TC] = ys[0:TC]
        out[1, c * TC:(c + 1) * TC] = ys[TC:2 * TC]
    return out


# version 21
# speedup vs baseline: 1.1402x; 1.0154x over previous
import numpy as np
from contextlib import ExitStack
import concourse.bass as bass
import concourse.mybir as mybir

F32 = mybir.dt.float32
BF16 = mybir.dt.bfloat16
I32 = mybir.dt.int32
AF = mybir.ActivationFunctionType
ALU = mybir.AluOpType
AX = mybir.AxisListType

TC = 1024
NT = 2 * TC
H, M, DK = 4, 64, 256
D = 1024
EPS = 1e-5
SCALE = 1.0 / 16.0

ENGINES = ("tensor", "vector", "scalar", "gpsimd", "sync")


class Buf:
    __slots__ = ("t", "writers", "readers", "name")

    def __init__(self, t, name):
        self.t = t
        self.name = name
        self.writers = []
        self.readers = []

    def __getitem__(self, key):
        return self.t.ap()[key] if not isinstance(self.t, bass.AP) else self.t[key]

    @property
    def ap(self):
        return self.t.ap() if not isinstance(self.t, bass.AP) else self.t


class Prog:
    def __init__(self, nc):
        self.nc = nc
        self.ops = []
        self.NDMASEM = 8
        self.counts = {f"eng_{e}": 0 for e in ENGINES}
        for q in ("gpsimd", "sync", "scalar"):
            for s in range(self.NDMASEM):
                self.counts[f"dma_{q}_{s}"] = 0
        self.counts["cc"] = 0
        self.dma_rr = {"gpsimd": 0, "sync": 0, "scalar": 0}
        self.floor = {}
        self.scopes = [[]]
        g = nc.sbuf_tensor("constpool", [128, 16], F32)
        self.constpool = Buf(g.__enter__(), "constpool")
        self.scopes[0].append(g)
        self.constvals = {}

    def push_scope(self):
        self.scopes.append([])

    def pop_scope(self):
        for g in reversed(self.scopes.pop()):
            g.__exit__(None, None, None)
        self.barrier()

    def op(self, engine, fn, r=(), w=(), dma=False, cc=False, tag=""):
        deps = dict(self.floor)

        def add(ev):
            s, v = ev
            if deps.get(s, 0) < v:
                deps[s] = v
        for b in r:
            for ev in b.writers:
                add(ev)
        for b in w:
            for ev in b.writers:
                add(ev)
            for ev in b.readers:
                add(ev)
        if cc:
            self.counts["cc"] += 1
            ev = ("cc", self.counts["cc"])
            amt = 1
        elif dma:
            s = self.dma_rr[engine]
            self.dma_rr[engine] = (s + 1) % self.NDMASEM
            k = f"dma_{engine}_{s}"
            if deps.get(k, 0) < self.counts[k]:
                deps[k] = self.counts[k]
            self.counts[k] += 16
            ev = (k, self.counts[k])
            amt = 16
        else:
            k = f"eng_{engine}"
            self.counts[k] += 1
            ev = (k, self.counts[k])
            amt = 1
        self.ops.append(dict(engine=engine, fn=fn, waits=deps, inc=(ev[0], ev[1], amt), tag=tag))
        for b in r:
            b.readers.append(ev)
        for b in w:
            b.writers = [ev]
            b.readers = []
        return ev

    def barrier(self):
        self.floor = {k: v for k, v in self.counts.items() if v > 0 and k != "cc"}

    def sb(self, name, shape, dtype):
        g = self.nc.sbuf_tensor(name, shape, dtype)
        t = g.__enter__()
        self.scopes[-1].append(g)
        return Buf(t, name)

    def ps(self, name, shape, dtype=F32):
        t = self.nc.alloc_psum_tensor(name, shape, dtype)
        return Buf(t, name)

    def dma(self, engine, out_ap, in_ap, r=(), w=(), tag="", slow=False):
        if slow:
            return self.op(engine,
                           lambda e: e.dma_start(out=out_ap, in_=in_ap,
                                                 allow_slow_non_contiguous=True),
                           r=r, w=w, dma=True, tag=tag)
        return self.op(engine, lambda e: e.dma_start(out=out_ap, in_=in_ap),
                       r=r, w=w, dma=True, tag=tag)

    def mm(self, out_ap, lhsT_ap, rhs_ap, start, stop, r=(), w=(), tag=""):
        return self.op("tensor",
                       lambda e: e.matmul(out=out_ap, lhsT=lhsT_ap, rhs=rhs_ap,
                                          start=start, stop=stop),
                       r=r, w=w, tag=tag)

    def tr(self, out_ap, in_ap, ident_ap, r=(), w=(), tag=""):
        return self.op("tensor",
                       lambda e: e.transpose(out=out_ap, in_=in_ap, identity=ident_ap),
                       r=r, w=w, tag=tag)

    def trp(self, out_ap, in_ap, ident_ap, start, stop, r=(), w=(), tag=""):
        return self.op("tensor",
                       lambda e: e.matmul(out=out_ap, lhsT=in_ap, rhs=ident_ap,
                                          start=start, stop=stop, is_transpose=True),
                       r=r, w=w, tag=tag)

    def cp(self, eng, out_ap, in_ap, r=(), w=(), tag=""):
        if eng == "scalar":
            return self.op(eng, lambda e: e.copy(out=out_ap, in_=in_ap), r=r, w=w, tag=tag)
        return self.op(eng, lambda e: e.tensor_copy(out=out_ap, in_=in_ap), r=r, w=w, tag=tag)

    def tt(self, eng, out_ap, a_ap, b_ap, alu, r=(), w=(), tag=""):
        return self.op(eng, lambda e: e.tensor_tensor(out=out_ap, in0=a_ap, in1=b_ap, op=alu),
                       r=r, w=w, tag=tag)

    def tss(self, eng, out_ap, in_ap, scalar, alu, r=(), w=(), tag=""):
        return self.op(eng, lambda e: e.tensor_single_scalar(out=out_ap, in_=in_ap,
                                                             scalar=scalar, op=alu),
                       r=r, w=w, tag=tag)

    def ts2(self, eng, out_ap, in_ap, s1, s2, op0, op1, r=(), w=(), tag=""):
        return self.op(eng, lambda e: e.tensor_scalar(out=out_ap, in0=in_ap, scalar1=s1,
                                                      scalar2=s2, op0=op0, op1=op1),
                       r=r, w=w, tag=tag)

    def stt(self, eng, out_ap, in0_ap, scalar, in1_ap, op0, op1, r=(), w=(), tag=""):
        return self.op(eng, lambda e: e.scalar_tensor_tensor(out=out_ap, in0=in0_ap,
                                                             scalar=scalar, in1=in1_ap,
                                                             op0=op0, op1=op1),
                       r=r, w=w, tag=tag)

    def register_const(self, val):
        if val in self.constvals:
            return
        i = len(self.constvals)
        col = self.constpool[:, i:i + 1]
        self.memset("vector", col, val, w=[self.constpool])
        self.nc.const_aps.register(F32, val, col)
        self.constvals[val] = col

    def act(self, out_ap, in_ap, func, bias=0.0, scale=1.0, accum_ap=None, r=(), w=(), tag=""):
        if isinstance(bias, float) and bias != 0.0:
            self.register_const(bias)
            r = list(r) + [self.constpool]
        return self.op("scalar",
                       lambda e: e.activation(out=out_ap, in_=in_ap, func=func, bias=bias,
                                              scale=scale, accum_out=accum_ap),
                       r=r, w=w, tag=tag)

    def red(self, eng, out_ap, in_ap, alu, negate=False, r=(), w=(), tag=""):
        return self.op(eng, lambda e: e.tensor_reduce(out=out_ap, in_=in_ap, axis=AX.X,
                                                      op=alu, negate=negate),
                       r=r, w=w, tag=tag)

    def scan_cumsum(self, out_ap, data_ap, init, r=(), w=(), tag=""):
        return self.op("vector",
                       lambda e: e.tensor_tensor_scan(out=out_ap, data0=data_ap, data1=data_ap,
                                                      initial=init, op0=ALU.add, op1=ALU.bypass),
                       r=r, w=w, tag=tag)

    def memset(self, eng, ap, val, w=(), tag=""):
        return self.op(eng, lambda e: e.memset(ap, val), w=w, tag=tag)

    def recip(self, out_ap, in_ap, r=(), w=(), tag=""):
        return self.op("vector", lambda e: e.reciprocal(out=out_ap, in_=in_ap), r=r, w=w, tag=tag)

    def collective(self, kind, alu, cc_in_ap, cc_out_ap, r=(), w=(), tag=""):
        return self.op("gpsimd",
                       lambda e: e.collective_compute(kind, alu,
                                                      replica_groups=[list(range(8))],
                                                      ins=[cc_in_ap], outs=[cc_out_ap]),
                       r=r, w=w, cc=True, tag=tag)

    def emit(self):
        nc = self.nc
        with ExitStack() as ctx:
            sems = {}
            for k, v in self.counts.items():
                if v > 0:
                    sems[k] = ctx.enter_context(nc.semaphore(k))
            block = ctx.enter_context(nc.Block())
            by_engine = {e: [o for o in self.ops if o["engine"] == e] for e in ENGINES}
            finals = {k: v for k, v in self.counts.items() if v > 0}

            def section(eng_obj, engine):
                guaranteed = {}
                own = f"eng_{engine}"
                own_count = 0
                for o in by_engine[engine]:
                    for s, v in sorted(o["waits"].items()):
                        if s == own and engine == "tensor":
                            continue
                        if s == own and v <= own_count and engine != "tensor":
                            pass
                        if guaranteed.get(s, 0) < v:
                            eng_obj.wait_ge(sems[s], v)
                            guaranteed[s] = v
                    inst = o["fn"](eng_obj)
                    s, v, amt = o["inc"]
                    inst.then_inc(sems[s], amt)
                    if s == own:
                        own_count = v
                if engine == "sync":
                    for s, v in finals.items():
                        if guaranteed.get(s, 0) < v:
                            eng_obj.wait_ge(sems[s], v)

            for e in ENGINES:
                if by_engine[e] or e == "sync":
                    getattr(block, e)(lambda obj, e=e: section(obj, e))
        return nc




WNAMES_F32 = [
    "L0_attn_Wq", "L0_attn_Wk", "L0_attn_Wv", "L0_attn_Wg", "L0_attn_Wo",
    "L0_attn_sk_w1", "L0_attn_sk_w2", "L0_attn_sk_b2",
    "L0_attn_sv_w1", "L0_attn_sv_w2", "L0_attn_sv_b2", "L0_attn_gnorm_w",
    "L1_attn_q_w", "L1_attn_q_b", "L1_attn_k_w", "L1_attn_k_b",
    "L1_attn_v_w", "L1_attn_v_b", "L1_attn_out_w", "L1_attn_out_b",
    "L1_attn_rel_emb", "L1_attn_grep_w", "L1_attn_grep_b", "L1_attn_grep_a",
    "L1_attn_bias_k", "L1_attn_bias_v",
    "L1_ffn_gate_w", "L1_ffn_w1", "L1_ffn_b1", "L1_ffn_w2", "L1_ffn_b2",
    "L1_ffn_w3", "L1_ffn_b3", "L1_ffn_sw1", "L1_ffn_sw2", "L1_ffn_sw3",
]
WNAMES_I32 = ["L0_ffn_w_gate", "L0_ffn_w_up", "L0_ffn_w_down"]

WSHAPES = {
    "L0_attn_Wq": (1024, 1024), "L0_attn_Wk": (1024, 1024), "L0_attn_Wv": (1024, 1024),
    "L0_attn_Wg": (1024, 1024), "L0_attn_Wo": (1024, 1024),
    "L0_attn_sk_w1": (1024, 16), "L0_attn_sk_w2": (16, 256), "L0_attn_sk_b2": (256,),
    "L0_attn_sv_w1": (1024, 16), "L0_attn_sv_w2": (16, 256), "L0_attn_sv_b2": (256,),
    "L0_attn_gnorm_w": (256,),
    "L0_ffn_w_gate": (3072, 1024), "L0_ffn_w_up": (3072, 1024), "L0_ffn_w_down": (1024, 3072),
    "L1_attn_q_w": (1024, 1024), "L1_attn_q_b": (1024,),
    "L1_attn_k_w": (1024, 1024), "L1_attn_k_b": (1024,),
    "L1_attn_v_w": (1024, 1024), "L1_attn_v_b": (1024,),
    "L1_attn_out_w": (1024, 1024), "L1_attn_out_b": (1024,),
    "L1_attn_rel_emb": (32, 16), "L1_attn_grep_w": (64, 8), "L1_attn_grep_b": (8,),
    "L1_attn_grep_a": (1, 16, 1, 1), "L1_attn_bias_k": (1, 1, 1024), "L1_attn_bias_v": (1, 1, 1024),
    "L1_ffn_gate_w": (8, 1024), "L1_ffn_w1": (8, 1024, 1024), "L1_ffn_b1": (8, 1024),
    "L1_ffn_w2": (8, 1024, 1024), "L1_ffn_b2": (8, 1024),
    "L1_ffn_w3": (8, 1024, 1024), "L1_ffn_b3": (8, 1024),
    "L1_ffn_sw1": (2048, 1024), "L1_ffn_sw2": (2048, 1024), "L1_ffn_sw3": (1024, 2048),
}

BLOBK_W = 8 * 128
BLOBVZ_W = 8 * 257
BLOBZK_W = 8


def build(cfg):
    upto = cfg.get("upto", 99)
    prefix = cfg.get("prefix", "cc")
    seg_only = cfg.get("seg_only", False)
    scanstop = cfg.get("scanstop", 99)
    a1stop = cfg.get("a1stop", 99)
    debug = cfg.get("debug", set())

    nc = bass.Bass("TRN2", target_bir_lowering=False)
    p = Prog(nc)

    SPECS = dict(WSHAPES)
    SPECS.update({"xs": (NT, D), "identf": (128, 128), "identb": (128, 128),
                  "utmask": (128, 128), "cos": (TC, 128), "sin": (TC, 128),
                  "pmask": (128, 8), "blobsK": (8 * 128, BLOBK_W),
                  "blobsVz": (8 * 64, BLOBVZ_W), "blobszk": (8 * 64, BLOBZK_W),
                  "dupm": (128, 256), "selm": (128, 512), "segm": (128, 128),
                  "gmaskm": (128, 128), "bcastP": (16, 2112)})
    BF16_IN = {"identb", "dupm", "selm", "segm", "bcastP"}

    class LazyInputs:
        def __init__(self):
            self.cache = {}

        def __getitem__(self, nm):
            if nm not in self.cache:
                dt = I32 if nm in WNAMES_I32 else (BF16 if nm in BF16_IN else F32)
                self.cache[nm] = nc.dram_tensor(nm, list(SPECS[nm]), dt,
                                                kind="ExternalInput").ap()
            return self.cache[nm]

    inp = LazyInputs()

    outs = []

    def out_t(name, shape, dtype=F32):
        outs.append(name)
        return nc.dram_tensor(name, shape, dtype, kind="ExternalOutput").ap()

    if seg_only:
        o_blobK = out_t("blobK", [128, BLOBK_W])
        o_blobVz = out_t("blobVz", [64, BLOBVZ_W])
        o_blobzk = out_t("blobzk", [64, BLOBZK_W])
    else:
        o_ys = out_t("ys", [NT, D])

    dbg = {}

    def dbg_t(name, shape, dtype=F32):
        if name in debug:
            dbg[name] = out_t("dbg_" + name, shape, dtype)

    dbg_t("xn", [8 * 128, NT])
    dbg_t("t1", [16, NT])
    dbg_t("segK", [128, BLOBK_W])
    dbg_t("segVz", [64, BLOBVZ_W])
    dbg_t("segzk", [64, BLOBZK_W])
    dbg_t("preK", [128, BLOBK_W])
    dbg_t("preVz", [64, BLOBVZ_W])
    dbg_t("prezk", [64, BLOBZK_W])
    dbg_t("ffn0", [8 * 128, NT])
    dbg_t("og", [16 * 128, TC])
    dbg_t("attn0", [8 * 128, NT])
    dbg_t("x1", [8 * 128, NT])
    dbg_t("xn1", [8 * 128, NT])
    dbg_t("attn1", [8 * 128, NT])
    dbg_t("moemsc", [16, NT])
    dbg_t("moe", [8 * 128, NT])
    dbg_t("zsh", [8 * 128, NT])

    identf = p.sb("identf_s", [128, 128], F32)
    identb = p.sb("identb_s", [128, 128], BF16)
    utmask = p.sb("utmask_s", [128, 128], F32)
    cos_s = p.sb("cos_s", [128, 8 * 128], BF16)
    sin_s = p.sb("sin_s", [128, 8 * 128], BF16)
    pmask_s = p.sb("pmask_s", [128, 8], F32)
    onesf = p.sb("onesf", [128, 128], F32)
    onesb = p.sb("onesb", [128, 128], BF16)

    p.dma("sync", identf[:, :], inp["identf"], w=[identf])
    p.dma("sync", identb[:, :], inp["identb"], w=[identb])
    p.dma("sync", utmask[:, :], inp["utmask"], w=[utmask])
    p.dma("sync", pmask_s[:, :], inp["pmask"], w=[pmask_s])
    for tt in range(8):
        p.dma("gpsimd", cos_s[:, tt * 128:(tt + 1) * 128], inp["cos"][tt * 128:(tt + 1) * 128, :], w=[cos_s])
        p.dma("gpsimd", sin_s[:, tt * 128:(tt + 1) * 128], inp["sin"][tt * 128:(tt + 1) * 128, :], w=[sin_s])
    p.memset("vector", onesf[:, :], 1.0, w=[onesf])
    p.memset("vector", onesb[:, :], 1.0, w=[onesb])

    psf = [p.ps(f"psf{i}", [128, 512]) for i in range(6)]
    PBA = p.ps("pbA", [128, 1024], BF16)
    PBB = p.ps("pbB", [128, 1024], BF16)

    acc = [p.sb(f"acc{k}", [128, NT], F32) for k in range(8)]
    p.push_scope()
    xnat = [p.sb(f"xnat{j}", [128, D], F32) for j in range(3)]
    for tt in range(16):
        xb = xnat[tt % 3]
        p.dma("sync", xb[:, :], inp["xs"][tt * 128:(tt + 1) * 128, :], w=[xb])
        for k in range(8):
            pb = psf[(tt * 8 + k) % 4]
            p.tr(pb[:, 0:128], xb[:, k * 128:(k + 1) * 128], identf[:, :], r=[xb, identf], w=[pb])
            p.cp("vector" if k % 2 == 0 else "scalar",
                 acc[k][:, tt * 128:(tt + 1) * 128], pb[:, 0:128], r=[pb], w=[acc[k]])
    p.pop_scope()

    def layernorm_fm(src_tiles, xn_tiles, tag):
        p.push_scope()
        stmp = [p.sb(f"stmp_{tag}{j}", [128, NT], F32) for j in range(2)]
        rowA = p.sb(f"rowA_{tag}", [1, NT], F32)
        rowB = p.sb(f"rowB_{tag}", [1, NT], F32)
        rowC = p.sb(f"rowC_{tag}", [1, NT], F32)
        xsq = [p.sb(f"xsq_{tag}{j}", [128, 512], BF16) for j in range(2)]
        for k in range(8):
            p.cp("vector", xn_tiles[k][:, :], src_tiles[k][:, :], r=[src_tiles[k]], w=[xn_tiles[k]])
        for mch in range(4):
            ms = slice(mch * 512, (mch + 1) * 512)
            pbx = psf[4]
            for k in range(8):
                p.mm(pbx[0:1, 0:512], onesb[:, 0:1], xn_tiles[k][:, ms],
                     start=(k == 0), stop=(k == 7), r=[onesb, xn_tiles[k]], w=[pbx])
            p.cp("vector", rowA[0:1, ms], pbx[0:1, 0:512], r=[pbx], w=[rowA])
            pbq = psf[5]
            for k in range(8):
                xq = xsq[k % 2]
                p.act(xq[:, 0:512], src_tiles[k][:, ms], AF.Square, r=[src_tiles[k]], w=[xq])
                p.mm(pbq[0:1, 0:512], onesb[:, 0:1], xq[:, 0:512],
                     start=(k == 0), stop=(k == 7), r=[onesb, xq], w=[pbq])
            p.cp("vector", rowB[0:1, ms], pbq[0:1, 0:512], r=[pbq], w=[rowB])
        p.tss("vector", rowA[0:1, :], rowA[0:1, :], 1.0 / D, ALU.mult, r=[rowA], w=[rowA])
        p.tt("vector", rowC[0:1, :], rowA[0:1, :], rowA[0:1, :], ALU.mult, r=[rowA], w=[rowC])
        p.stt("vector", rowB[0:1, :], rowB[0:1, :], 1.0 / D, rowC[0:1, :], ALU.mult, ALU.subtract,
              r=[rowB, rowC], w=[rowB])
        p.act(rowB[0:1, :], rowB[0:1, :], AF.Sqrt, bias=EPS, scale=1.0, r=[rowB], w=[rowB])
        p.recip(rowB[0:1, :], rowB[0:1, :], r=[rowB], w=[rowB])
        p.tt("vector", rowC[0:1, :], rowA[0:1, :], rowB[0:1, :], ALU.mult, r=[rowA, rowB], w=[rowC])
        rstd_bc = p.sb(f"rstdbc_{tag}", [128, NT], F32)
        mr_bc = p.sb(f"mrbc_{tag}", [128, NT], F32)
        for mch in range(4):
            ms = slice(mch * 512, (mch + 1) * 512)
            pbx = psf[4]
            p.mm(pbx[:, 0:512], onesf[0:1, :], rowB[0:1, ms], start=True, stop=True,
                 r=[onesf, rowB], w=[pbx])
            p.cp("vector", rstd_bc[:, ms], pbx[:, 0:512], r=[pbx], w=[rstd_bc])
            pbq = psf[5]
            p.mm(pbq[:, 0:512], onesf[0:1, :], rowC[0:1, ms], start=True, stop=True,
                 r=[onesf, rowC], w=[pbq])
            p.cp("scalar", mr_bc[:, ms], pbq[:, 0:512], r=[pbq], w=[mr_bc])
        for k in range(8):
            p.tt("vector", stmp[k % 2][:, :], src_tiles[k][:, :], rstd_bc[:, :], ALU.mult,
                 r=[src_tiles[k], rstd_bc], w=[stmp[k % 2]])
            p.tt("vector", xn_tiles[k][:, :], stmp[k % 2][:, :], mr_bc[:, :], ALU.subtract,
                 r=[stmp[k % 2], mr_bc], w=[xn_tiles[k]])
        p.pop_scope()

    xn = [p.sb(f"xn{k}", [128, NT], BF16) for k in range(8)]
    layernorm_fm(acc, xn, "ln0")

    def dump_fm(name, tiles, conv=True):
        if name not in dbg:
            return
        for k in range(len(tiles)):
            if tiles[k].t.dtype == BF16:
                p.dma("gpsimd", dbg[name][k * 128:(k + 1) * 128, :], tiles[k][:, :], r=[tiles[k]])
            else:
                p.dma("sync", dbg[name][k * 128:(k + 1) * 128, :], tiles[k][:, :], r=[tiles[k]])

    dump_fm("xn", xn)
    if upto <= 1:
        return finish(p, nc, outs, inp)

    p.push_scope()
    skw1 = p.sb("skw1", [128, 8 * 16], BF16)
    svw1 = p.sb("svw1", [128, 8 * 16], BF16)
    for k in range(8):
        p.dma("gpsimd", skw1[:, k * 16:(k + 1) * 16], inp["L0_attn_sk_w1"][k * 128:(k + 1) * 128, :], w=[skw1])
        p.dma("gpsimd", svw1[:, k * 16:(k + 1) * 16], inp["L0_attn_sv_w1"][k * 128:(k + 1) * 128, :], w=[svw1])
    t1k = p.sb("t1k", [16, NT], BF16)
    t1v = p.sb("t1v", [16, NT], BF16)
    for mch in range(4):
        ms = slice(mch * 512, (mch + 1) * 512)
        pbx = psf[0]
        for k in range(8):
            p.mm(pbx[0:16, 0:512], skw1[:, k * 16:(k + 1) * 16], xn[k][:, ms],
                 start=(k == 0), stop=(k == 7), r=[skw1, xn[k]], w=[pbx])
        p.cp("vector", t1k[0:16, ms], pbx[0:16, 0:512], r=[pbx], w=[t1k])
        pby = psf[1]
        for k in range(8):
            p.mm(pby[0:16, 0:512], svw1[:, k * 16:(k + 1) * 16], xn[k][:, ms],
                 start=(k == 0), stop=(k == 7), r=[svw1, xn[k]], w=[pby])
        p.cp("vector", t1v[0:16, ms], pby[0:16, 0:512], r=[pby], w=[t1v])
    if "t1" in dbg:
        tf = p.sb("t1f", [16, NT], F32)
        p.cp("vector", tf[:, :], t1k[:, :], r=[t1k], w=[tf])
        p.dma("sync", dbg["t1"], tf[:, :], r=[tf])

    skw2 = p.sb("skw2", [16, 256], BF16)
    svw2 = p.sb("svw2", [16, 256], BF16)
    p.dma("gpsimd", skw2[:, :], inp["L0_attn_sk_w2"], w=[skw2])
    p.dma("gpsimd", svw2[:, :], inp["L0_attn_sv_w2"], w=[svw2])
    skb2r = p.sb("skb2r", [4, 64], F32)
    svb2r = p.sb("svb2r", [4, 64], F32)
    p.dma("gpsimd", skb2r[:, :], inp["L0_attn_sk_b2"].rearrange("(a b) -> a b", b=64), w=[skb2r])
    p.dma("gpsimd", svb2r[:, :], inp["L0_attn_sv_b2"].rearrange("(a b) -> a b", b=64), w=[svb2r])
    skb2c = p.sb("skb2c", [64, 4], F32)
    svb2c = p.sb("svb2c", [64, 4], F32)
    pbx = psf[0]
    p.tr(pbx[0:64, 0:4], skb2r[:, :], identf[0:4, 0:4], r=[skb2r, identf], w=[pbx])
    p.cp("vector", skb2c[:, :], pbx[0:64, 0:4], r=[pbx], w=[skb2c])
    pby = psf[1]
    p.tr(pby[0:64, 0:4], svb2r[:, :], identf[0:4, 0:4], r=[svb2r, identf], w=[pby])
    p.cp("vector", svb2c[:, :], pby[0:64, 0:4], r=[pby], w=[svb2c])

    def load_wslice(buf, wname, h, k):
        p.dma("gpsimd", buf[:, k * 256:(k + 1) * 256],
              inp[wname][k * 128:(k + 1) * 128, h * 256:(h + 1) * 256], w=[buf])

    ktm_b = [p.sb(f"ktm{j}", [128, 256], BF16) for j in range(2)]
    vtm_b = [p.sb(f"vtm{j}", [128, 256], BF16) for j in range(2)]
    wkf_b = [p.sb(f"wkf{j}", [64, 128], BF16) for j in range(2)]
    wvf_b = [p.sb(f"wvf{j}", [64, 128], BF16) for j in range(2)]
    wktm_b = [p.sb(f"wktm{j}", [128, 64], BF16) for j in range(2)]
    wvtm_b = [p.sb(f"wvtm{j}", [128, 64], BF16) for j in range(2)]
    rtmp = [p.sb(f"rtmp{j}", [128, 256], F32) for j in range(2)]
    rtmp += [p.sb(f"rtmp{j+2}", [64, 128], F32) for j in range(2)]

    def proj_chunk(b, h, tc, j, Wkh=None, Wvh=None, need=("k", "v", "wk", "wv")):
        col = b * TC + tc * 128
        cs = slice(col, col + 128)
        ts = slice(tc * 128, (tc + 1) * 128)
        res = {}
        if "k" in need:
            pk = psf[0]
            for k in range(8):
                p.mm(pk[:, 0:256], xn[k][:, cs], Wkh[:, k * 256:(k + 1) * 256],
                     start=(k == 0), stop=(k == 7), r=[xn[k], Wkh], w=[pk])
            t0 = rtmp[0]
            t1_ = rtmp[1]
            p.tt("vector", t0[:, 0:128], pk[:, 0:128], cos_s[:, ts], ALU.mult, r=[pk, cos_s], w=[t0])
            p.tt("vector", t0[:, 128:256], pk[:, 128:256], sin_s[:, ts], ALU.mult, r=[pk, sin_s], w=[t0])
            p.tt("vector", t1_[:, 0:128], pk[:, 128:256], cos_s[:, ts], ALU.mult, r=[pk, cos_s], w=[t1_])
            p.tt("vector", t1_[:, 128:256], pk[:, 0:128], sin_s[:, ts], ALU.mult, r=[pk, sin_s], w=[t1_])
            p.tt("vector", ktm_b[j][:, 0:128], t0[:, 0:128], t0[:, 128:256], ALU.subtract,
                 r=[t0], w=[ktm_b[j]])
            p.tt("vector", ktm_b[j][:, 128:256], t1_[:, 0:128], t1_[:, 128:256], ALU.add,
                 r=[t1_], w=[ktm_b[j]])
            res["ktm"] = ktm_b[j]
        if "v" in need:
            pv = psf[1]
            for k in range(8):
                p.mm(pv[:, 0:256], xn[k][:, cs], Wvh[:, k * 256:(k + 1) * 256],
                     start=(k == 0), stop=(k == 7), r=[xn[k], Wvh], w=[pv])
            p.cp("vector", vtm_b[j][:, :], pv[:, 0:256], r=[pv], w=[vtm_b[j]])
            res["vtm"] = vtm_b[j]
        if "wk" in need:
            pw = psf[2]
            p.mm(pw[0:64, 0:128], skw2[:, h * 64:(h + 1) * 64], t1k[:, cs],
                 start=True, stop=True, r=[skw2, t1k], w=[pw])
            hc = skb2c[0:64, h:h + 1]
            wtmp = rtmp[2]
            p.ts2("vector", wtmp[0:64, 0:128], pw[0:64, 0:128], hc, 32.0, ALU.add, ALU.min,
                  r=[pw, skb2c], w=[wtmp])
            p.tss("vector", wtmp[0:64, 0:128], wtmp[0:64, 0:128], -32.0, ALU.max,
                  r=[wtmp], w=[wtmp])
            p.act(wkf_b[j][0:64, 0:128], wtmp[0:64, 0:128], AF.Exp, r=[wtmp], w=[wkf_b[j]])
            p.tr(PBA[0:128, 0:64], wkf_b[j][0:64, 0:128], identb[0:64, 0:64],
                 r=[wkf_b[j], identb], w=[PBA])
            p.cp("vector", wktm_b[j][:, :], PBA[0:128, 0:64], r=[PBA], w=[wktm_b[j]])
            res["wkf"] = wkf_b[j]
            res["wktm"] = wktm_b[j]
        if "wv" in need:
            pw = psf[2]
            p.mm(pw[0:64, 0:128], svw2[:, h * 64:(h + 1) * 64], t1v[:, cs],
                 start=True, stop=True, r=[svw2, t1v], w=[pw])
            hc = svb2c[0:64, h:h + 1]
            wtmp = rtmp[3]
            p.ts2("vector", wtmp[0:64, 0:128], pw[0:64, 0:128], hc, 32.0, ALU.add, ALU.min,
                  r=[pw, svb2c], w=[wtmp])
            p.tss("vector", wtmp[0:64, 0:128], wtmp[0:64, 0:128], -32.0, ALU.max,
                  r=[wtmp], w=[wtmp])
            p.act(wvf_b[j][0:64, 0:128], wtmp[0:64, 0:128], AF.Exp, r=[wtmp], w=[wvf_b[j]])
            p.tr(PBB[0:128, 0:64], wvf_b[j][0:64, 0:128], identb[0:64, 0:64],
                 r=[wvf_b[j], identb], w=[PBB])
            p.cp("vector", wvtm_b[j][:, :], PBB[0:128, 0:64], r=[PBB], w=[wvtm_b[j]])
            res["wvf"] = wvf_b[j]
            res["wvtm"] = wvtm_b[j]
        return res

    if prefix != "cc":
        preK = p.sb("preK", [128, BLOBK_W], F32)
        preVz = p.sb("preVz", [64, BLOBVZ_W], F32)
        prezk = p.sb("prezk", [64, BLOBZK_W], F32)
    else:
        preK = preVz = prezk = None

    p.push_scope()
    blobK = p.sb("blobK_s", [128, BLOBK_W], F32)
    blobVz = p.sb("blobVz_s", [64, BLOBVZ_W], F32)
    blobzk = p.sb("blobzk_s", [64, BLOBZK_W], F32)
    onecol = p.sb("onecol", [128, 1], BF16)
    p.memset("vector", onecol[:, :], 1.0, w=[onecol])

    p.push_scope()
    Wkh_bufs = [p.sb(f"Wkh{j}", [128, 8 * 256], BF16) for j in range(2)]
    Wvh_bufs = [p.sb(f"Wvh{j}", [128, 8 * 256], BF16) for j in range(2)]
    for h in range(4):
        Wkh, Wvh = Wkh_bufs[h % 2], Wvh_bufs[h % 2]
        for k in range(8):
            load_wslice(Wkh, "L0_attn_Wk", h, k)
            load_wslice(Wvh, "L0_attn_Wv", h, k)
        for b in range(2):
            bh = b * 4 + h
            nvz = 0
            for tc in range(8):
                j = tc % 2
                r = proj_chunk(b, h, tc, j, Wkh=Wkh, Wvh=Wvh)
                st, sp = (tc == 0), (tc == 7)
                p.mm(psf[4][:, 0:64], r["ktm"][:, 0:128], r["wktm"][:, :],
                     start=st, stop=sp, r=[r["ktm"], r["wktm"]], w=[psf[4]])
                p.mm(psf[5][:, 0:64], r["ktm"][:, 128:256], r["wktm"][:, :],
                     start=st, stop=sp, r=[r["ktm"], r["wktm"]], w=[psf[5]])
                p.mm(psf[3][0:64, 0:256], r["wvtm"][:, :], r["vtm"][:, :],
                     start=(nvz == 0), stop=False, r=[r["wvtm"], r["vtm"]], w=[psf[3]])
                nvz += 1
                p.mm(psf[3][0:64, 256:257], r["wvtm"][:, :], onecol[:, :],
                     start=False, stop=False, r=[r["wvtm"], onecol], w=[psf[3]])
                p.mm(psf[3][0:64, 260:261], r["wktm"][:, :], onecol[:, :],
                     start=False, stop=(tc == 7), r=[r["wktm"], onecol], w=[psf[3]])
            p.cp("vector", blobK[:, bh * 128:bh * 128 + 64], psf[4][:, 0:64], r=[psf[4]], w=[blobK])
            p.cp("vector", blobK[:, bh * 128 + 64:(bh + 1) * 128], psf[5][:, 0:64], r=[psf[5]], w=[blobK])
            p.cp("vector", blobVz[0:64, bh * 257:(bh + 1) * 257], psf[3][0:64, 0:257],
                 r=[psf[3]], w=[blobVz])
            p.cp("vector", blobzk[0:64, bh:bh + 1], psf[3][0:64, 260:261], r=[psf[3]], w=[blobzk])
    p.pop_scope()

    if "segK" in dbg:
        p.dma("sync", dbg["segK"], blobK[:, :], r=[blobK])
    if "segVz" in dbg:
        p.dma("sync", dbg["segVz"], blobVz[:, :], r=[blobVz])
    if "segzk" in dbg:
        p.dma("sync", dbg["segzk"], blobzk[:, :], r=[blobzk])
    if seg_only:
        p.dma("sync", o_blobK, blobK[:, :], r=[blobK])
        p.dma("sync", o_blobVz, blobVz[:, :], r=[blobVz])
        p.dma("sync", o_blobzk, blobzk[:, :], r=[blobzk])
        return finish(p, nc, outs, inp)
    if upto <= 2:
        return finish(p, nc, outs, inp)

    if prefix == "zero":
        p.memset("vector", preK[:, :], 0.0, w=[preK])
        p.memset("vector", preVz[:, :], 0.0, w=[preVz])
        p.memset("vector", prezk[:, :], 0.0, w=[prezk])
    elif prefix == "input":
        p.memset("vector", preK[:, :], 0.0, w=[preK])
        p.memset("vector", preVz[:, :], 0.0, w=[preVz])
        p.memset("vector", prezk[:, :], 0.0, w=[prezk])
        bK = p.sb("binK", [128, BLOBK_W], F32)
        bVz = p.sb("binVz", [64, BLOBVZ_W], F32)
        bzk = p.sb("binzk", [64, BLOBZK_W], F32)
        for cc_ in range(8):
            p.dma("sync", bK[:, :], inp["blobsK"][cc_ * 128:(cc_ + 1) * 128, :], w=[bK])
            p.dma("sync", bVz[:, :], inp["blobsVz"][cc_ * 64:(cc_ + 1) * 64, :], w=[bVz])
            p.dma("sync", bzk[:, :], inp["blobszk"][cc_ * 64:(cc_ + 1) * 64, :], w=[bzk])
            p.stt("vector", preK[:, :], bK[:, :], pmask_s[0:128, cc_:cc_ + 1], preK[:, :],
                  ALU.mult, ALU.add, r=[bK, pmask_s, preK], w=[preK])
            p.stt("vector", preVz[:, :], bVz[:, :], pmask_s[0:64, cc_:cc_ + 1], preVz[:, :],
                  ALU.mult, ALU.add, r=[bVz, pmask_s, preVz], w=[preVz])
            p.stt("vector", prezk[:, :], bzk[:, :], pmask_s[0:64, cc_:cc_ + 1], prezk[:, :],
                  ALU.mult, ALU.add, r=[bzk, pmask_s, prezk], w=[prezk])
    else:
        ccK_in = nc.dram_tensor("ccK_in", [8 * 128, BLOBK_W], F32).ap()
        ccK_out = nc.dram_tensor("ccK_out", [128, BLOBK_W], F32).ap()
        ccVz_in = nc.dram_tensor("ccVz_in", [8 * 64, BLOBVZ_W], F32).ap()
        ccVz_out = nc.dram_tensor("ccVz_out", [64, BLOBVZ_W], F32).ap()
        cczk_in = nc.dram_tensor("cczk_in", [8 * 64, BLOBZK_W], F32).ap()
        cczk_out = nc.dram_tensor("cczk_out", [64, BLOBZK_W], F32).ap()
        bK_d = Buf(None, "ccKin_d")
        bVz_d = Buf(None, "ccVzin_d")
        bzk_d = Buf(None, "cczkin_d")
        oK_d = Buf(None, "ccKout_d")
        oVz_d = Buf(None, "ccVzout_d")
        ozk_d = Buf(None, "cczkout_d")
        msK = p.sb("msK", [128, BLOBK_W], F32)
        msVz = p.sb("msVz", [64, BLOBVZ_W], F32)
        mszk = p.sb("mszk", [64, BLOBZK_W], F32)
        for jj in range(8):
            p.tss("vector", msK[:, :], blobK[:, :], pmask_s[0:128, jj:jj + 1], ALU.mult,
                  r=[blobK, pmask_s], w=[msK])
            p.dma("gpsimd", ccK_in[jj * 128:(jj + 1) * 128, :], msK[:, :], r=[msK], w=[bK_d])
            p.tss("vector", msVz[:, :], blobVz[:, :], pmask_s[0:64, jj:jj + 1], ALU.mult,
                  r=[blobVz, pmask_s], w=[msVz])
            p.dma("gpsimd", ccVz_in[jj * 64:(jj + 1) * 64, :], msVz[:, :], r=[msVz], w=[bVz_d])
            p.tss("vector", mszk[:, :], blobzk[:, :], pmask_s[0:64, jj:jj + 1], ALU.mult,
                  r=[blobzk, pmask_s], w=[mszk])
            p.dma("gpsimd", cczk_in[jj * 64:(jj + 1) * 64, :], mszk[:, :], r=[mszk], w=[bzk_d])
        p.collective("ReduceScatter", ALU.add, ccK_in.opt(), ccK_out.opt(),
                     r=[bK_d], w=[oK_d], tag="rsK")
        p.collective("ReduceScatter", ALU.add, ccVz_in.opt(), ccVz_out.opt(),
                     r=[bVz_d], w=[oVz_d], tag="rsVz")
        p.collective("ReduceScatter", ALU.add, cczk_in.opt(), cczk_out.opt(),
                     r=[bzk_d], w=[ozk_d], tag="rszk")
        readback = (ccK_out, ccVz_out, cczk_out, [oK_d, oVz_d, ozk_d])
    p.pop_scope()

    if "preK" in dbg and prefix != "cc":
        p.dma("sync", dbg["preK"], preK[:, :], r=[preK])
        p.dma("sync", dbg["preVz"], preVz[:, :], r=[preVz])
        p.dma("sync", dbg["prezk"], prezk[:, :], r=[prezk])
    if upto <= 3:
        return finish(p, nc, outs, inp)

    S127 = 1.0 / 127.0
    p.push_scope()
    hbuf = [p.sb(f"hb{i}", [128, 1024], BF16) for i in range(24)]
    wnat = [p.sb(f"wnat{j}", [128, 1024], F32) for j in range(2)]
    wnb = [p.sb(f"wnb{j}", [128, 1024], BF16) for j in range(1)]
    wT = [p.sb(f"wTf{j}", [128, 8 * 128], BF16) for j in range(4)]
    wTd8 = [p.sb(f"wTd8_{j}", [128, 8 * 128], BF16) for j in range(2)]
    gsil0s = [p.sb(f"gsil0_{i}", [128, 512], BF16) for i in range(2)]

    def load_wT(dst, wname, rows, kk_n, j, col0=0):
        for blk in range(0, kk_n, 8):
            nb = min(8, kk_n - blk)
            wn = wnat[j]
            wb = wnb[0]
            p.dma("gpsimd", wn[:, 0:nb * 128],
                  inp[wname][rows:rows + 128, col0 + blk * 128:col0 + (blk + nb) * 128], w=[wn])
            p.cp("vector" if j % 2 == 0 else "gpsimd", wb[:, 0:nb * 128], wn[:, 0:nb * 128],
                 r=[wn], w=[wb])
            pb = PBA if (j + blk // 8) % 2 == 0 else PBB
            for k in range(nb):
                p.trp(pb[:, k * 128:(k + 1) * 128], wb[:, k * 128:(k + 1) * 128], identb[:, :],
                      start=(k == 0), stop=(k == nb - 1), r=[wb, identb], w=[pb])
            p.cp("vector" if j % 2 == 0 else "scalar",
                 dst[:, blk * 128:(blk + nb) * 128], pb[:, 0:nb * 128], r=[pb], w=[dst])

    for c2 in range(2):
        tbase = c2 * 1024
        load_wT(wT[0], "L0_ffn_w_gate", 0, 8, 0)
        load_wT(wT[1], "L0_ffn_w_up", 0, 8, 1)
        for ht in range(24):
            jg, ju = 2 * (ht % 2), 2 * (ht % 2) + 1
            if False:
                pass
            for m2 in range(2):
                ms = slice(tbase + m2 * 512, tbase + (m2 + 1) * 512)
                pg = psf[0 + 2 * m2]
                for k in range(8):
                    p.mm(pg[:, 0:512], wT[jg][:, k * 128:(k + 1) * 128], xn[k][:, ms],
                         start=(k == 0), stop=(k == 7), r=[wT[jg], xn[k]], w=[pg])
                p.act(gsil0s[m2][:, 0:512], pg[:, 0:512], AF.Silu, scale=S127, r=[pg], w=[gsil0s[m2]])
                pu = psf[1 + 2 * m2]
                for k in range(8):
                    p.mm(pu[:, 0:512], wT[ju][:, k * 128:(k + 1) * 128], xn[k][:, ms],
                         start=(k == 0), stop=(k == 7), r=[wT[ju], xn[k]], w=[pu])
                p.stt("vector", hbuf[ht][:, m2 * 512:(m2 + 1) * 512], pu[:, 0:512], S127,
                      gsil0s[m2][:, 0:512], ALU.mult, ALU.mult, r=[pu, gsil0s[m2]], w=[hbuf[ht]])
            if ht + 1 < 24:
                load_wT(wT[2 * ((ht + 1) % 2)], "L0_ffn_w_gate", (ht + 1) * 128, 8, 0)
                load_wT(wT[2 * ((ht + 1) % 2) + 1], "L0_ffn_w_up", (ht + 1) * 128, 8, 1)
        load_wT(wTd8[0], "L0_ffn_w_down", 0, 8, 0, col0=0)
        for dt_ in range(8):
            pd0, pd1 = psf[2], psf[3]
            for blk in range(3):
                gidx = dt_ * 3 + blk
                wsl = wTd8[gidx % 2]
                if blk < 2:
                    load_wT(wTd8[(gidx + 1) % 2], "L0_ffn_w_down", dt_ * 128, 8,
                            (gidx + 1) % 2, col0=(blk + 1) * 1024)
                elif dt_ + 1 < 8:
                    load_wT(wTd8[(gidx + 1) % 2], "L0_ffn_w_down", (dt_ + 1) * 128, 8,
                            (gidx + 1) % 2, col0=0)
                for m2 in range(2):
                    pd = pd0 if m2 == 0 else pd1
                    ms0 = tbase + m2 * 512
                    for k in range(8):
                        p.mm(pd[:, 0:512], wsl[:, k * 128:(k + 1) * 128],
                             hbuf[blk * 8 + k][:, ms0 - tbase + tbase:ms0 + 512]
                             if False else hbuf[blk * 8 + k][:, m2 * 512:(m2 + 1) * 512],
                             start=(blk == 0 and k == 0), stop=(blk == 2 and k == 7),
                             r=[wsl, hbuf[blk * 8 + k]], w=[pd])
            for m2 in range(2):
                ms = slice(tbase + m2 * 512, tbase + (m2 + 1) * 512)
                pd = pd0 if m2 == 0 else pd1
                p.stt("vector", acc[dt_][:, ms], pd[:, 0:512], S127, acc[dt_][:, ms],
                      ALU.mult, ALU.add, r=[pd, acc[dt_]], w=[acc[dt_]])
    p.pop_scope()
    dump_fm("ffn0", acc)
    if upto <= 4:
        return finish(p, nc, outs, inp)

    if prefix == "cc":
        ccKo, ccVzo, cczko, ccdeps = readback

    p.push_scope()
    p.push_scope()
    cosT_s = p.sb("cosT_s", [128, 8 * 128], BF16)
    sinT_s = p.sb("sinT_s", [128, 8 * 128], BF16)
    for tt in range(8):
        pb = PBA if tt % 2 == 0 else PBB
        p.trp(pb[:, 0:128], cos_s[:, tt * 128:(tt + 1) * 128], identb[:, :], start=True,
              stop=False, r=[cos_s, identb], w=[pb])
        p.trp(pb[:, 128:256], sin_s[:, tt * 128:(tt + 1) * 128], identb[:, :], start=False,
              stop=True, r=[sin_s, identb], w=[pb])
        p.cp("vector", cosT_s[:, tt * 128:(tt + 1) * 128], pb[:, 0:128], r=[pb], w=[cosT_s])
        p.cp("vector", sinT_s[:, tt * 128:(tt + 1) * 128], pb[:, 128:256], r=[pb], w=[sinT_s])

    gnr = p.sb("gnr", [2, 128], F32)
    p.dma("gpsimd", gnr[:, :], inp["L0_attn_gnorm_w"].rearrange("(a b) -> a b", b=128), w=[gnr])
    gnc = p.sb("gnc", [128, 2], F32)
    p.tr(psf[0][:, 0:2], gnr[:, :], identf[0:2, 0:2], r=[gnr, identf], w=[psf[0]])
    p.cp("vector", gnc[:, :], psf[0][:, 0:2], r=[psf[0]], w=[gnc])
    og4 = [p.sb(f"og4_{i}", [128, 256], BF16) for i in range(8)]
    wo_t = [p.sb(f"wo{kk}", [128, D], BF16) for kk in range(4)]
    wbig = [p.sb(f"wbig{j}", [128, 8 * 512], BF16) for j in range(4)]

    def load_wpair(dst, wname, hp):
        for k in range(8):
            p.dma("gpsimd", dst[:, k * 512:(k + 1) * 512],
                  inp[wname][k * 128:(k + 1) * 128, hp * 512:(hp + 1) * 512], w=[dst])
        return dst
    ktm2 = [[p.sb(f"ktm2_{j}{b}", [128, 512], BF16) for b in range(2)] for j in range(1)]
    vtm2 = [[p.sb(f"vtm2_{j}{b}", [128, 512], BF16) for b in range(2)] for j in range(1)]
    gsil2 = [[p.sb(f"gsil2_{j}{b}", [128, 512], BF16) for b in range(2)] for j in range(1)]
    qfm4 = [[p.sb(f"qfm4_{j}{hh}", [128, 512], BF16) for hh in range(2)] for j in range(2)]
    kfm4 = [[p.sb(f"kfm4_{j}{b}", [128, 512], BF16) for b in range(2)] for j in range(2)]
    wkf4 = [[p.sb(f"wkf4_{j}{s}", [64, 128], BF16) for s in range(4)] for j in range(1)]
    wvf4 = [[p.sb(f"wvf4_{j}{s}", [64, 128], BF16) for s in range(4)] for j in range(1)]
    wktm2 = [[p.sb(f"wktm2_{j}{b}", [128, 128], BF16) for b in range(2)] for j in range(1)]
    wvtm2 = [[p.sb(f"wvtm2_{j}{b}", [128, 128], BF16) for b in range(2)] for j in range(1)]
    zkb = [[p.sb(f"zkb_{j}{b}", [128, 128], F32) for b in range(2)] for j in range(1)]
    zvb = [[p.sb(f"zvb_{j}{b}", [128, 128], F32) for b in range(2)] for j in range(1)]
    rzT2 = [[p.sb(f"rzT2_{j}{b}", [128, 256], F32) for b in range(2)] for j in range(1)]
    rt4 = [p.sb(f"rt4_{j}", [128, 512], F32) for j in range(1)]
    rt4.append(rt4[0])
    zkc4 = [p.sb(f"zkc4_{s}", [128, 1], F32) for s in range(4)]
    zvc4 = [p.sb(f"zvc4_{s}", [128, 1], F32) for s in range(4)]
    KstF4 = [p.sb(f"KstF4_{s}", [128, 128], F32) for s in range(4)]
    KstB4 = [p.sb(f"KstB4_{s}", [128, 128], BF16) for s in range(4)]
    VstF4 = [p.sb(f"VstF4_{s}", [64, 256], F32) for s in range(4)]
    VstB4 = [p.sb(f"VstB4_{s}", [64, 256], BF16) for s in range(4)]
    st4 = [p.sb(f"st4_{s}", [128, 128], BF16) for s in range(4)]
    lg4 = [p.sb(f"lg4_{s}", [128, 64], F32) for s in range(4)]
    nmx4 = [p.sb(f"nmx4_{s}", [128, 1], F32) for s in range(4)]
    sume4 = [p.sb(f"sume4_{s}", [128, 1], F32) for s in range(4)]
    et4 = [p.sb(f"et4_{s}", [128, 64], BF16) for s in range(4)]
    pttm4 = [p.sb(f"pttm4_{s}", [128, 64], BF16) for s in range(4)]
    ptfm4 = [p.sb(f"ptfm4_{s}", [64, 128], BF16) for s in range(4)]
    at4 = [p.sb(f"at4_{s}", [128, 128], BF16) for s in range(4)]
    sqs4 = [p.sb(f"sqs4_{s}", [128, 256], BF16) for s in range(1)]
    ssq4 = [p.sb(f"ssq4_{s}", [128, 1], F32) for s in range(4)]
    rstd4 = [p.sb(f"rstd4_{s}", [128, 1], F32) for s in range(4)]
    ogtm4 = [p.sb(f"ogtm4_{s}", [128, 256], BF16) for s in range(4)]

    def xn_b2t(tc):
        return [xn[k].ap.rearrange("p (b t) -> p b t", b=2)[:, :, tc * 128:(tc + 1) * 128]
                for k in range(8)]

    for hp in range(2):
        Wkp = load_wpair(wbig[0], "L0_attn_Wk", hp)
        Wvp = load_wpair(wbig[1], "L0_attn_Wv", hp)
        Wgp = load_wpair(wbig[2], "L0_attn_Wg", hp)
        Wqp = load_wpair(wbig[3], "L0_attn_Wq", hp)
        for kk in range(4):
            p.dma("gpsimd", wo_t[kk][:, :],
                  inp["L0_attn_Wo"][(hp * 4 + kk) * 128:(hp * 4 + kk + 1) * 128, :], w=[wo_t[kk]])
            p.tss("vector", wo_t[kk][:, :], wo_t[kk][:, :], gnc[0:128, kk % 2:kk % 2 + 1],
                  ALU.mult, r=[wo_t[kk], gnc], w=[wo_t[kk]])
        for s2 in range(4):
            b, hh = s2 // 2, s2 % 2
            bh = b * 4 + hp * 2 + hh
            if prefix == "cc":
                rdeps = ccdeps
                p.dma("sync", KstF4[s2][:, :], ccKo[0:128, bh * 128:(bh + 1) * 128],
                      r=rdeps, w=[KstF4[s2]])
                p.dma("gpsimd", KstB4[s2][:, :], ccKo[0:128, bh * 128:(bh + 1) * 128],
                      r=rdeps, w=[KstB4[s2]])
                p.dma("sync", VstF4[s2][:, :], ccVzo[0:64, bh * 257:bh * 257 + 256],
                      r=rdeps, w=[VstF4[s2]])
                p.dma("gpsimd", VstB4[s2][:, :], ccVzo[0:64, bh * 257:bh * 257 + 256],
                      r=rdeps, w=[VstB4[s2]])
                p.dma("sync", zvc4[s2][0:64, :], ccVzo[0:64, bh * 257 + 256:bh * 257 + 257],
                      r=rdeps, w=[zvc4[s2]], slow=True)
                p.dma("sync", zkc4[s2][0:64, :], cczko[0:64, bh:bh + 1], r=rdeps,
                      w=[zkc4[s2]], slow=True)
            else:
                p.cp("vector", KstF4[s2][:, :], preK[:, bh * 128:(bh + 1) * 128], r=[preK], w=[KstF4[s2]])
                p.cp("vector", KstB4[s2][:, :], preK[:, bh * 128:(bh + 1) * 128], r=[preK], w=[KstB4[s2]])
                p.cp("vector", VstF4[s2][:, :], preVz[0:64, bh * 257:bh * 257 + 256], r=[preVz], w=[VstF4[s2]])
                p.cp("vector", VstB4[s2][:, :], preVz[0:64, bh * 257:bh * 257 + 256], r=[preVz], w=[VstB4[s2]])
                p.cp("vector", zvc4[s2][0:64, :], preVz[0:64, bh * 257 + 256:bh * 257 + 257],
                     r=[preVz], w=[zvc4[s2]])
                p.cp("vector", zkc4[s2][0:64, :], prezk[0:64, bh:bh + 1], r=[prezk], w=[zkc4[s2]])
        for tc in range(8):
            jj = tc % 2
            ts = slice(tc * 128, (tc + 1) * 128)
            rhsb = xn_b2t(tc)
            if scanstop <= 11:
                continue
            for b in range(2):
                cs = slice(b * TC + tc * 128, b * TC + (tc + 1) * 128)
                pk = psf[b]
                for k in range(8):
                    p.mm(pk[:, 0:512], xn[k][:, cs], Wkp[:, k * 512:(k + 1) * 512],
                         start=(k == 0), stop=(k == 7), r=[xn[k], Wkp], w=[pk])
                P4 = pk.ap[:, 0:512].rearrange("p (hh half t) -> p hh half t", hh=2, half=2)
                K4 = ktm2[0][b].ap.rearrange("p (hh half t) -> p hh half t", hh=2, half=2)
                cosb = cos_s.ap[:, ts].unsqueeze(1).to_broadcast([128, 2, 128])
                sinb = sin_s.ap[:, ts].unsqueeze(1).to_broadcast([128, 2, 128])
                ra = rt4[0]
                rav = ra.ap.rearrange("p (hh t) -> p hh t", hh=2)
                p.tt("vector", rav[:, :, 0:128], P4[:, :, 0, :], cosb, ALU.mult, r=[pk, cos_s], w=[ra])
                p.tt("vector", rav[:, :, 128:256], P4[:, :, 1, :], sinb, ALU.mult, r=[pk, sin_s], w=[ra])
                p.tt("vector", K4[:, :, 0, :], rav[:, :, 0:128], rav[:, :, 128:256], ALU.subtract,
                     r=[ra], w=[ktm2[0][b]])
                p.tt("vector", rav[:, :, 0:128], P4[:, :, 1, :], cosb, ALU.mult, r=[pk, cos_s], w=[ra])
                p.tt("vector", rav[:, :, 128:256], P4[:, :, 0, :], sinb, ALU.mult, r=[pk, sin_s], w=[ra])
                p.tt("vector", K4[:, :, 1, :], rav[:, :, 0:128], rav[:, :, 128:256], ALU.add,
                     r=[ra], w=[ktm2[0][b]])
                pv = psf[2 + b]
                for k in range(8):
                    p.mm(pv[:, 0:512], xn[k][:, cs], Wvp[:, k * 512:(k + 1) * 512],
                         start=(k == 0), stop=(k == 7), r=[xn[k], Wvp], w=[pv])
                p.cp("vector", vtm2[0][b][:, :], pv[:, 0:512], r=[pv], w=[vtm2[0][b]])
            if scanstop <= 12:
                continue
            for b in range(2):
                cs = slice(b * TC + tc * 128, b * TC + (tc + 1) * 128)
                pg = psf[4 + b]
                for k in range(8):
                    p.mm(pg[:, 0:512], xn[k][:, cs], Wgp[:, k * 512:(k + 1) * 512],
                         start=(k == 0), stop=(k == 7), r=[xn[k], Wgp], w=[pg])
                p.act(gsil2[0][b][:, :], pg[:, 0:512], AF.Silu, r=[pg], w=[gsil2[0][b]])
            for hh in range(2):
                pq = psf[hh]
                for dt_ in range(2):
                    for k in range(8):
                        p.mm(pq[:, dt_ * 256:(dt_ + 1) * 256],
                             Wqp[:, k * 512 + hh * 256 + dt_ * 128:k * 512 + hh * 256 + (dt_ + 1) * 128],
                             rhsb[k],
                             start=(dt_ == 0 and k == 0), stop=(dt_ == 1 and k == 7),
                             r=[Wqp, xn[k]], w=[pq])
                P4 = pq.ap[:, 0:512].rearrange("p (dt b t) -> p dt b t", dt=2, b=2)
                Q4 = qfm4[jj][hh].ap.rearrange("p (dt b t) -> p dt b t", dt=2, b=2)
                cosTb = cosT_s.ap[:, ts].unsqueeze(1).to_broadcast([128, 2, 128])
                sinTb = sinT_s.ap[:, ts].unsqueeze(1).to_broadcast([128, 2, 128])
                ra = rt4[0]
                rav = ra.ap.rearrange("p (x t) -> p x t", x=2)
                p.tt("vector", rav[:, :, 0:128], P4[:, 0, :, :], cosTb, ALU.mult, r=[pq, cosT_s], w=[ra])
                p.tt("vector", rav[:, :, 128:256], P4[:, 1, :, :], sinTb, ALU.mult, r=[pq, sinT_s], w=[ra])
                p.tt("vector", Q4[:, 0, :, :], rav[:, :, 0:128], rav[:, :, 128:256], ALU.subtract,
                     r=[ra], w=[qfm4[jj][hh]])
                p.tt("vector", rav[:, :, 0:128], P4[:, 1, :, :], cosTb, ALU.mult, r=[pq, cosT_s], w=[ra])
                p.tt("vector", rav[:, :, 128:256], P4[:, 0, :, :], sinTb, ALU.mult, r=[pq, sinT_s], w=[ra])
                p.tt("vector", Q4[:, 1, :, :], rav[:, :, 0:128], rav[:, :, 128:256], ALU.add,
                     r=[ra], w=[qfm4[jj][hh]])
            if scanstop <= 13:
                continue
            for b in range(2):
                cs = slice(b * TC + tc * 128, b * TC + (tc + 1) * 128)
                pw = psf[2 + b]
                for hh in range(2):
                    h = hp * 2 + hh
                    p.mm(pw[0:64, hh * 128:(hh + 1) * 128], skw2[:, h * 64:(h + 1) * 64],
                         t1k[:, cs], start=(hh == 0), stop=False, r=[skw2, t1k], w=[pw])
                    p.mm(pw[0:64, 256 + hh * 128:256 + (hh + 1) * 128], svw2[:, h * 64:(h + 1) * 64],
                         t1v[:, cs], start=False, stop=(hh == 1), r=[svw2, t1v], w=[pw])
                wtk = rt4[b % 2]
                for hh in range(2):
                    h = hp * 2 + hh
                    p.ts2("vector", wtk[0:64, hh * 128:(hh + 1) * 128],
                          pw[0:64, hh * 128:(hh + 1) * 128], skb2c[0:64, h:h + 1], 32.0,
                          ALU.add, ALU.min, r=[pw, skb2c], w=[wtk])
                    p.ts2("vector", wtk[0:64, 256 + hh * 128:256 + (hh + 1) * 128],
                          pw[0:64, 256 + hh * 128:256 + (hh + 1) * 128], svb2c[0:64, h:h + 1], 32.0,
                          ALU.add, ALU.min, r=[pw, svb2c], w=[wtk])
                p.tss("vector", wtk[0:64, 0:512], wtk[0:64, 0:512], -32.0, ALU.max, r=[wtk], w=[wtk])
                for hh in range(2):
                    s2 = b * 2 + hh
                    p.act(wkf4[0][s2][0:64, :], wtk[0:64, hh * 128:(hh + 1) * 128], AF.Exp,
                          r=[wtk], w=[wkf4[0][s2]])
                    p.act(wvf4[0][s2][0:64, :], wtk[0:64, 256 + hh * 128:256 + (hh + 1) * 128],
                          AF.Exp, r=[wtk], w=[wvf4[0][s2]])
                zk_ = zkb[0][b]
                zv_ = zvb[0][b]
                for hh in range(2):
                    s2 = b * 2 + hh
                    p.scan_cumsum(zk_[hh * 64:(hh + 1) * 64, :], wkf4[0][s2][0:64, :],
                                  zkc4[s2][0:64, 0:1], r=[wkf4[0][s2], zkc4[s2]], w=[zk_])
                    p.cp("vector", zkc4[s2][0:64, :], zk_[hh * 64:(hh + 1) * 64, 127:128],
                         r=[zk_], w=[zkc4[s2]])
                    p.scan_cumsum(zv_[hh * 64:(hh + 1) * 64, :], wvf4[0][s2][0:64, :],
                                  zvc4[s2][0:64, 0:1], r=[wvf4[0][s2], zvc4[s2]], w=[zv_])
                    p.cp("vector", zvc4[s2][0:64, :], zv_[hh * 64:(hh + 1) * 64, 127:128],
                         r=[zv_], w=[zvc4[s2]])
                p.recip(zk_[:, :], zk_[:, :], r=[zk_], w=[zk_])
                p.recip(zv_[:, :], zv_[:, :], r=[zv_], w=[zv_])
                prz = psf[2 + b]
                p.tr(prz[:, 0:128], zk_[:, :], identf[:, :], r=[zk_, identf], w=[prz])
                p.cp("vector", rzT2[0][b][:, 0:128], prz[:, 0:128], r=[prz], w=[rzT2[0][b]])
                p.tr(prz[:, 128:256], zv_[:, :], identf[:, :], r=[zv_, identf], w=[prz])
                p.cp("vector", rzT2[0][b][:, 128:256], prz[:, 128:256], r=[prz], w=[rzT2[0][b]])
            if scanstop <= 14:
                continue
            for b in range(2):
                pt = PBA if b == 0 else PBB
                for hh in range(2):
                    s2 = b * 2 + hh
                    p.trp(pt[:, hh * 64:(hh + 1) * 64], wkf4[0][s2][0:64, :], identb[0:64, 0:64],
                          start=(hh == 0), stop=False, r=[wkf4[0][s2], identb], w=[pt])
                    p.trp(pt[:, 128 + hh * 64:128 + (hh + 1) * 64], wvf4[0][s2][0:64, :],
                          identb[0:64, 0:64], start=False, stop=(hh == 1),
                          r=[wvf4[0][s2], identb], w=[pt])
                eng_b = "vector" if b == 0 else "scalar"
                p.cp(eng_b, wktm2[0][b][:, :], pt[:, 0:128], r=[pt], w=[wktm2[0][b]])
                p.cp(eng_b, wvtm2[0][b][:, :], pt[:, 128:256], r=[pt], w=[wvtm2[0][b]])
            if scanstop <= 15:
                continue
            for b in range(2):
                pt = PBA if b == 0 else PBB
                for c4 in range(4):
                    p.trp(pt[:, c4 * 128:(c4 + 1) * 128], ktm2[0][b][:, c4 * 128:(c4 + 1) * 128],
                          identb[:, :], start=(c4 == 0), stop=(c4 == 3),
                          r=[ktm2[0][b], identb], w=[pt])
                p.cp("vector", kfm4[jj][b][:, :], pt[:, 0:512], r=[pt], w=[kfm4[jj][b]])

            def S(s2):
                return s2 // 2, s2 % 2
            if scanstop <= 1:
                continue
            for s2 in range(4):
                b, hh = S(s2)
                pst = psf[s2]
                for dt_ in range(2):
                    p.mm(pst[:, 0:128], kfm4[jj][b][:, hh * 256 + dt_ * 128:hh * 256 + (dt_ + 1) * 128],
                         qfm4[jj][hh][:, dt_ * 256 + b * 128:dt_ * 256 + (b + 1) * 128],
                         start=(dt_ == 0), stop=(dt_ == 1), r=[kfm4[jj][b], qfm4[jj][hh]], w=[pst])
            for s2 in range(4):
                p.tt("vector", st4[s2][:, :], psf[s2][:, 0:128], utmask[:, :], ALU.mult,
                     r=[psf[s2], utmask], w=[st4[s2]])
            for s2 in range(4):
                b, hh = S(s2)
                plg = psf[4 + s2 % 2]
                off = (s2 // 2) * 128
                for dt_ in range(2):
                    p.mm(plg[:, off + 0:off + 64],
                         qfm4[jj][hh][:, dt_ * 256 + b * 128:dt_ * 256 + (b + 1) * 128],
                         KstB4[s2][:, dt_ * 64:(dt_ + 1) * 64],
                         start=(dt_ == 0 and s2 < 2), stop=False,
                         r=[qfm4[jj][hh], KstB4[s2]], w=[plg])
                p.mm(plg[:, off + 0:off + 64], st4[s2][:, :],
                     wktm2[0][b][:, hh * 64:(hh + 1) * 64],
                     start=False, stop=(s2 >= 2), r=[st4[s2], wktm2[0][b]], w=[plg])
            for s2 in range(4):
                b, hh = S(s2)
                plg = psf[4 + s2 % 2]
                off = (s2 // 2) * 128
                p.stt("vector", lg4[s2][:, :], plg[:, off:off + 64], SCALE,
                      rzT2[0][b][:, hh * 64:(hh + 1) * 64], ALU.mult, ALU.mult,
                      r=[plg, rzT2[0][b]], w=[lg4[s2]])
            for s2 in range(4):
                b, hh = S(s2)
                p.red("vector", nmx4[s2][:, :], lg4[s2][:, :], ALU.max, negate=True,
                      r=[lg4[s2]], w=[nmx4[s2]])
                p.act(et4[s2][:, :], lg4[s2][:, :], AF.Exp, bias=nmx4[s2][0:128, 0:1],
                      accum_ap=sume4[s2][:, :], r=[lg4[s2], nmx4[s2]], w=[et4[s2], sume4[s2]])
                p.recip(sume4[s2][:, :], sume4[s2][:, :], r=[sume4[s2]], w=[sume4[s2]])
                p.stt("vector", pttm4[s2][:, :], et4[s2][:, :], sume4[s2][0:128, 0:1],
                      rzT2[0][b][:, 128 + hh * 64:128 + (hh + 1) * 64], ALU.mult, ALU.mult,
                      r=[et4[s2], sume4[s2], rzT2[0][b]], w=[pttm4[s2]])
            for b in range(2):
                pt = PBA if b == 0 else PBB
                for hh in range(2):
                    s2 = b * 2 + hh
                    p.trp(pt[0:64, hh * 128:(hh + 1) * 128], pttm4[s2][:, :], identb[:, :],
                          start=(hh == 0), stop=(hh == 1), r=[pttm4[s2], identb], w=[pt])
                for hh in range(2):
                    s2 = b * 2 + hh
                    p.cp("vector" if b == 0 else "scalar", ptfm4[s2][:, :],
                         pt[0:64, hh * 128:(hh + 1) * 128], r=[pt], w=[ptfm4[s2]])
            if scanstop <= 2:
                continue
            for s2 in range(4):
                b, hh = S(s2)
                pat = psf[s2]
                p.mm(pat[:, 0:128], wvf4[0][s2][0:64, :], ptfm4[s2][:, :],
                     start=True, stop=True, r=[wvf4[0][s2], ptfm4[s2]], w=[pat])
            for s2 in range(4):
                p.tt("vector", at4[s2][:, :], psf[s2][:, 0:128], utmask[:, :], ALU.mult,
                     r=[psf[s2], utmask], w=[at4[s2]])
            for s2 in range(4):
                b, hh = S(s2)
                po = psf[4 + s2 % 2]
                off = (s2 // 2) * 256
                p.mm(po[:, off:off + 256], ptfm4[s2][:, :], VstB4[s2][:, :],
                     start=(s2 < 2), stop=False, r=[ptfm4[s2], VstB4[s2]], w=[po])
                p.mm(po[:, off:off + 256], at4[s2][:, :],
                     vtm2[0][b].ap[:, hh * 256:(hh + 1) * 256], start=False, stop=(s2 >= 2),
                     r=[at4[s2], vtm2[0][b]], w=[po])
            for s2 in range(4):
                b, hh = S(s2)
                po = psf[4 + s2 % 2]
                off = (s2 // 2) * 256
                p.act(sqs4[0][:, :], po[:, off:off + 256], AF.Square,
                      accum_ap=ssq4[s2][:, :], r=[po], w=[sqs4[0], ssq4[s2]])
                p.act(rstd4[s2][:, :], ssq4[s2][:, :], AF.Sqrt, bias=EPS, scale=1.0 / 256.0,
                      r=[ssq4[s2]], w=[rstd4[s2]])
                p.recip(rstd4[s2][:, :], rstd4[s2][:, :], r=[rstd4[s2]], w=[rstd4[s2]])
                p.stt("vector", ogtm4[s2][:, :], po[:, off:off + 256], rstd4[s2][0:128, 0:1],
                      gsil2[0][b].ap[:, hh * 256:(hh + 1) * 256], ALU.mult, ALU.mult,
                      r=[po, rstd4[s2], gsil2[0][b]], w=[ogtm4[s2]])
            for b in range(2):
                pt = PBA if b == 0 else PBB
                for hh in range(2):
                    s2 = b * 2 + hh
                    for dt_ in range(2):
                        p.trp(pt[:, hh * 256 + dt_ * 128:hh * 256 + (dt_ + 1) * 128],
                              ogtm4[s2][:, dt_ * 128:(dt_ + 1) * 128], identb[:, :],
                              start=(hh == 0 and dt_ == 0), stop=(hh == 1 and dt_ == 1),
                              r=[ogtm4[s2], identb], w=[pt])
                for hh in range(2):
                    s2 = b * 2 + hh
                    for dt_ in range(2):
                        p.cp("vector" if b == 0 else "scalar",
                             og4[b * 4 + hh * 2 + dt_][:, (tc % 2) * 128:(tc % 2 + 1) * 128],
                             pt[:, hh * 256 + dt_ * 128:hh * 256 + (dt_ + 1) * 128],
                             r=[pt], w=[og4[b * 4 + hh * 2 + dt_]])
            if scanstop <= 3:
                continue
            for s2 in range(4):
                b, hh = S(s2)
                pk4 = psf[s2]
                p.mm(pk4[:, 0:64], ktm2[0][b].ap[:, hh * 256:hh * 256 + 128],
                     wktm2[0][b][:, hh * 64:(hh + 1) * 64],
                     start=True, stop=False, r=[ktm2[0][b], wktm2[0][b]], w=[pk4])
                p.mm(pk4[:, 64:128], ktm2[0][b].ap[:, hh * 256 + 128:hh * 256 + 256],
                     wktm2[0][b][:, hh * 64:(hh + 1) * 64],
                     start=False, stop=True, r=[ktm2[0][b], wktm2[0][b]], w=[pk4])
            for s2 in range(4):
                p.tt("vector", KstF4[s2][:, :], KstF4[s2][:, :], psf[s2][:, 0:128], ALU.add,
                     r=[KstF4[s2], psf[s2]], w=[KstF4[s2]])
                p.cp("vector", KstB4[s2][:, :], KstF4[s2][:, :], r=[KstF4[s2]], w=[KstB4[s2]])
            for s2 in range(4):
                b, hh = S(s2)
                pv5 = psf[4 + s2 % 2]
                off = (s2 // 2) * 256
                p.mm(pv5[0:64, off:off + 256], wvtm2[0][b][:, hh * 64:(hh + 1) * 64],
                     vtm2[0][b].ap[:, hh * 256:(hh + 1) * 256],
                     start=(s2 < 2), stop=(s2 >= 2), r=[wvtm2[0][b], vtm2[0][b]], w=[pv5])
            for s2 in range(4):
                pv5 = psf[4 + s2 % 2]
                off = (s2 // 2) * 256
                p.tt("vector", VstF4[s2][:, :], VstF4[s2][:, :], pv5[0:64, off:off + 256], ALU.add,
                     r=[VstF4[s2], pv5], w=[VstF4[s2]])
                p.cp("vector", VstB4[s2][:, :], VstF4[s2][:, :], r=[VstF4[s2]], w=[VstB4[s2]])
            if tc % 2 == 1 and scanstop > 4:
                for b in range(2):
                    for dt_ in range(8):
                        pw = psf[(dt_) % 4]
                        for kk in range(4):
                            p.mm(pw[:, 0:256], wo_t[kk][:, dt_ * 128:(dt_ + 1) * 128],
                                 og4[b * 4 + kk][:, 0:256],
                                 start=(kk == 0), stop=(kk == 3), r=[wo_t[kk], og4[b * 4 + kk]], w=[pw])
                        ms = slice(b * TC + (tc - 1) * 128, b * TC + (tc + 1) * 128)
                        p.tt("vector", acc[dt_][:, ms], acc[dt_][:, ms], pw[:, 0:256], ALU.add,
                             r=[acc[dt_], pw], w=[acc[dt_]])
    p.pop_scope()
    p.pop_scope()
    p.pop_scope()
    dump_fm("x1", acc)
    if upto <= 6:
        return finish(p, nc, outs, inp)

    layernorm_fm(acc, xn, "ln1")
    dump_fm("xn1", xn)

    p.push_scope()
    def bias_cols(name, tag):
        br = p.sb(f"br_{tag}", [8, 128], F32)
        p.dma("sync", br[:, :], inp[name].rearrange("(a b) -> a b", b=128), w=[br])
        bc = p.sb(f"bc_{tag}", [128, 8], F32)
        p.tr(psf[5][0:128, 0:8], br[:, :], identf[0:8, 0:8], r=[br, identf], w=[psf[5]])
        p.cp("vector", bc[:, :], psf[5][0:128, 0:8], r=[psf[5]], w=[bc])
        return bc

    garowfm = p.sb("garowfm", [2, 16], F32)
    ga_in = inp["L1_attn_grep_a"].rearrange("a b c d -> (a c d) b")
    for t in range(2):
        dstv = garowfm.ap[t:t + 1, :].rearrange("a (h two) -> a h two", two=2)
        for j in range(2):
            p.dma("gpsimd", dstv[:, :, j:j + 1].squeeze(2), ga_in[0:1, t * 8:(t + 1) * 8],
                  w=[garowfm], slow=True)
    BUCKETS = [[0, 17, 18], [1, 0, 17]]
    pbrow6 = p.sb("pbrow6", [6, 16], F32)
    for js in range(3):
        for t in range(2):
            r_ = js * 2 + t
            dstv = pbrow6.ap[r_:r_ + 1, :].rearrange("a (h two) -> a h two", two=2)
            for parity in range(2):
                bk = BUCKETS[parity][js]
                p.dma("gpsimd", dstv[:, :, parity:parity + 1].squeeze(2),
                      inp["L1_attn_rel_emb"][bk:bk + 1, t * 8:(t + 1) * 8],
                      w=[pbrow6], slow=True)

    qbc = bias_cols("L1_attn_q_b", "qb")
    kbc = bias_cols("L1_attn_k_b", "kb")
    vbc = bias_cols("L1_attn_v_b", "vb")
    obc = bias_cols("L1_attn_out_b", "ob")

    dupm = p.sb("dupm_s", [128, 256], BF16)
    selm = p.sb("selm_s", [128, 512], BF16)
    segm = p.sb("segm_s", [128, 128], BF16)
    gmaskm = p.sb("gmaskm_s", [128, 128], F32)
    bcP = p.sb("bcP", [16, 2112], BF16)
    p.dma("sync", dupm[:, :], inp["dupm"], w=[dupm])
    p.dma("sync", selm[:, :], inp["selm"], w=[selm])
    p.dma("sync", segm[:, :], inp["segm"], w=[segm])
    p.dma("sync", gmaskm[:, :], inp["gmaskm"], w=[gmaskm])
    p.dma("sync", bcP[:, :], inp["bcastP"], w=[bcP])

    def bias_fm(name, tag):
        b64 = p.sb(f"b64_{tag}", [16, 64], F32)
        p.dma("sync", b64[:, :], inp[name].rearrange("a b c -> (a b) c"), w=[b64])
        p.tr(psf[5][0:64, 0:16], b64[:, :], identf[0:16, 0:16], r=[b64, identf], w=[psf[5]])
        bT = p.sb(f"bT_{tag}", [64, 16], BF16)
        p.cp("vector", bT[:, :], psf[5][0:64, 0:16], r=[psf[5]], w=[bT])
        p.mm(psf[4][0:128, 0:16], dupm[0:64, 0:128], bT[:, :], start=True, stop=True,
             r=[dupm, bT], w=[psf[4]])
        bfm = p.sb(f"bfm_{tag}", [128, 16], F32)
        p.cp("vector", bfm[:, :], psf[4][0:128, 0:16], r=[psf[4]], w=[bfm])
        return bfm

    kbias_fm = bias_fm("L1_attn_bias_k", "kb")
    vbias_fm = bias_fm("L1_attn_bias_v", "vb")

    gwsb = p.sb("gwsb", [64, 8], F32)
    p.dma("sync", gwsb[:, :], inp["L1_attn_grep_w"], w=[gwsb])
    gws = p.sb("gws", [64, 1], F32)
    p.red("vector", gws[:, :], gwsb[:, :], ALU.add, r=[gwsb], w=[gws])
    gws16 = p.sb("gws16", [64, 1], BF16)
    p.cp("vector", gws16[:, :], gws[:, :], r=[gws], w=[gws16])
    p.mm(psf[4][0:128, 0:1], dupm[0:64, 0:128], gws16[:, :], start=True, stop=True,
         r=[dupm, gws16], w=[psf[4]])
    gws2 = p.sb("gws2", [128, 1], F32)
    p.cp("vector", gws2[:, :], psf[4][0:128, 0:1], r=[psf[4]], w=[gws2])
    gsegs = p.sb("gsegs", [128, 128], BF16)
    p.tss("vector", gsegs[:, :], gmaskm[:, :], gws2[0:128, 0:1], ALU.mult,
          r=[gmaskm, gws2], w=[gsegs])
    gbr = p.sb("gbr", [1, 8], F32)
    p.dma("sync", gbr[:, :], inp["L1_attn_grep_b"].rearrange("(a b) -> a b", a=1), w=[gbr])
    gbs = p.sb("gbs", [1, 1], F32)
    p.red("vector", gbs[:, :], gbr[:, :], ALU.add, r=[gbr], w=[gbs])
    gb_col = p.sb("gb_col", [128, 1], F32)
    p.mm(psf[5][:, 0:1], onesf[0:1, :], gbs[0:1, :], start=True, stop=True, r=[onesf, gbs], w=[psf[5]])
    p.cp("vector", gb_col[:, :], psf[5][:, 0:1], r=[psf[5]], w=[gb_col])

    ga2 = p.sb("ga2", [16, 2], F32)
    pb_fm = p.sb("pb_fm", [16, 6], F32)

    def a1_gapb_setup():
        p.tr(psf[5][0:16, 0:2], garowfm[:, :], identf[0:2, 0:2], r=[garowfm, identf], w=[psf[5]])
        p.cp("vector", ga2[:, :], psf[5][0:16, 0:2], r=[psf[5]], w=[ga2])
        p.tr(psf[5][0:16, 0:6], pbrow6[:, :], identf[0:6, 0:6], r=[pbrow6, identf], w=[psf[5]])
        p.cp("vector", pb_fm[:, :], psf[5][0:16, 0:6], r=[psf[5]], w=[pb_fm])

    wq1 = [p.sb(f"wq1_{k}", [128, D], BF16) for k in range(8)]
    wk1 = [p.sb(f"wk1_{k}", [128, D], BF16) for k in range(8)]
    wv1 = [p.sb(f"wv1_{k}", [128, D], BF16) for k in range(8)]
    outw = [p.sb(f"outw{k}", [128, D], BF16) for k in range(8)]
    for k in range(8):
        p.dma("gpsimd", wq1[k][:, :], inp["L1_attn_q_w"][k * 128:(k + 1) * 128, :], w=[wq1[k]])
        p.dma("gpsimd", wk1[k][:, :], inp["L1_attn_k_w"][k * 128:(k + 1) * 128, :], w=[wk1[k]])
        p.dma("gpsimd", wv1[k][:, :], inp["L1_attn_v_w"][k * 128:(k + 1) * 128, :], w=[wv1[k]])
        p.dma("gpsimd", outw[k][:, :], inp["L1_attn_out_w"][k * 128:(k + 1) * 128, :], w=[outw[k]])

    qt = [p.sb(f"a1qt{ft}", [128, 256], BF16) for ft in range(8)]
    kt = [p.sb(f"a1kt{ft}", [128, 256], BF16) for ft in range(8)]
    vt = [[p.sb(f"a1vt{pr_}_{ft}", [128, 256], BF16) for ft in range(8)] for pr_ in range(2)]
    prodk = [p.sb(f"a1pk{i}", [128, 384], BF16) for i in range(3)]
    attfm = p.sb("a1attfm", [128, 2048], BF16)
    etmpB = [p.sb(f"a1etmp{i}", [128, 256], BF16) for i in range(2)]
    vselsb = [p.sb(f"a1vsel{i}", [128, 128], BF16) for i in range(2)]
    lg3 = p.sb("a1lg3", [16, 768], F32)
    glsb = p.sb("a1glsb", [16, 256], F32)
    mx2 = p.sb("a1mx2", [16, 256], F32)
    pj3 = [p.sb(f"a1pj{i}", [16, 768], BF16) for i in range(2)]

    def xn_b2(st_i):
        return [xn[k].ap.rearrange("p (b t) -> p b t", b=2)[:, :, st_i * 128:(st_i + 1) * 128]
                for k in range(8)]

    def a1_proj(wt, bc, dst, st_i):
        rhsv = xn_b2(st_i)
        for ft in range(8):
            pp = psf[ft % 2]
            for k in range(8):
                p.mm(pp[:, 0:256], wt[k][:, ft * 128:(ft + 1) * 128], rhsv[k],
                     start=(k == 0), stop=(k == 7), r=[wt[k], xn[k]], w=[pp])
            if ft % 2 == 0:
                p.tss("vector", dst[ft][:, :], pp[:, 0:256], bc[0:128, ft:ft + 1],
                      ALU.add, r=[pp, bc], w=[dst[ft]])
            else:
                p.act(dst[ft][:, :], pp[:, 0:256], AF.Identity, bias=bc[0:128, ft:ft + 1],
                      r=[pp, bc], w=[dst[ft]])

    def a1_kdup(t, ft):
        hp = t * 8 + ft
        idx = t * 8 + ft
        bank = psf[2 + idx % 2]
        ndata = max(0, min(3, 32 - 3 * hp))
        eng = "vector"
        for js in range(ndata):
            ck = 3 * hp + js
            tk, hk = ck // 16, ck % 16
            p.mm(bank[:, js * 128:(js + 1) * 128],
                 dupm[:, (hk % 2) * 128:(hk % 2 + 1) * 128],
                 kt[hk // 2][:, tk * 128:(tk + 1) * 128],
                 start=(js == 0), stop=(js == ndata - 1), r=[dupm, kt[hk // 2]], w=[bank])
        q3 = qt[ft].ap[:, t * 128:(t + 1) * 128]
        pkb = prodk[(t * 8 + ft) % 3]
        pk3 = pkb.ap[:, 0:384].rearrange("p (j s) -> p j s", s=128)
        if ndata == 3:
            p.tt(eng, pk3, q3.unsqueeze(1).to_broadcast([128, 3, 128]),
                 bank.ap[:, 0:384].rearrange("p (j s) -> p j s", s=128),
                 ALU.mult, r=[qt[ft], bank], w=[pkb])
        elif ndata == 0:
            cb0 = 3 * hp - 32
            p.tt(eng, pk3, q3.unsqueeze(1).to_broadcast([128, 3, 128]),
                 kbias_fm.ap[:, cb0:cb0 + 3].unsqueeze(2).to_broadcast([128, 3, 128]),
                 ALU.mult, r=[qt[ft], kbias_fm], w=[pkb])
        else:
            p.tt(eng, pkb.ap[:, 0:256].rearrange("p (j s) -> p j s", s=128),
                 q3.unsqueeze(1).to_broadcast([128, 2, 128]),
                 bank.ap[:, 0:256].rearrange("p (j s) -> p j s", s=128),
                 ALU.mult, r=[qt[ft], bank], w=[pkb])
            cb = 3 * hp + 2 - 32
            p.tss(eng, pkb[:, 256:384], q3, kbias_fm[0:128, cb:cb + 1],
                  ALU.mult, r=[qt[ft], kbias_fm], w=[pkb])

    def a1_seg(t, ft):
        st_g = (t == 0 and ft == 0)
        sp_g = (t == 1 and ft == 7)
        for js in range(3):
            if js == 0:
                area, c0 = psf[4], t * 128
            elif js == 1:
                area, c0 = psf[5], t * 128
            else:
                area, c0 = psf[0], 256 + t * 128
            pkb = prodk[(t * 8 + ft) % 3]
            p.mm(area[0:16, c0:c0 + 128], segm[:, ft * 16:(ft + 1) * 16],
                 pkb[:, js * 128:(js + 1) * 128],
                 start=st_g, stop=sp_g, r=[segm, pkb], w=[area])
        p.mm(psf[1][0:16, 256 + t * 128:256 + (t + 1) * 128],
             gsegs[:, ft * 16:(ft + 1) * 16], qt[ft][:, t * 128:(t + 1) * 128],
             start=st_g, stop=sp_g, r=[gsegs, qt[ft]], w=[psf[1]])

    def a1_projall(st_i):
        a1_proj(wq1, qbc, qt, st_i)
        a1_proj(wk1, kbc, kt, st_i)
        a1_proj(wv1, vbc, vt[st_i % 2], st_i)

    def a1_logits(st_i):
        for t in range(2):
            a1_kdup(t, 0)
            for ft in range(1, 8):
                a1_kdup(t, ft)
                a1_seg(t, ft - 1)
            a1_seg(t, 7)

    def a1_softmax(st_i):
        par = st_i % 2
        p.cp("vector", lg3[:, 0:256], psf[4][0:16, 0:256], r=[psf[4]], w=[lg3])
        p.cp("vector", lg3[:, 256:512], psf[5][0:16, 0:256], r=[psf[5]], w=[lg3])
        p.cp("vector", lg3[:, 512:768], psf[0][0:16, 256:512], r=[psf[0]], w=[lg3])
        p.cp("scalar", glsb[:, :], psf[1][0:16, 256:512], r=[psf[1]], w=[glsb])
        p.act(glsb[:, :], glsb[:, :], AF.Sigmoid, bias=gb_col[0:16, 0:1],
              r=[glsb, gb_col], w=[glsb])
        for t in range(2):
            p.tss("vector", glsb[:, t * 128:(t + 1) * 128], glsb[:, t * 128:(t + 1) * 128],
                  ga2[0:16, t:t + 1], ALU.mult, r=[glsb, ga2], w=[glsb])
        for js in range(3):
            for t in range(2):
                sl = slice(js * 256 + t * 128, js * 256 + (t + 1) * 128)
                p.stt("vector", lg3.ap[:, sl], glsb[:, t * 128:(t + 1) * 128],
                      pb_fm[0:16, js * 2 + t:js * 2 + t + 1], lg3.ap[:, sl],
                      ALU.mult, ALU.add, r=[glsb, pb_fm, lg3], w=[lg3])
        p.tt("vector", mx2[:, :], lg3[:, 0:256], lg3[:, 256:512], ALU.max, r=[lg3], w=[mx2])
        p.tt("vector", mx2[:, :], mx2[:, :], lg3[:, 512:768], ALU.max, r=[lg3, mx2], w=[mx2])
        for js in range(3):
            p.tt("vector", lg3[:, js * 256:(js + 1) * 256], lg3[:, js * 256:(js + 1) * 256],
                 mx2[:, :], ALU.subtract, r=[lg3, mx2], w=[lg3])
        p.act(lg3[:, 0:768], lg3[:, 0:768], AF.Exp, r=[lg3], w=[lg3])
        p.tt("vector", mx2[:, :], lg3[:, 0:256], lg3[:, 256:512], ALU.add, r=[lg3], w=[mx2])
        p.tt("vector", mx2[:, :], mx2[:, :], lg3[:, 512:768], ALU.add, r=[lg3, mx2], w=[mx2])
        p.recip(mx2[:, :], mx2[:, :], r=[mx2], w=[mx2])
        for js in range(3):
            p.tt("vector", pj3[par][:, js * 256:(js + 1) * 256], lg3[:, js * 256:(js + 1) * 256],
                 mx2[:, :], ALU.mult, r=[lg3, mx2], w=[pj3[par]])

    def a1_passB(st_i):
        par = st_i % 2
        vts, pj = vt[par], pj3[par]
        for js in range(3):
            for fo in range(8):
                bank = psf[2 + fo % 2]
                eng = "vector"
                vss = vselsb[fo % 2]
                data = []
                for hh in range(2):
                    ck = 3 * (2 * fo + hh) + js
                    if ck < 32:
                        data.append((hh, ck % 16, ck // 16))
                nmm = 4 + len(data)
                i_mm = 0
                for jt in range(2):
                    for hh in range(2):
                        m = 4 * fo + 2 * hh + jt
                        t, hq = m // 16, m % 16
                        o_ = 128 * hq - 64 * hh + 64
                        p.mm(bank[:, jt * 128:(jt + 1) * 128],
                             bcP[:, o_:o_ + 128],
                             pj[0:16, js * 256 + t * 128:js * 256 + (t + 1) * 128],
                             start=(i_mm == 0), stop=(i_mm == nmm - 1), r=[bcP, pj], w=[bank])
                        i_mm += 1
                vc0 = 256 + (js % 2) * 128
                for (hh, hk, tk) in data:
                    p.mm(bank[:, vc0:vc0 + 128],
                         selm[:, (hh * 2 + hk % 2) * 128:(hh * 2 + hk % 2 + 1) * 128],
                         vts[hk // 2][:, tk * 128:(tk + 1) * 128],
                         start=(i_mm == 0), stop=(i_mm == nmm - 1),
                         r=[selm, vts[hk // 2]], w=[bank])
                    i_mm += 1
                if data:
                    p.cp(eng, vss[:, :], bank[:, vc0:vc0 + 128], r=[bank], w=[vss])
                tbuf = attfm if js == 0 else etmpB[fo % 2]
                toff = fo * 256 if js == 0 else 0
                if a1stop < 6:
                    continue
                if len(data) == 2:
                    dst = tbuf.ap[:, toff:toff + 256].rearrange("p (j s) -> p j s", s=128)
                    p.tt(eng, dst, bank.ap[:, 0:256].rearrange("p (j s) -> p j s", s=128),
                         vss.ap[:, :].unsqueeze(1).to_broadcast([128, 2, 128]),
                         ALU.mult, r=[bank, vss], w=[tbuf])
                else:
                    for hh in range(2):
                        lo, hi = hh * 64, (hh + 1) * 64
                        dst = tbuf.ap[lo:hi, toff:toff + 256].rearrange("p (j s) -> p j s", s=128)
                        src0 = bank.ap[lo:hi, 0:256].rearrange("p (j s) -> p j s", s=128)
                        if data and data[0][0] == hh:
                            in1 = vss.ap[lo:hi, :].unsqueeze(1).to_broadcast([64, 2, 128])
                            p.tt(eng, dst, src0, in1, ALU.mult, r=[bank, vss], w=[tbuf])
                        else:
                            cb = 3 * (2 * fo + hh) + js - 32
                            in1 = vbias_fm.ap[lo:hi, cb:cb + 1].unsqueeze(2).to_broadcast(
                                [64, 2, 128])
                            p.tt(eng, dst, src0, in1, ALU.mult, r=[bank, vbias_fm], w=[tbuf])
                if js > 0:
                    p.tt("gpsimd", attfm[:, fo * 256:(fo + 1) * 256],
                         attfm[:, fo * 256:(fo + 1) * 256], etmpB[fo % 2][:, 0:256],
                         ALU.add, r=[etmpB[fo % 2], attfm], w=[attfm])
        if a1stop < 7:
            return
        for dt_ in range(8):
            pp = psf[4 + dt_ % 2]
            c0 = 0
            for fo in range(8):
                p.mm(pp[:, c0:c0 + 256], outw[fo][:, dt_ * 128:(dt_ + 1) * 128],
                     attfm[:, fo * 256:(fo + 1) * 256],
                     start=(fo == 0), stop=(fo == 7), r=[outw[fo], attfm], w=[pp])
            accv = acc[dt_].ap.rearrange("p (b t) -> p b t", b=2)[:, :, st_i * 128:(st_i + 1) * 128]
            ppv = pp.ap[:, c0:c0 + 256].rearrange("p (b t) -> p b t", b=2)
            p.stt("vector", accv, ppv, obc[0:128, dt_:dt_ + 1],
                  accv, ALU.add, ALU.add, r=[pp, obc, acc[dt_]], w=[acc[dt_]])

    if a1stop >= 2:
        for st_i in range(8):
            a1_projall(st_i)
            if st_i == 0:
                a1_gapb_setup()
            if a1stop >= 5 and st_i > 0:
                a1_passB(st_i - 1)
            if a1stop >= 3:
                a1_logits(st_i)
            if a1stop >= 4:
                a1_softmax(st_i)
        if a1stop >= 5:
            a1_passB(7)
    p.pop_scope()
    dump_fm("attn1", acc)
    if upto <= 8:
        return finish(p, nc, outs, inp)

    p.push_scope()
    msc_fm = p.sb("msc_fm", [16, NT], BF16)
    p.push_scope()
    gwsb2 = p.sb("gwsb2", [8, D], BF16)
    p.dma("gpsimd", gwsb2[:, :], inp["L1_ffn_gate_w"], w=[gwsb2])
    gwT = p.sb("gwT", [128, 64], BF16)
    for k in range(8):
        pt = PBA
        p.tr(pt[0:128, 0:8], gwsb2[0:8, k * 128:(k + 1) * 128], identb[0:8, 0:8],
             r=[gwsb2, identb], w=[pt])
        p.cp("vector", gwT[:, k * 8:(k + 1) * 8], pt[0:128, 0:8], r=[pt], w=[gwT])
    lgm = p.sb("lgm", [128, 8], F32)
    scm = p.sb("scm", [128, 8], F32)
    mx1 = p.sb("mx1", [128, 1], F32)
    m1m = p.sb("m1m", [128, 8], F32)
    tm8 = p.sb("tm8", [128, 8], F32)
    msc = p.sb("msc", [128, 16], F32)
    for tt in range(16):
        cs = slice(tt * 128, (tt + 1) * 128)
        pp = psf[0]
        for k in range(8):
            p.mm(pp[:, 0:8], xn[k][:, cs], gwT[:, k * 8:(k + 1) * 8],
                 start=(k == 0), stop=(k == 7), r=[xn[k], gwT], w=[pp])
        p.cp("vector", lgm[:, :], pp[:, 0:8], r=[pp], w=[lgm])
        p.red("vector", mx1[:, :], lgm[:, :], ALU.max, negate=True, r=[lgm], w=[mx1])
        p.act(scm[:, :], lgm[:, :], AF.Exp, bias=mx1[0:128, 0:1], r=[lgm, mx1], w=[scm])
        p.red("vector", mx1[:, :], scm[:, :], ALU.add, r=[scm], w=[mx1])
        p.recip(mx1[:, :], mx1[:, :], r=[mx1], w=[mx1])
        p.tss("vector", scm[:, :], scm[:, :], mx1[0:128, 0:1], ALU.mult, r=[scm, mx1], w=[scm])
        p.red("vector", mx1[:, :], scm[:, :], ALU.max, r=[scm], w=[mx1])
        p.tss("vector", m1m[:, :], scm[:, :], mx1[0:128, 0:1], ALU.is_ge, r=[scm, mx1], w=[m1m])
        p.stt("vector", tm8[:, :], m1m[:, :], -1e30, scm[:, :], ALU.mult, ALU.add,
              r=[m1m, scm], w=[tm8])
        p.red("vector", mx1[:, :], tm8[:, :], ALU.max, r=[tm8], w=[mx1])
        p.tss("vector", msc[:, 8:16], scm[:, :], mx1[0:128, 0:1], ALU.is_ge, r=[scm, mx1], w=[msc])
        p.tt("vector", msc[:, 0:8], msc[:, 8:16], scm[:, :], ALU.mult, r=[scm, msc], w=[msc])
        p.tr(psf[3][0:16, 0:128], msc[:, :], identf[:, :], r=[msc, identf], w=[psf[3]])
        p.cp("vector", msc_fm[0:16, cs], psf[3][0:16, 0:128], r=[psf[3]], w=[msc_fm])
    p.pop_scope()
    if "moemsc" in dbg:
        p.dma("gpsimd", dbg["moemsc"], msc_fm[:, :], r=[msc_fm])
    mrow = p.sb("mrow", [1, NT], BF16)
    krow = p.sb("krow", [1, NT], BF16)
    m_bc = p.sb("m_bc", [128, NT], F32)
    k_bc = p.sb("k_bc", [128, NT], BF16)
    h1 = [p.sb(f"h1_{i}", [128, NT], BF16) for i in range(8)]
    x3m = [p.sb(f"x3m_{i}", [128, NT], BF16) for i in range(8)]
    wstr2 = [p.sb(f"wstr2_{j}", [128, D], BF16) for j in range(2)]
    w1T = [p.sb(f"w1T_{j}", [128, 8 * 128], BF16) for j in range(2)]
    bcol = p.sb("bcol", [8, 128], F32)
    b1c = p.sb("b1c", [128, 8], F32)
    b2c = p.sb("b2c", [128, 8], F32)
    b3c = p.sb("b3c", [128, 8], F32)
    etmps = [p.sb(f"etmp{i}", [128, 512], F32) for i in range(2)]

    def load_w1T(dst, wname, e, rows, j):
        wn = wstr2[j]
        p.dma("gpsimd", wn[:, :], inp[wname][e, rows:rows + 128, :], w=[wn])
        pt = PBA if j == 0 else PBB
        for k in range(8):
            p.trp(pt[:, k * 128:(k + 1) * 128], wn[:, k * 128:(k + 1) * 128], identb[:, :],
                  start=(k == 0), stop=(k == 7), r=[wn, identb], w=[pt])
        p.cp("vector" if j == 0 else "scalar", dst[:, 0:1024], pt[:, 0:1024], r=[pt], w=[dst])

    def load_bcol(dst, name, e):
        p.dma("sync", bcol[:, :], inp[name][e].rearrange("(a b) -> a b", b=128), w=[bcol])
        p.tr(psf[5][0:128, 0:8], bcol[:, :], identf[0:8, 0:8], r=[bcol, identf], w=[psf[5]])
        p.cp("vector", dst[:, :], psf[5][0:128, 0:8], r=[psf[5]], w=[dst])

    b2c2 = [b2c, p.sb("b2cB", [128, 8], F32)]

    def prep_expert(e):
        p.dma("gpsimd", mrow[0:1, :], msc_fm[e:e + 1, :], r=[msc_fm], w=[mrow])
        p.dma("gpsimd", krow[0:1, :], msc_fm[8 + e:9 + e, :], r=[msc_fm], w=[krow])
        for m2 in range(4):
            ms = slice(m2 * 512, (m2 + 1) * 512)
            pbx = psf[4]
            p.mm(pbx[:, 0:512], onesb[0:1, :], mrow[0:1, ms], start=True, stop=True,
                 r=[onesb, mrow], w=[pbx])
            p.cp("vector", m_bc[:, ms], pbx[:, 0:512], r=[pbx], w=[m_bc])
            pby = psf[5]
            p.mm(pby[:, 0:512], onesb[0:1, :], krow[0:1, ms], start=True, stop=True,
                 r=[onesb, krow], w=[pby])
            p.cp("scalar", k_bc[:, ms], pby[:, 0:512], r=[pby], w=[k_bc])
        load_bcol(b1c, "L1_ffn_b1", e)
        load_bcol(b2c2[e % 2], "L1_ffn_b2", e)
        load_bcol(b3c, "L1_ffn_b3", e)

    prep_expert(0)
    for e in range(8):
        m_bc_, k_bc_ = m_bc, k_bc
        b1c_, b2c_, b3c_ = b1c, b2c2[e % 2], b3c
        load_w1T(w1T[0], "L1_ffn_w1", e, 0, 0)
        for ht in range(8):
            for m2 in range(4):
                ms = slice(m2 * 512, (m2 + 1) * 512)
                pp = psf[0 + 3 * (m2 % 2)]
                for k in range(8):
                    p.mm(pp[:, 0:512], w1T[ht % 2][:, k * 128:(k + 1) * 128], xn[k][:, ms],
                         start=(k == 0), stop=(k == 7), r=[w1T[ht % 2], xn[k]], w=[pp])
                p.tt("vector", etmps[m2 % 2][:, 0:512], pp[:, 0:512], m_bc_[:, ms], ALU.mult,
                     r=[pp, m_bc_], w=[etmps[m2 % 2]])
                p.act(h1[ht][:, ms], etmps[m2 % 2][:, 0:512], AF.Silu, bias=b1c_[0:128, ht:ht + 1],
                      r=[etmps[m2 % 2], b1c_], w=[h1[ht]])
            if ht + 1 < 8:
                load_w1T(w1T[(ht + 1) % 2], "L1_ffn_w1", e, (ht + 1) * 128, (ht + 1) % 2)
        load_w1T(w1T[0], "L1_ffn_w3", e, 0, 0)
        for ht in range(8):
            for m2 in range(4):
                ms = slice(m2 * 512, (m2 + 1) * 512)
                pp = psf[1 + 3 * (m2 % 2)]
                for k in range(8):
                    p.mm(pp[:, 0:512], w1T[ht % 2][:, k * 128:(k + 1) * 128], xn[k][:, ms],
                         start=(k == 0), stop=(k == 7), r=[w1T[ht % 2], xn[k]], w=[pp])
                p.tt("vector", etmps[m2 % 2][:, 0:512], pp[:, 0:512], m_bc_[:, ms], ALU.mult,
                     r=[pp, m_bc_], w=[etmps[m2 % 2]])
                p.stt("vector", x3m[ht][:, ms], k_bc_[:, ms], b3c_[0:128, ht:ht + 1],
                      etmps[m2 % 2][:, 0:512], ALU.mult, ALU.add,
                      r=[k_bc_, b3c_, etmps[m2 % 2]], w=[x3m[ht]])
            if ht + 1 < 8:
                load_w1T(w1T[(ht + 1) % 2], "L1_ffn_w3", e, (ht + 1) * 128, (ht + 1) % 2)
        if e + 1 < 8:
            prep_expert(e + 1)
        load_w1T(w1T[0], "L1_ffn_w2", e, 0, 0)
        for dt_ in range(8):
            for m2 in range(4):
                ms = slice(m2 * 512, (m2 + 1) * 512)
                pp = psf[2 + 3 * (m2 % 2)]
                for k in range(8):
                    p.mm(pp[:, 0:512], w1T[dt_ % 2][:, k * 128:(k + 1) * 128], h1[k][:, ms],
                         start=(k == 0), stop=(k == 7), r=[w1T[dt_ % 2], h1[k]], w=[pp])
                p.stt("vector", etmps[m2 % 2][:, 0:512], pp[:, 0:512], b2c_[0:128, dt_:dt_ + 1],
                      x3m[dt_][:, ms], ALU.add, ALU.mult, r=[pp, b2c_, x3m[dt_]], w=[etmps[m2 % 2]])
                p.tt("vector" if (dt_ + m2) % 2 == 0 else "gpsimd",
                     acc[dt_][:, ms], acc[dt_][:, ms], etmps[m2 % 2][:, 0:512], ALU.add,
                     r=[etmps[m2 % 2], acc[dt_]], w=[acc[dt_]])
            if dt_ + 1 < 8:
                load_w1T(w1T[(dt_ + 1) % 2], "L1_ffn_w2", e, (dt_ + 1) * 128, (dt_ + 1) % 2)
    p.pop_scope()
    dump_fm("moe", acc)
    if upto <= 9:
        return finish(p, nc, outs, inp)

    p.push_scope()
    hh = [p.sb(f"hh{i}", [128, 1024], BF16) for i in range(16)]
    wsn = [p.sb(f"wsn{j}", [128, D], BF16) for j in range(2)]
    wsT = [p.sb(f"wsT{j}", [128, 8 * 128], BF16) for j in range(2)]
    gs2s = [p.sb(f"gs2_{i}", [128, 512], F32) for i in range(2)]

    def load_wsT(dst, wname, rows, cols0, ncols, j):
        wn = wsn[j]
        assert ncols <= 8
        p.dma("gpsimd", wn[:, 0:ncols * 128], inp[wname][rows:rows + 128, cols0:cols0 + ncols * 128],
              w=[wn])
        pt = PBA if j == 0 else PBB
        for k in range(ncols):
            p.trp(pt[:, k * 128:(k + 1) * 128], wn[:, k * 128:(k + 1) * 128], identb[:, :],
                  start=(k == 0), stop=(k == ncols - 1), r=[wn, identb], w=[pt])
        p.cp("vector" if j == 0 else "scalar", dst[:, 0:ncols * 128], pt[:, 0:ncols * 128],
             r=[pt], w=[dst])

    for c2 in range(2):
        tbase = c2 * 1024
        load_wsT(wsT[0], "L1_ffn_sw1", 0, 0, 8, 0)
        load_wsT(wsT[1], "L1_ffn_sw2", 0, 0, 8, 1)
        for ht in range(16):
            for m2 in range(2):
                ms = slice(tbase + m2 * 512, tbase + (m2 + 1) * 512)
                pg = psf[0 + 2 * m2]
                for k in range(8):
                    p.mm(pg[:, 0:512], wsT[0][:, k * 128:(k + 1) * 128], xn[k][:, ms],
                         start=(k == 0), stop=(k == 7), r=[wsT[0], xn[k]], w=[pg])
                p.act(gs2s[m2][:, 0:512], pg[:, 0:512], AF.Silu, r=[pg], w=[gs2s[m2]])
                pu = psf[1 + 2 * m2]
                for k in range(8):
                    p.mm(pu[:, 0:512], wsT[1][:, k * 128:(k + 1) * 128], xn[k][:, ms],
                         start=(k == 0), stop=(k == 7), r=[wsT[1], xn[k]], w=[pu])
                p.tt("vector", hh[ht][:, m2 * 512:(m2 + 1) * 512], pu[:, 0:512], gs2s[m2][:, 0:512],
                     ALU.mult, r=[pu, gs2s[m2]], w=[hh[ht]])
            if ht + 1 < 16:
                load_wsT(wsT[2 + (ht + 1) % 2] if False else wsT[0], "L1_ffn_sw1", (ht + 1) * 128, 0, 8, 0)
                load_wsT(wsT[1], "L1_ffn_sw2", (ht + 1) * 128, 0, 8, 1)
        for half in range(2):
            load_wsT(wsT[half], "L1_ffn_sw3", 0, half * 1024, 8, half)
        for dt_ in range(8):
            for m2 in range(2):
                ms = slice(tbase + m2 * 512, tbase + (m2 + 1) * 512)
                pd = psf[2 + dt_ % 2]
                for k in range(16):
                    wsrc = wsT[k // 8]
                    p.mm(pd[:, 0:512], wsrc[:, (k % 8) * 128:((k % 8) + 1) * 128],
                         hh[k][:, m2 * 512:(m2 + 1) * 512],
                         start=(k == 0), stop=(k == 15), r=[wsrc, hh[k]], w=[pd])
                p.tt("vector", acc[dt_][:, ms], acc[dt_][:, ms], pd[:, 0:512], ALU.add,
                     r=[pd, acc[dt_]], w=[acc[dt_]])
            if dt_ + 1 < 8:
                for half in range(2):
                    load_wsT(wsT[half], "L1_ffn_sw3", (dt_ + 1) * 128, half * 1024, 8, half)
    p.pop_scope()
    dump_fm("zsh", acc)

    p.push_scope()
    onat = p.sb("onat", [128, D], F32)
    for tt in range(16):
        for k in range(8):
            pb = psf[k % 4]
            p.tr(pb[:, 0:128], acc[k][:, tt * 128:(tt + 1) * 128], identf[:, :],
                 r=[acc[k], identf], w=[pb])
            p.cp("vector" if k % 2 == 0 else "scalar", onat[:, k * 128:(k + 1) * 128],
                 pb[:, 0:128], r=[pb], w=[onat])
        p.dma("sync", o_ys[tt * 128:(tt + 1) * 128, :], onat[:, :], r=[onat])
    p.pop_scope()

    return finish(p, nc, outs, inp)


def finish(p, nc, outs, inp):
    p.emit()
    return nc, outs, list(inp.cache)


from concourse.bass_utils import run_bass_kernel_spmd
import ml_dtypes

_T_FULL = 8192


def _rope_tables():
    half = 128
    inv_freq = 1.0 / 10000.0 ** (np.arange(half, dtype=np.float64) / half)
    ang = np.arange(_T_FULL, dtype=np.float64)[:, None] * inv_freq[None, :]
    return np.cos(ang).astype(np.float32), np.sin(ang).astype(np.float32)


def _attn1_consts():
    bf = ml_dtypes.bfloat16
    SC1 = (1.0 / 8.0) / 32.0
    dupm = np.zeros((128, 256), np.float32)
    selm = np.zeros((128, 512), np.float32)
    segm = np.zeros((128, 128), np.float32)
    gmaskm = np.zeros((128, 128), np.float32)
    bcastP = np.zeros((16, 2112), np.float32)
    for e in range(64):
        dupm[e, e] = dupm[e, 64 + e] = 1.0
        dupm[64 + e, 128 + e] = dupm[64 + e, 192 + e] = 1.0
    for dh in range(2):
        for sh in range(2):
            g = dh * 2 + sh
            for e in range(64):
                selm[sh * 64 + e, g * 128 + dh * 64 + e] = 1.0
    for ft in range(8):
        for pp in range(128):
            cc_ = 2 * ft + (1 if pp >= 64 else 0)
            segm[pp, ft * 16 + cc_] = SC1
            gmaskm[pp, ft * 16 + cc_] = 1.0
    for pq in range(16):
        bcastP[pq, 128 * pq + 64:128 * pq + 128] = 1.0
    return {"dupm": dupm.astype(bf), "selm": selm.astype(bf),
            "segm": segm.astype(bf), "gmaskm": gmaskm,
            "bcastP": bcastP.astype(bf)}


def _core_inputs(inputs, c, cos_f, sin_f, prefix_blobs=None):
    x = inputs["L0_attn_hidden_states"]
    sl = slice(c * TC, (c + 1) * TC)
    xs = np.ascontiguousarray(
        np.concatenate([x[0, sl], x[1, sl]], axis=0)).astype(np.float32, copy=False)
    pm = np.zeros((128, 8), np.float32)
    for j in range(8):
        pm[:, j] = 1.0 if c < j else 0.0
    m = {
        "xs": xs,
        "identf": np.eye(128, dtype=np.float32),
        "identb": np.eye(128, dtype=np.float32).astype(ml_dtypes.bfloat16),
        "utmask": np.triu(np.ones((128, 128), np.float32)),
        "cos": np.ascontiguousarray(cos_f[sl]),
        "sin": np.ascontiguousarray(sin_f[sl]),
        "pmask": pm,
    }
    m.update(_attn1_consts())
    for k, v in inputs.items():
        if k != "L0_attn_hidden_states":
            m[k] = np.ascontiguousarray(v)
    if prefix_blobs is not None:
        m["blobsK"], m["blobsVz"], m["blobszk"] = prefix_blobs
    return m


FUSED = True


def kernel(**inputs):
    cos_f, sin_f = _rope_tables()
    if FUSED:
        ncB, outsB, innB = build(dict(prefix="cc"))
        in_mapsB = []
        for c in range(8):
            m = _core_inputs(inputs, c, cos_f, sin_f)
            in_mapsB.append({k: m[k] for k in innB})
        resB = run_bass_kernel_spmd(ncB, in_mapsB, core_ids=list(range(8)))
    else:
        ncA, outsA, innA = build(dict(seg_only=True, prefix="zero"))
        in_mapsA = []
        for c in range(8):
            m = _core_inputs(inputs, c, cos_f, sin_f)
            in_mapsA.append({k: m[k] for k in innA})
        resA = run_bass_kernel_spmd(ncA, in_mapsA, core_ids=list(range(8)))
        blobsK = np.concatenate([resA.results[c]["blobK"] for c in range(8)], axis=0)
        blobsVz = np.concatenate([resA.results[c]["blobVz"] for c in range(8)], axis=0)
        blobszk = np.concatenate([resA.results[c]["blobzk"] for c in range(8)], axis=0)
        ncB, outsB, innB = build(dict(prefix="input"))
        in_mapsB = []
        for c in range(8):
            m = _core_inputs(inputs, c, cos_f, sin_f,
                             prefix_blobs=(blobsK, blobsVz, blobszk))
            in_mapsB.append({k: m[k] for k in innB})
        resB = run_bass_kernel_spmd(ncB, in_mapsB, core_ids=list(range(8)))
    out = np.empty((2, _T_FULL, 1024), np.float32)
    for c in range(8):
        ys = resB.results[c]["ys"]
        out[0, c * TC:(c + 1) * TC] = ys[0:TC]
        out[1, c * TC:(c + 1) * TC] = ys[TC:2 * TC]
    return out
